# version 12
# baseline (speedup 1.0000x reference)
import math

import numpy as np
import ml_dtypes
import concourse.bass as bass
import concourse.mybir as mybir
from concourse.bass_utils import run_bass_kernel_spmd

F32 = mybir.dt.float32
BF16 = mybir.dt.bfloat16
U16 = mybir.dt.uint16
AF = mybir.ActivationFunctionType
ALU = mybir.AluOpType
AX = mybir.AxisListType
bfdt = ml_dtypes.bfloat16

B, S, D = 2, 4096, 2048
NH, HD = 16, 128
R, C, WIN = 16, 256, 128
HID2, H = 16384, 8192
TOK, TEXT, NCH, PFX = 1024, 1152, 8, 32
EPS_LN, EPS_RMS = 1e-5, 1e-6
SCALE = 1.0 / math.sqrt(HD)
SB_BASE = 16512
SB_BUDGET = 229376


class Sched:
    ENGINES = ("sync", "gpsimd", "scalar", "vector", "tensor")
    STREAMS = {
        "setup_s": ("sync", 1), "setup_p": ("gpsimd", 1),
        "x": ("sync", 2), "entw": ("sync", 2), "stk": ("sync", 1),
        "whatw": ("gpsimd", 2),
        "aoscw": ("sync", 2), "ffnw": ("sync", 2), "x2spw": ("sync", 1),
        "outw": ("sync", 1),
        "aslab": ("sync", 2), "qcast_gu": ("gpsimd", 2), "qcast_p": ("gpsimd", 2),
        "qcast_a": ("gpsimd", 2),
        "wt": ("scalar", 3), "gslab": ("scalar", 2), "pslab": ("scalar", 2),
    }

    def __init__(self, nc):
        self.nc = nc
        self.items = {e: [] for e in self.ENGINES}
        self.csem = {e: nc.alloc_semaphore(f"cs_{e}") for e in self.ENGINES}
        self.ccount = {e: 0 for e in self.ENGINES}
        self.qsem = {}
        self.qcount = {}
        self.qrr = {}
        for s, (q, ns) in self.STREAMS.items():
            for i in range(ns):
                self.qsem[(s, i)] = nc.alloc_semaphore(f"dq_{s}{i}")
                self.qcount[(s, i)] = 0
            self.qrr[s] = 0

    def op(self, eng, fn, waits=(), sig=True):
        tok = None
        if sig:
            self.ccount[eng] += 1
            tok = (self.csem[eng], self.ccount[eng])
        wl = [w for w in waits if w is not None]
        self.items[eng].append(("op", fn, wl, tok))
        return tok

    def dma(self, stream, fn, waits=()):
        q, ns = self.STREAMS[stream]
        i = self.qrr[stream]
        self.qrr[stream] = (i + 1) % ns
        key = (stream, i)
        prev = (self.qsem[key], self.qcount[key]) if self.qcount[key] else None
        self.qcount[key] += 16
        tok = (self.qsem[key], self.qcount[key])
        wl = [w for w in list(waits) + [prev] if w is not None]
        self.items[q].append(("dma", fn, wl, tok))
        return tok

    def barrier(self, streams=()):
        toks = []
        for e in ("gpsimd", "scalar", "vector", "tensor"):
            pv = (self.csem[e], self.ccount[e])
            toks.append(self.op(e, lambda eng: eng.nop(), waits=[pv], sig=True))
        for s in streams:
            for i in range(self.STREAMS[s][1]):
                if self.qcount[(s, i)]:
                    toks.append((self.qsem[(s, i)], self.qcount[(s, i)]))
        return toks

    def build(self):
        nc = self.nc
        with nc.Block() as block:
            def make(eng_name):
                def body(eng):
                    seen = {}
                    for kind, fn, waits, tok in self.items[eng_name]:
                        for (sem, val) in waits:
                            key = id(sem)
                            if seen.get(key, 0) < val:
                                eng.wait_ge(sem, val)
                                seen[key] = val
                        inst = fn(eng)
                        if tok is not None:
                            inst.then_inc(tok[0], 16 if kind == "dma" else 1)
                return body
            block.sync(make("sync"))
            block.gpsimd(make("gpsimd"))
            block.scalar(make("scalar"))
            block.vector(make("vector"))
            block.tensor(make("tensor"))


class Bank:
    def __init__(self, tile):
        self.t = tile
        self.readers = []
        self.writer = None

    def wr_waits(self):
        w = list(self.readers) + ([self.writer] if self.writer else [])
        self.readers = []
        return w

    def set_writer(self, tok):
        assert tok is not None
        self.writer = tok

    def rd_waits(self):
        return [self.writer]

    def add_reader(self, tok):
        self.readers.append(tok)


def build_program():
    nc = bass.Bass("TRN2", target_bir_lowering=False)
    Sc = Sched(nc)

    def ext(name, shape, dtype=F32):
        return nc.dram_tensor(name, list(shape), dtype, kind="ExternalInput").ap()

    xb = ext("xb", [S, D])
    xo = ext("xo", [TEXT, D])
    rtcc = ext("rtcc", [64, TEXT]); rtns = ext("rtns", [64, TEXT])
    cktcc = ext("cktcc", [64, C]); cktns = ext("cktns", [64, C])
    cmv = ext("cm", [NCH, 128, C], BF16); lmv = ext("lm", [2, 128, 256], BF16)
    Wq = ext("Wq", [D, D]); Wk = ext("Wk", [D, HD]); Wv = ext("Wv", [D, HD])
    Wlk = ext("Wlk", [D, D]); Wlv = ext("Wlv", [D, D]); Wo = ext("Wo", [D, D])
    Wgu = ext("Wgu", [HID2, D]); Wp = ext("Wp", [D, H])
    bqT = ext("bqT", [128, 16]); blkT = ext("blkT", [128, 16])
    blvT = ext("blvT", [128, 16]); boT = ext("boT", [128, 16])
    bkc = ext("bkc", [128, 1]); bvc = ext("bvc", [128, 1]); bcc = ext("bcc", [128, 1])
    qnc = ext("qnc", [128, 1]); knc = ext("knc", [128, 1])
    sinkTf = ext("sinkTf", [128, 16]); sinkvRf = ext("sinkvRf", [1, NH * HD])
    wcrepf = ext("wcrepf", [128, D])

    out = nc.dram_tensor("out", [TOK, D], F32, kind="ExternalOutput").ap()

    wbf_gu = nc.dram_tensor("wbf_gu", [HID2, D], BF16, kind="Internal").ap()
    wbf_q = nc.dram_tensor("wbf_q", [D, D], BF16, kind="Internal").ap()
    wbf_lk = nc.dram_tensor("wbf_lk", [D, D], BF16, kind="Internal").ap()
    wbf_lv = nc.dram_tensor("wbf_lv", [D, D], BF16, kind="Internal").ap()
    wbf_o = nc.dram_tensor("wbf_o", [D, D], BF16, kind="Internal").ap()
    wbf_p = nc.dram_tensor("wbf_p", [D, H], BF16, kind="Internal").ap()
    whatT_gu = nc.dram_tensor("whatT_gu", [16, 128, HID2], BF16, kind="Internal").ap()
    whatT_p = nc.dram_tensor("whatT_p", [16, 64, 128, 128], BF16, kind="Internal").ap()
    x2sp = nc.dram_tensor("x2sp", [NCH, 128, D], F32, kind="Internal").ap()
    aosc = nc.dram_tensor("aosc", [NCH, 128, D], F32, kind="Internal").ap()
    ffnsc = nc.dram_tensor("ffnsc", [NCH, 128, D], F32, kind="Internal").ap()
    entsc_d = nc.dram_tensor("entsc_d", [PFX, 8, D], F32, kind="Internal").ap()

    idn_bf_c = nc.inline_tensor(np.eye(128, dtype=bfdt), name="idn_bf").ap()
    ones_col_c = nc.inline_tensor(np.ones((128, 1), dtype=bfdt), name="ones_col").ap()
    ones_row_c = nc.inline_tensor(np.ones((1, 128), dtype=np.float32), name="ones_row").ap()
    patt = np.zeros((128, 8), dtype=np.float32)
    for i in range(128):
        patt[i, i // 16] = 1.0
    patt8_c = nc.inline_tensor(patt, name="patt8").ap()
    patt8T_c = nc.inline_tensor(np.ascontiguousarray(patt.T), name="patt8T").ap()
    patt8b_c = nc.inline_tensor(patt.astype(bfdt), name="patt8b").ap()
    i2 = np.zeros((128, 2), dtype=bfdt); i2[:64, 0] = 1.0 / 128; i2[64:, 1] = 1.0 / 128
    I2_c = nc.inline_tensor(i2, name="I2").ap()
    e2 = np.zeros((2, 128), dtype=bfdt); e2[0, :64] = 1; e2[1, 64:] = 1
    E2_c = nc.inline_tensor(e2, name="E2").ap()
    sw = np.zeros((64, 64), dtype=bfdt)
    for i in range(64):
        sw[i, (i + 32) % 64] = 1
    SW64_c = nc.inline_tensor(sw, name="SW64").ap()
    cstv = np.zeros((128, 4), dtype=np.float32)
    cstv[:, 0] = EPS_LN; cstv[:, 1] = EPS_RMS; cstv[:, 2] = SCALE; cstv[:, 3] = 0.5
    cst_c = nc.inline_tensor(cstv, name="cstc").ap()

    cur = [SB_BASE]

    def sba(name, shape, dtype, at=None, track=True):
        if at is None:
            cur[0] = (cur[0] + 63) // 64 * 64
            at = cur[0]
            if track:
                cur[0] = at + int(np.prod(shape[1:])) * mybir.dt.size(dtype)
                assert cur[0] <= SB_BUDGET, f"SBUF overflow at {name}: {cur[0]}"
        return nc.alloc_sbuf_tensor_at(name, list(shape), dtype, offset=at)

    sb_idbf = sba("sb_idbf", [128, 128], BF16)
    sb_onesc = sba("sb_onesc", [128, 1], BF16)
    sb_onesr = sba("sb_onesr", [1, 128], F32)
    sb_patt8 = sba("sb_patt8", [128, 8], F32)
    sb_patt8T = sba("sb_patt8T", [8, 128], F32)
    sb_patt8b = sba("sb_patt8b", [128, 8], BF16)
    sb_I2 = sba("sb_I2", [128, 2], BF16)
    sb_E2 = sba("sb_E2", [2, 128], BF16)
    sb_SW = sba("sb_SW", [64, 64], BF16)
    sb_bqT = sba("sb_bqT", [128, 16], F32)
    sb_blkT = sba("sb_blkT", [128, 16], F32)
    sb_blvT = sba("sb_blvT", [128, 16], F32)
    sb_boT = sba("sb_boT", [128, 16], F32)
    sb_cols = sba("sb_cols", [128, 5], F32)
    sb_sinkT = sba("sb_sinkT", [128, 16], BF16)
    sb_sinkvR = sba("sb_sinkvR", [1, NH * HD], BF16)
    sb_rtcc = sba("sb_rtcc", [64, TEXT], BF16)
    sb_rtns = sba("sb_rtns", [64, TEXT], BF16)
    sb_ckcc = sba("sb_ckcc", [64, C], BF16)
    sb_ckns = sba("sb_ckns", [64, C], BF16)
    sb_ckr = sba("sb_ckr", [128, C], BF16)
    sb_cvT = sba("sb_cvT", [128, 2, 128], BF16)
    sb_cst = sba("sb_cst", [128, 4], F32)
    sb_cmask = sba("sb_cmask", [128, NCH, C], BF16)
    sb_lmask = sba("sb_lmask", [128, 2, 256], BF16)

    r1 = (cur[0] + 63) // 64 * 64
    cur[0] = r1
    sb_xnT = sba("sb_xnT", [128, 16, TEXT], BF16)
    r1b = (cur[0] + 63) // 64 * 64
    cur[0] = r1b
    sb_merged = sba("sb_merged", [128, 16, TOK], BF16)
    r1_end = cur[0]
    sb_wcrep = sba("sb_wcrep", [128, D], BF16, at=r1)
    sb_x2 = sba("sb_x2", [128, 2, D], F32, at=r1)
    sb_outst = sba("sb_outst", [128, 2, D], F32, at=r1 + 16384)
    sb_fo = sba("sb_fo", [128, D], F32, at=r1b)
    sb_fo2 = sba("sb_fo2", [128, D], F32, at=r1b + 8192)
    sb_ent = sba("sb_ent", [128, 2, D], F32, at=r1b)
    sb_entbf = sba("sb_entbf", [128, 2, D], BF16, at=r1b + 16384)
    sb_entsc = sba("sb_entsc", [8, D], F32, at=r1b + 24576)
    sb_hh = sba("sb_hh", [128, 64, 512], BF16, at=r1)

    def h_tile(j):
        return sb_hh[:, j, :]

    r2 = (r1_end + 63) // 64 * 64
    cur[0] = r2
    sb_xn2T = sba("sb_xn2T", [128, 16, TOK], BF16)
    r2_end = cur[0]
    a = r2
    sb_qf = sba("sb_qf", [128, TEXT], F32, at=a); a += TEXT * 4
    sb_rsqb = sba("sb_rsqb", [128, TEXT], F32, at=a); a += TEXT * 4
    sb_row = sba("sb_row", [1, TEXT], F32, at=a); a += TEXT * 4
    sb_qs = sba("sb_qs", [128, TEXT], BF16, at=a); a += TEXT * 2
    sb_usw = sba("sb_usw", [64, TEXT], BF16, at=a); a += TEXT * 2
    sb_u = sba("sb_u", [64, TEXT], BF16, at=a); a += TEXT * 2
    sb_v = sba("sb_v", [64, TEXT], BF16, at=a); a += TEXT * 2
    sb_lv = sba("sb_lv", [128, TEXT], BF16, at=a); a += TEXT * 2
    sb_qr2 = sba("sb_qr2", [128, 2, TOK], BF16, at=a); a += 2 * TOK * 2
    assert a <= r2_end, (a, r2_end)

    cur[0] = r2_end
    sb_xc = sba("sb_xc", [128, 2, D], F32)
    sb_xnbf = sba("sb_xnbf", [128, 2, D], BF16)
    sb_dump = sba("sb_dump", [128, D], BF16)
    sb_stat = sba("sb_stat", [128, 26], F32)
    sc_at = (cur[0] + 63) // 64 * 64
    sb_sc = sba("sb_sc", [128, 2, 512], F32)
    sb_evf = sba("sb_evf", [128, 512], F32, at=sc_at)
    sb_esc = sba("sb_esc", [128, 2, 512], BF16)
    sb_attbf = sba("sb_attbf", [128, 2, 512], BF16)
    sb_attT = sba("sb_attT", [128, 2, 512], BF16)
    sb_attS = sba("sb_attS", [1, 2, 128], BF16)
    sb_mtmp = sba("sb_mtmp", [128, 2, 128], F32)
    sb_snk = sba("sb_snk", [128, 2, 20], F32)
    sb_lkr2 = sba("sb_lkr2", [128, 2, TEXT], BF16)
    sb_lvT2 = sba("sb_lvT2", [128, 2, 9, 128], BF16)
    sb_ask = sba("sb_ask", [128, 2], BF16)
    sb_ckf = sba("sb_ckf", [128, C], F32, at=r2)
    sb_ckq = sba("sb_ckq", [128, C], BF16, at=r2 + 1024)
    sb_wsl = sba("sb_wsl", [128, 4, 16, 128], BF16)
    sb_wt = sba("sb_wt", [128, 2, 1024], BF16)
    sb_absq = sba("sb_absq", [128, 2, 1024], BF16)
    sb_shs = sba("sb_shs", [2, 4, 512], BF16)
    sb_shf = sba("sb_shf", [128, 4, 512], BF16)
    sb_mq = sba("sb_mq", [128, 1, 512], BF16)
    sb_sgq = sba("sb_sgq", [128, 1, 512], U16)
    sb_ssq = sba("sb_ssq", [128, 1, 512], U16)
    sb_what = sba("sb_what", [128, 3, 1024], BF16)

    assert cur[0] <= SB_BUDGET, cur[0]

    bkA0 = Bank(nc.alloc_psum_tensor("pbA0", [128, 512], F32))
    bkA1 = Bank(nc.alloc_psum_tensor("pbA1", [128, 512], F32))
    bkB0 = Bank(nc.alloc_psum_tensor("pbB0", [128, 512], F32))
    bkB1 = Bank(nc.alloc_psum_tensor("pbB1", [128, 512], F32))
    bkTB = Bank(nc.alloc_psum_tensor("ptrb", [128, 1024], BF16))
    bkTF = Bank(nc.alloc_psum_tensor("ptrf", [128, 512], F32))
    bkSC = Bank(nc.alloc_psum_tensor("psc", [128, 512], F32))
    bkMS = Bank(nc.alloc_psum_tensor("pmisc", [128, 512], F32))

    for dst, src in ((sb_idbf, idn_bf_c), (sb_onesc, ones_col_c), (sb_onesr, ones_row_c),
                     (sb_patt8, patt8_c), (sb_patt8T, patt8T_c), (sb_patt8b, patt8b_c),
                     (sb_I2, I2_c), (sb_E2, E2_c), (sb_SW, SW64_c), (sb_cst, cst_c),
                     (sb_bqT, bqT), (sb_blkT, blkT), (sb_blvT, blvT), (sb_boT, boT)):
        Sc.dma("setup_s", lambda e, d=dst, s=src: e.dma_start(out=d[:], in_=s))
    for i, src in enumerate((bkc, bvc, bcc, qnc, knc)):
        Sc.dma("setup_s", lambda e, i=i, s=src: e.dma_start(out=sb_cols[:, i:i + 1], in_=s))
    Sc.dma("setup_s", lambda e: e.dma_start(out=sb_cmask[:], in_=cmv.rearrange("c p n -> p c n")))
    Sc.dma("setup_s", lambda e: e.dma_start(out=sb_lmask[:], in_=lmv.rearrange("c p n -> p c n")))
    for dst, src in ((sb_sinkT, sinkTf), (sb_sinkvR, sinkvRf), (sb_wcrep, wcrepf),
                     (sb_rtcc, rtcc), (sb_rtns, rtns), (sb_ckcc, cktcc), (sb_ckns, cktns)):
        Sc.dma("setup_p", lambda e, d=dst, s=src: e.dma_start(out=d[:], in_=s))
    Sc.dma("setup_p", lambda e: e.dma_start(
        out=sb_wsl[:, 2, :, :], in_=Wk.rearrange("(k p) n -> p k n", p=128)))
    Sc.dma("setup_p", lambda e: e.dma_start(
        out=sb_wsl[:, 3, :, :], in_=Wv.rearrange("(k p) n -> p k n", p=128)))
    qcast_gu = [Sc.dma("qcast_gu", lambda e, rr=rr: e.dma_start(
        out=wbf_gu[rr * 4096:(rr + 1) * 4096, :], in_=Wgu[rr * 4096:(rr + 1) * 4096, :]))
        for rr in range(4)]
    qcast_p = [Sc.dma("qcast_p", lambda e, rr=rr: e.dma_start(
        out=wbf_p[rr * 512:(rr + 1) * 512, :], in_=Wp[rr * 512:(rr + 1) * 512, :]))
        for rr in range(4)]
    qcast_a = {}
    for nm, (dst, srcw) in {"q": (wbf_q, Wq), "lk": (wbf_lk, Wlk),
                            "lv": (wbf_lv, Wlv), "o": (wbf_o, Wo)}.items():
        qcast_a[nm] = Sc.dma("qcast_a", lambda e, d=dst, s=srcw: e.dma_start(
            out=d, in_=s))
    BAR0 = Sc.barrier(streams=("setup_s", "setup_p"))

    qslot = [0]
    qcons = [[], []]
    qring = [[], [], [], []]
    gsring = [[], [], [], []]
    shf_cons = [[], [], [], []]
    qstate = {}
    wtok = {}
    bguard = [[], []]
    wslot = [0]
    whatg = [[], [], []]

    def emit_qA(which, k, hb):
        sl = qslot[0]; qslot[0] ^= 1
        if which == "gu":
            src = wbf_gu[hb * 1024:(hb + 1) * 1024, k * 128:(k + 1) * 128]
            cast_tok = [qcast_gu[hb // 4]]
        else:
            src = wbf_p[k * 1024:(k + 1) * 1024, hb * 128:(hb + 1) * 128]
            cast_tok = [qcast_p[2 * k], qcast_p[2 * k + 1]]
        t_tr = Sc.dma("wt", lambda e, src=src, sl=sl: e.dma_start(
            out=sb_wt[:, sl, :], in_=src, transpose=True),
            waits=cast_tok + qcons[sl] + BAR0)
        qcons[sl] = []
        t_abs = Sc.op("vector", lambda e, sl=sl: e.tensor_scalar(
            sb_absq[:, sl, :].bitcast(U16), sb_wt[:, sl, :].bitcast(U16), 0x7FFF,
            None, ALU.bitwise_and), waits=[t_tr])
        qcons[sl].append(t_abs)
        tshf2 = []
        for q2 in range(2):
            fs = q2 * 512
            ssl = sl * 2 + q2
            tm = Sc.op("tensor", lambda e, fs=fs, sl=sl: e.matmul(
                bkMS.t[0:2, 0:512], sb_I2[:], sb_absq[:, sl, fs:fs + 512],
                start=True, stop=True), waits=bkMS.wr_waits() + [t_abs])
            bkMS.set_writer(tm)
            tse = Sc.op("scalar", lambda e, ssl=ssl: e.activation(
                sb_shs[:, ssl, :], bkMS.t[0:2, 0:512], AF.Identity),
                waits=bkMS.rd_waits() + gsring[ssl])
            bkMS.add_reader(tse)
            tex = Sc.op("tensor", lambda e, ssl=ssl: e.matmul(
                bkTF.t[:, 0:512], sb_E2[:], sb_shs[:, ssl, :], start=True, stop=True),
                waits=bkTF.wr_waits() + [tse])
            gsring[ssl] = [tex]
            bkTF.set_writer(tex)
            tshf = Sc.op("scalar", lambda e, ssl=ssl: e.activation(
                sb_shf[:, ssl, :], bkTF.t[:, 0:512], AF.Identity),
                waits=bkTF.rd_waits() + shf_cons[ssl])
            shf_cons[ssl] = []
            bkTF.add_reader(tshf)
            tshf2.append(tshf)
        qstate[sl] = (which, k, hb, tshf2)

    def emit_qB(sl):
        which, k, hb, tshf2 = qstate[sl]
        w3 = wslot[0]
        wslot[0] = (w3 + 1) % 3
        outs = []
        for q2 in range(2):
            fs = q2 * 512
            ssl = sl * 2 + q2
            tmq = Sc.op("vector", lambda e, fs=fs, sl=sl, ssl=ssl: e.tensor_tensor(
                sb_mq[:, 0, :], sb_absq[:, sl, fs:fs + 512], sb_shf[:, ssl, :],
                ALU.is_gt), waits=[tshf2[q2]] + qring[ssl] + bguard[0])
            tsg = Sc.op("vector", lambda e, fs=fs, sl=sl, q2=q2: e.tensor_scalar(
                sb_sgq[:, 0, :], sb_wt[:, sl, fs:fs + 512].bitcast(U16), 0x8000,
                None, ALU.bitwise_and), waits=qring[ssl] + bguard[1])
            tss = Sc.op("vector", lambda e, q2=q2, ssl=ssl: e.tensor_tensor(
                sb_ssq[:, 0, :], sb_sgq[:, 0, :], sb_shf[:, ssl, :].bitcast(U16),
                ALU.bitwise_or), waits=[tsg, tmq] + bguard[0])
            shf_cons[ssl].append(tss)
            twh = Sc.op("vector", lambda e, fs=fs, w3=w3, q2=q2: e.scalar_tensor_tensor(
                sb_what[:, w3, fs:fs + 512], sb_mq[:, 0, :], 2.0,
                sb_ssq[:, 0, :].bitcast(BF16), ALU.mult, ALU.mult),
                waits=[tss] + (whatg[w3] if q2 == 0 else []))
            bguard[0] = [twh]
            bguard[1] = [tss]
            qring[ssl] = [twh]
            outs.append(twh)
            qcons[sl].append(tmq)
            qcons[sl].append(tsg)
        if which == "gu":
            dst = whatT_gu[k, :, hb * 1024:(hb + 1) * 1024]
        else:
            dst = whatT_p[k * 8:(k + 1) * 8, hb, :, :].rearrange("m p f -> p m f")
        t_out = Sc.dma("whatw", lambda e, dst=dst, w3=w3: e.dma_start(
            out=dst, in_=sb_what[:, w3, :].rearrange("p (j f) -> p j f", f=128)),
            waits=outs)
        whatg[w3] = [t_out]
        if which == "gu":
            for j in range(hb * 8, hb * 8 + 8):
                wtok[("gu", j, k)] = t_out
        else:
            wtok[("p", k, hb)] = t_out

    lnst = {"sl": 0, "cons": [[], []]}
    dump_guard = []

    def emit_ln(x_ap, out_bf_ap, deps):
        nonlocal dump_guard
        sl = lnst["sl"]; lnst["sl"] ^= 1
        st = sb_stat[:, sl * 10:sl * 10 + 10]
        cons = lnst["cons"][sl]
        t_sum = Sc.op("scalar", lambda e: e.activation(
            sb_dump[:], x_ap, AF.Identity, accum_out=st[:, 0:1]),
            waits=deps + cons + dump_guard)
        t_sq = Sc.op("scalar", lambda e: e.activation(
            sb_dump[:], x_ap, AF.Square, accum_out=st[:, 1:2]),
            waits=deps + cons + [t_sum])
        dump_guard = [t_sq]
        lnst["cons"][sl] = []
        cons = lnst["cons"][sl]
        t1 = Sc.op("vector", lambda e: e.tensor_scalar(
            st[:, 2:3], st[:, 0:1], 1.0 / D, None, ALU.mult), waits=[t_sum])
        t2 = Sc.op("vector", lambda e: e.tensor_scalar(
            st[:, 3:4], st[:, 1:2], 1.0 / D, None, ALU.mult), waits=[t_sq])
        t3 = Sc.op("vector", lambda e: e.tensor_tensor(
            st[:, 4:5], st[:, 2:3], st[:, 2:3], ALU.mult), waits=[t1])
        t4 = Sc.op("vector", lambda e: e.tensor_tensor(
            st[:, 5:6], st[:, 3:4], st[:, 4:5], ALU.subtract), waits=[t2, t3])
        t5 = Sc.op("scalar", lambda e: e.activation(
            st[:, 6:7], st[:, 5:6], AF.Sqrt, bias=sb_cst[:, 0:1]), waits=[t4])
        t6 = Sc.op("vector", lambda e: e.reciprocal(st[:, 7:8], st[:, 6:7]), waits=[t5])
        t7 = Sc.op("vector", lambda e: e.scalar_tensor_tensor(
            st[:, 8:9], st[:, 2:3], -1.0, st[:, 7:8], ALU.mult, ALU.mult), waits=[t6])
        tn = None
        if out_bf_ap is not None:
            tn = Sc.op("scalar", lambda e: e.activation(
                out_bf_ap, x_ap, AF.Identity, bias=st[:, 8:9], scale=st[:, 7:8]),
                waits=[t7])
            cons.append(tn)
        return tn, st, t7

    quant_iter = ([("gu", k, hb) for hb in range(16) for k in range(16)] +
                  [("p", k2, hb) for k2 in range(2) for hb in range(64)])
    qi = [0]
    qpend = []

    def pump_quant(n):
        for _ in range(n):
            if qi[0] < len(quant_iter):
                sl_next = qslot[0]
                emit_qA(*quant_iter[qi[0]])
                qi[0] += 1
                qpend.append(sl_next)
            if len(qpend) > 1:
                emit_qB(qpend.pop(0))

    def drain_quant():
        while qpend:
            emit_qB(qpend.pop(0))

    xc_cons = [[], []]
    xn_cons = [[], []]
    ent_dmas = []
    t_blk_g = [None]
    x_pend = {}

    def issue_x(i):
        sl = i % 2
        x_pend[i] = Sc.dma("x", lambda e, i=i, sl=sl: e.dma_start(
            out=sb_xc[:, sl, :], in_=xb[i * 128:(i + 1) * 128, :]),
            waits=xc_cons[sl] + BAR0)
        xc_cons[sl] = []

    issue_x(0)
    for i in range(PFX):
        sl = i % 2
        t_x = x_pend.pop(i)
        if i + 1 < PFX:
            issue_x(i + 1)
        t_xn, _, _ = emit_ln(sb_xc[:, sl, :], sb_xnbf[:, sl, :], [t_x] + xn_cons[sl])
        xn_cons[sl] = []
        xc_cons[sl].append(t_xn)
        t_wc = Sc.op("vector", lambda e, sl=sl: e.scalar_tensor_tensor(
            sb_dump[:], sb_xnbf[:, sl, :], 1.0, sb_wcrep[:], ALU.mult, ALU.mult,
            accum_out=sb_stat[:, 20:21]), waits=[t_xn] + dump_guard)
        dump_guard = [t_wc]
        if i >= 6:
            pump_quant(3)
        t_ew = Sc.op("scalar", lambda e: e.activation(
            sb_stat[:, 21:22], sb_stat[:, 20:21], AF.Exp, bias=sb_cols[:, 2:3]),
            waits=[t_wc])
        t_sm = Sc.op("tensor", lambda e: e.matmul(
            bkMS.t[0:8, 0:1], sb_patt8[:], sb_stat[:, 21:22], start=True, stop=True),
            waits=bkMS.wr_waits() + [t_ew])
        bkMS.set_writer(t_sm)
        t_ev = Sc.op("vector", lambda e: e.tensor_copy(
            sb_stat[0:8, 22:23], bkMS.t[0:8, 0:1]), waits=bkMS.rd_waits())
        bkMS.add_reader(t_ev)
        t_rc = Sc.op("vector", lambda e: e.reciprocal(
            sb_stat[0:8, 23:24], sb_stat[0:8, 22:23]), waits=[t_ev])
        t_blk = Sc.op("vector", lambda e: e.tensor_scalar(
            sb_attbf[:, 0, 1:9], sb_patt8b[:], sb_stat[:, 21:22], None, ALU.mult),
            waits=[t_ew, t_blk_g[0]] if t_blk_g[0] else [t_ew])
        t_blk_g[0] = None
        if i >= 6:
            pump_quant(3)
        ev_toks = []
        tm = None
        for q4 in range(4):
            bnk = bkSC if q4 % 2 == 0 else bkB0
            tm = Sc.op("tensor", lambda e, q4=q4, sl=sl, b=bnk: e.matmul(
                b.t[0:8, 0:512], sb_attbf[:, 0, 1:9],
                sb_xnbf[:, sl, q4 * 512:(q4 + 1) * 512], start=True, stop=True),
                waits=bnk.wr_waits() + [t_blk])
            bnk.set_writer(tm)
            te = Sc.op("vector", lambda e, q4=q4, b=bnk: e.tensor_scalar(
                sb_entsc[:, q4 * 512:(q4 + 1) * 512], b.t[0:8, 0:512],
                sb_stat[0:8, 23:24], None, ALU.mult),
                waits=bnk.rd_waits() + [t_rc] + (ent_dmas[-2:] if q4 == 0 else []))
            bnk.add_reader(te)
            ev_toks.append(te)
            t_blk_g[0] = tm
        xn_cons[sl] = [tm, t_wc]
        t_ed = Sc.dma("entw", lambda e, i=i: e.dma_start(
            out=entsc_d[i, :, :], in_=sb_entsc[:]), waits=ev_toks)
        ent_dmas.append(t_ed)
        xn_cons[sl].append(t_ed)
        if i >= 6:
            pump_quant(6)

    t_stk = Sc.dma("stk", lambda e: e.dma_start(
        out=sb_ent[:],
        in_=entsc_d.rearrange("c e d -> (c e) d").rearrange("(t p) d -> p t d", p=128)),
        waits=ent_dmas)
    t_eb = [Sc.op("vector", lambda e, tt=tt: e.tensor_copy(
        sb_entbf[:, tt, :], sb_ent[:, tt, :]), waits=[t_stk]) for tt in range(2)]
    entT_toks = []
    for kd in range(16):
        for tt in range(2):
            tm = Sc.op("tensor", lambda e, kd=kd, tt=tt: e.transpose(
                bkTB.t[:, 0:128], sb_entbf[:, tt, kd * 128:(kd + 1) * 128], sb_idbf[:]),
                waits=bkTB.wr_waits() + t_eb)
            bkTB.set_writer(tm)
            te = Sc.op("vector", lambda e, kd=kd, tt=tt: e.tensor_copy(
                sb_xnT[:, kd, tt * 128:(tt + 1) * 128], bkTB.t[:, 0:128]),
                waits=bkTB.rd_waits())
            bkTB.add_reader(te)
            entT_toks.append(te)

    def emit_ckcv(is_ck):
        slab = 2 if is_ck else 3
        w0 = bkSC.wr_waits() + entT_toks + BAR0
        last = None
        for kd in range(16):
            last = Sc.op("tensor", lambda e, kd=kd, slab=slab: e.matmul(
                bkSC.t[:, 0:256], sb_wsl[:, slab, kd, :], sb_xnT[:, kd, 0:256],
                start=(kd == 0), stop=(kd == 15)),
                waits=w0 if kd == 0 else [], sig=(kd == 15))
        bkSC.set_writer(last)
        bias = sb_cols[:, 0:1] if is_ck else sb_cols[:, 1:2]
        t_ev = Sc.op("scalar", lambda e, bias=bias: e.activation(
            sb_ckf[:], bkSC.t[:, 0:256], AF.Identity, bias=bias),
            waits=bkSC.rd_waits())
        bkSC.add_reader(t_ev)
        return t_ev, last

    t_ckf, ck_mm = emit_ckcv(True)
    t_cq = Sc.op("scalar", lambda e: e.activation(sb_ckq[:], sb_ckf[:], AF.Square),
                 waits=[t_ckf])
    t_cs = Sc.op("tensor", lambda e: e.matmul(
        bkMS.t[0:1, 0:256], sb_onesc[:], sb_ckq[:], start=True, stop=True),
        waits=bkMS.wr_waits() + [t_cq])
    bkMS.set_writer(t_cs)
    t_ce = Sc.op("vector", lambda e: e.tensor_scalar(
        sb_row[0:1, 0:256], bkMS.t[0:1, 0:256], 1.0 / HD, None, ALU.mult),
        waits=bkMS.rd_waits())
    bkMS.add_reader(t_ce)
    t_cs2 = Sc.op("scalar", lambda e: e.activation(
        sb_row[0:1, 256:512], sb_row[0:1, 0:256], AF.Sqrt, bias=sb_cst[0:1, 1:2]),
        waits=[t_ce])
    t_cr = Sc.op("vector", lambda e: e.reciprocal(
        sb_row[0:1, 512:768], sb_row[0:1, 256:512]), waits=[t_cs2])
    t_cb = Sc.op("tensor", lambda e: e.matmul(
        bkTF.t[:, 0:256], sb_onesr[:], sb_row[0:1, 512:768], start=True, stop=True),
        waits=bkTF.wr_waits() + [t_cr])
    bkTF.set_writer(t_cb)
    t_cbe = Sc.op("vector", lambda e: e.tensor_copy(sb_rsqb[:, 0:256], bkTF.t[:, 0:256]),
                  waits=bkTF.rd_waits())
    bkTF.add_reader(t_cbe)
    t_ct = Sc.op("vector", lambda e: e.scalar_tensor_tensor(
        sb_ckr[:], sb_ckf[:], sb_cols[:, 4:5], sb_rsqb[:, 0:256], ALU.mult, ALU.mult),
        waits=[t_cbe])
    t_csw = Sc.op("tensor", lambda e: e.matmul(
        bkMS.t[0:64, 0:256], sb_SW[:], sb_ckr[0:64, :], start=True, stop=True),
        waits=bkMS.wr_waits() + [t_ct])
    bkMS.set_writer(t_csw)
    t_cswe = Sc.op("vector", lambda e: e.tensor_copy(sb_usw[:, 0:256], bkMS.t[0:64, 0:256]),
                   waits=bkMS.rd_waits())
    bkMS.add_reader(t_cswe)
    t_cu = Sc.op("vector", lambda e: e.tensor_tensor(
        sb_u[:, 0:256], sb_ckr[0:64, :], sb_ckcc[:], ALU.mult), waits=[t_ct])
    t_cv2 = Sc.op("vector", lambda e: e.tensor_tensor(
        sb_v[:, 0:256], sb_usw[:, 0:256], sb_ckns[:], ALU.mult), waits=[t_cswe])
    t_ckdone = Sc.op("vector", lambda e: e.tensor_tensor(
        sb_ckr[0:64, :], sb_u[:, 0:256], sb_v[:, 0:256], ALU.add),
        waits=[t_cu, t_cv2])
    t_cvf, cv_mm = emit_ckcv(False)
    t_cvb = Sc.op("vector", lambda e: e.tensor_copy(sb_ckq[:], sb_ckf[:]), waits=[t_cvf])
    cvT_toks = []
    for tt in range(2):
        tm = Sc.op("tensor", lambda e, tt=tt: e.transpose(
            bkTB.t[:, 0:128], sb_ckq[:, tt * 128:(tt + 1) * 128], sb_idbf[:]),
            waits=bkTB.wr_waits() + [t_cvb])
        bkTB.set_writer(tm)
        te = Sc.op("vector", lambda e, tt=tt: e.tensor_copy(
            sb_cvT[:, tt, :], bkTB.t[:, 0:128]), waits=bkTB.rd_waits())
        bkTB.add_reader(te)
        cvT_toks.append(te)
    ckcv_toks = [t_ckdone] + cvT_toks
    BARP = Sc.barrier(streams=("entw", "stk"))
    slab_seed = {2: [ck_mm], 3: [cv_mm]}

    xnT_last = []
    x2_pend = {}

    def issue_x2(i):
        sl = i % 2
        x2_pend[i] = Sc.dma("x", lambda e, i=i, sl=sl: e.dma_start(
            out=sb_xc[:, sl, :], in_=xo[i * 128:(i + 1) * 128, :]),
            waits=xc_cons[sl] + (BARP if i < 2 else []))
        xc_cons[sl] = []

    issue_x2(0)
    for i in range(9):
        sl = i % 2
        t_x = x2_pend.pop(i)
        if i + 1 < 9:
            issue_x2(i + 1)
        t_xn, _, _ = emit_ln(sb_xc[:, sl, :], sb_xnbf[:, sl, :], [t_x] + xn_cons[sl])
        xn_cons[sl] = []
        xc_cons[sl].append(t_xn)
        for kd in range(16):
            tm = Sc.op("tensor", lambda e, kd=kd, sl=sl: e.transpose(
                bkTB.t[:, 0:128], sb_xnbf[:, sl, kd * 128:(kd + 1) * 128], sb_idbf[:]),
                waits=bkTB.wr_waits() + [t_xn] + (BARP if i == 0 else []))
            bkTB.set_writer(tm)
            te = Sc.op("vector", lambda e, kd=kd, i=i: e.tensor_copy(
                sb_xnT[:, kd, i * 128:(i + 1) * 128], bkTB.t[:, 0:128]),
                waits=bkTB.rd_waits())
            bkTB.add_reader(te)
            if kd == 7:
                pump_quant(4)
            if kd == 15:
                xn_cons[sl].append(tm)
                xnT_last = [te]
        pump_quant(4)
    BAR1 = Sc.barrier()

    slab_cons = {i: slab_seed.get(i, []) for i in range(4)}
    slab_rot = [0]
    merged_toks = []
    def next_slab():
        s = slab_rot[0]
        slab_rot[0] = (slab_rot[0] + 1) % 4
        return s

    def emit_proj(Wsrc, cast_tok, h, bias_col, c0, ncol, out_f32, deps, out_bf=None,
                  bank_fixed=None):
        slab = next_slab()
        t_sl = Sc.dma("aslab", lambda e, slab=slab: e.dma_start(
            out=sb_wsl[:, slab, :, :],
            in_=Wsrc[:, h * 128:(h + 1) * 128].rearrange("(k p) n -> p k n", p=128)),
            waits=slab_cons[slab] + [cast_tok])
        slab_cons[slab] = []
        ev_toks = []
        for nb in range((ncol + 511) // 512):
            f0 = nb * 512
            cw = min(512, ncol - f0)
            bank = bank_fixed or (bkA0 if nb % 2 == 0 else bkA1)
            w0 = bank.wr_waits() + [t_sl] + deps
            last = None
            for kd in range(16):
                last = Sc.op("tensor", lambda e, kd=kd, f0=f0, cw=cw, slab=slab, b=bank: e.matmul(
                    b.t[:, 0:cw], sb_wsl[:, slab, kd, :],
                    sb_xnT[:, kd, c0 + f0:c0 + f0 + cw],
                    start=(kd == 0), stop=(kd == 15)),
                    waits=w0 if kd == 0 else [], sig=(kd == 15))
            bank.set_writer(last)
            slab_cons[slab].append(last)
            dst = out_f32 if out_bf is None else out_bf
            tev = Sc.op("scalar", lambda e, f0=f0, cw=cw, dst=dst, b=bank: e.activation(
                dst[:, f0:f0 + cw], b.t[:, 0:cw], AF.Identity, bias=bias_col),
                waits=bank.rd_waits())
            bank.add_reader(tev)
            ev_toks.append(tev)
        return ev_toks

    def emit_rms_rope(ncol, toff, gain_col, out_ap, deps):
        nonlocal dump_guard
        t_sq = Sc.op("scalar", lambda e: e.activation(
            sb_dump[:, 0:ncol], sb_qf[:, 0:ncol], AF.Square),
            waits=deps + dump_guard)
        dump_guard = []
        rs_toks = []
        for nb in range((ncol + 511) // 512):
            f0 = nb * 512; cw = min(512, ncol - f0)
            tm = Sc.op("tensor", lambda e, f0=f0, cw=cw: e.matmul(
                bkMS.t[0:1, 0:cw], sb_onesc[:], sb_dump[:, f0:f0 + cw],
                start=True, stop=True), waits=bkMS.wr_waits() + [t_sq])
            bkMS.set_writer(tm)
            dump_guard.append(tm)
            te = Sc.op("vector", lambda e, f0=f0, cw=cw: e.tensor_scalar(
                sb_row[0:1, f0:f0 + cw], bkMS.t[0:1, 0:cw], 1.0 / HD, None, ALU.mult),
                waits=bkMS.rd_waits())
            bkMS.add_reader(te)
            rs_toks.append(te)
        t_rc2 = Sc.op("scalar", lambda e: e.activation(
            sb_row[0:1, 0:ncol], sb_row[0:1, 0:ncol], AF.Sqrt, bias=sb_cst[0:1, 1:2]),
            waits=rs_toks)
        bc_toks = []
        for nb in range((ncol + 511) // 512):
            f0 = nb * 512; cw = min(512, ncol - f0)
            tm = Sc.op("tensor", lambda e, f0=f0, cw=cw: e.matmul(
                bkTF.t[:, 0:cw], sb_onesr[:], sb_row[0:1, f0:f0 + cw],
                start=True, stop=True), waits=bkTF.wr_waits() + [t_rc2])
            bkTF.set_writer(tm)
            te = Sc.op("vector", lambda e, f0=f0, cw=cw: e.reciprocal(
                sb_rsqb[:, f0:f0 + cw], bkTF.t[:, 0:cw]), waits=bkTF.rd_waits())
            bkTF.add_reader(te)
            bc_toks.append(te)
        t_t = Sc.op("vector", lambda e: e.scalar_tensor_tensor(
            sb_qs[:, 0:ncol], sb_qf[:, 0:ncol], gain_col, sb_rsqb[:, 0:ncol],
            ALU.mult, ALU.mult), waits=bc_toks + deps)
        sw_toks = []
        for nb in range((ncol + 511) // 512):
            f0 = nb * 512; cw = min(512, ncol - f0)
            tm = Sc.op("tensor", lambda e, f0=f0, cw=cw: e.matmul(
                bkMS.t[0:64, 0:cw], sb_SW[:], sb_qs[0:64, f0:f0 + cw],
                start=True, stop=True), waits=bkMS.wr_waits() + [t_t])
            bkMS.set_writer(tm)
            te = Sc.op("vector", lambda e, f0=f0, cw=cw: e.tensor_copy(
                sb_usw[:, f0:f0 + cw], bkMS.t[0:64, 0:cw]), waits=bkMS.rd_waits())
            bkMS.add_reader(te)
            sw_toks.append(te)
        t_u = Sc.op("vector", lambda e: e.tensor_tensor(
            sb_u[:, 0:ncol], sb_qs[0:64, 0:ncol], sb_rtcc[:, toff:toff + ncol],
            ALU.mult), waits=[t_t])
        t_v = Sc.op("vector", lambda e: e.tensor_tensor(
            sb_v[:, 0:ncol], sb_usw[:, 0:ncol], sb_rtns[:, toff:toff + ncol],
            ALU.mult), waits=sw_toks)
        t_o1 = Sc.op("vector", lambda e: e.tensor_tensor(
            out_ap[0:64, 0:ncol], sb_u[:, 0:ncol], sb_v[:, 0:ncol], ALU.add),
            waits=[t_u, t_v])
        t_rest = Sc.op("vector", lambda e: e.tensor_copy(
            out_ap[64:128, 0:ncol], sb_qs[64:128, 0:ncol]), waits=[t_t])
        return [t_o1, t_rest]

    prev_qr_cons = [[], []]
    prev_lkr_cons = [[], []]
    prev_lv_cons = [[], []]
    prev_lvT_cons = [[], []]
    prev_snk_cons = [[], []]
    ropeg = {"qs": [], "u": [], "row": []}
    front_state = [None, None]

    def front_gen(h):
        hs = h % 2
        deps0 = (BAR1 if h == 0 else []) + prev_qr_cons[hs]
        prev_qr_cons[hs] = []
        if h > 0:
            deps0 = deps0 + front_state[1 - hs]["qf_free"]
        q_ev = emit_proj(wbf_q, qcast_a["q"], h, sb_bqT[:, h:h + 1], 128, TOK,
                         sb_qf, deps0, bank_fixed=bkA0)
        yield
        qr_toks = emit_rms_rope(TOK, 128, sb_cols[:, 3:4], sb_qr2[:, hs, :], q_ev)
        yield
        lk_ev = emit_proj(wbf_lk, qcast_a["lk"], h, sb_blkT[:, h:h + 1], 0, TEXT,
                          sb_qf, qr_toks + prev_lkr_cons[hs], bank_fixed=bkA0)
        prev_lkr_cons[hs] = []
        yield
        lkr_toks = emit_rms_rope(TEXT, 0, sb_cols[:, 4:5], sb_lkr2[:, hs, :], lk_ev)
        yield
        lv_ev = emit_proj(wbf_lv, qcast_a["lv"], h, sb_blvT[:, h:h + 1], 0, TEXT,
                          None, lkr_toks + prev_lv_cons[hs], out_bf=sb_lv,
                          bank_fixed=bkA0)
        prev_lv_cons[hs] = []
        yield
        lvT_toks = []
        for j in range(9):
            tm = Sc.op("tensor", lambda e, j=j: e.transpose(
                bkTB.t[:, 0:128], sb_lv[:, j * 128:(j + 1) * 128], sb_idbf[:]),
                waits=bkTB.wr_waits() + lv_ev)
            bkTB.set_writer(tm)
            te = Sc.op("vector", lambda e, j=j, hs=hs: e.tensor_copy(
                sb_lvT2[:, hs, j, :], bkTB.t[:, 0:128]),
                waits=bkTB.rd_waits() + (prev_lvT_cons[hs] if j == 0 else []))
            bkTB.add_reader(te)
            lvT_toks.append(te)
            prev_lv_cons[hs].append(tm)
        prev_lvT_cons[hs] = []
        yield
        t_sr0 = Sc.op("tensor", lambda e, h=h, hs=hs: e.matmul(
            bkMS.t[0:1, 0:512], sb_sinkT[:, h:h + 1], sb_qr2[:, hs, 0:512],
            start=True, stop=True), waits=bkMS.wr_waits() + qr_toks)
        bkMS.set_writer(t_sr0)
        t_se0 = Sc.op("vector", lambda e: e.tensor_scalar(
            sb_row[0:1, 0:512], bkMS.t[0:1, 0:512], float(SCALE), None, ALU.mult),
            waits=bkMS.rd_waits() + ropeg["row"])
        bkMS.add_reader(t_se0)
        t_sr1 = Sc.op("tensor", lambda e, h=h, hs=hs: e.matmul(
            bkMS.t[0:1, 0:512], sb_sinkT[:, h:h + 1], sb_qr2[:, hs, 512:1024],
            start=True, stop=True), waits=bkMS.wr_waits())
        bkMS.set_writer(t_sr1)
        t_se1 = Sc.op("vector", lambda e: e.tensor_scalar(
            sb_row[0:1, 512:1024], bkMS.t[0:1, 0:512], float(SCALE), None, ALU.mult),
            waits=bkMS.rd_waits())
        bkMS.add_reader(t_se1)
        t_sbf = Sc.op("vector", lambda e: e.tensor_copy(
            sb_u[0:1, 0:TOK], sb_row[0:1, 0:TOK]),
            waits=[t_se0, t_se1] + ropeg["u"] + lkr_toks)
        snk_evs = []
        tmx = None
        for cch in range(NCH):
            tmx = Sc.op("tensor", lambda e, cch=cch: e.transpose(
                bkTB.t[:, 896:897], sb_u[0:1, cch * 128:(cch + 1) * 128],
                sb_idbf[0:1, 0:1]),
                waits=bkTB.wr_waits() + [t_sbf] + snk_evs[-1:])
            bkTB.set_writer(tmx)
            tev = Sc.op("vector", lambda e, cch=cch, hs=hs: e.tensor_copy(
                sb_snk[:, hs, cch:cch + 1], bkTB.t[:, 896:897]),
                waits=bkTB.rd_waits() + (prev_snk_cons[hs] if cch == 0 else []))
            bkTB.add_reader(tev)
            snk_evs.append(tev)
        prev_snk_cons[hs] = []
        t_nsk = Sc.op("vector", lambda e, hs=hs: e.tensor_scalar(
            sb_snk[:, hs, 8:16], sb_snk[:, hs, 0:8], -1.0, None, ALU.mult),
            waits=snk_evs)
        ropeg["row"] = [t_sbf]
        ropeg["u"] = [tmx]
        front_state[hs] = {"qr": qr_toks, "lkr": lkr_toks, "lvT": lvT_toks,
                           "nsk": [t_nsk], "qf_free": lkr_toks}
        yield

    def run_gen(g):
        if g is not None:
            try:
                next(g)
                return g
            except StopIteration:
                return None
        return None

    merged_toks = merged_toks if False else merged_toks
    chain_g = [[], []]
    esg = [[], []]
    att_g = [[], []]
    askg = [[], []]
    mtg = [[], []]
    g0 = front_gen(0)
    for _ in range(10):
        g0 = run_gen(g0)
        if g0 is None:
            break

    for h in range(NH):
        hs = h % 2
        fs = front_state[hs]
        qr_toks = fs["qr"]; lkr_toks = fs["lkr"]; lvT_toks = fs["lvT"]
        nsk_toks = fs["nsk"]
        gen_next = front_gen(h + 1) if h + 1 < NH else None

        def emit_scores(cch):
            q0 = cch * 128
            par = cch % 2
            P = bkSC if par == 0 else bkA1
            mm1 = Sc.op("tensor", lambda e, q0=q0, b=P, hs=hs: e.matmul(
                b.t[:, 0:256], sb_qr2[:, hs, q0:q0 + 128], sb_ckr[:],
                start=True, stop=False),
                waits=P.wr_waits() + qr_toks + ckcv_toks + lkr_toks, sig=False)
            mm2 = Sc.op("tensor", lambda e, q0=q0, b=P, hs=hs: e.matmul(
                b.t[:, 256:512], sb_qr2[:, hs, q0:q0 + 128],
                sb_lkr2[:, hs, q0:q0 + 256], start=False, stop=True), waits=[])
            P.set_writer(mm2)
            prev_qr_cons[hs].append(mm2)
            prev_lkr_cons[hs].append(mm2)
            return P

        def emit_chain(cch, P):
            q0 = cch * 128
            par = cch % 2
            X = bkB0 if par == 0 else bkB1
            stc = sb_stat[:, 14 + par * 6: 20 + par * 6]
            midx = 0 if cch == 0 else 1
            t_ev0 = Sc.op("vector", lambda e, cch=cch, par=par, b=P: e.scalar_tensor_tensor(
                sb_sc[:, par, 0:256], b.t[:, 0:256], float(SCALE), sb_cmask[:, cch, :],
                ALU.mult, ALU.add), waits=P.rd_waits() + chain_g[par])
            t_ev = Sc.op("vector", lambda e, midx=midx, par=par, b=P: e.scalar_tensor_tensor(
                sb_sc[:, par, 256:512], b.t[:, 256:512], float(SCALE),
                sb_lmask[:, midx, :], ALU.mult, ALU.add), waits=[t_ev0])
            P.add_reader(t_ev)
            t_mx = Sc.op("vector", lambda e, par=par, stc=stc: e.reduce_max(
                stc[:, 0:2], sb_sc[:, par, :].rearrange("p (g w) -> p g w", w=256),
                AX.X, negate=True), waits=[t_ev])
            t_mf = Sc.op("vector", lambda e, cch=cch, stc=stc, hs=hs: e.tensor_scalar(
                stc[:, 0:1], stc[:, 0:1], sb_snk[:, hs, 8 + cch:9 + cch], None,
                ALU.min), waits=[t_mx] + nsk_toks)
            t_e1 = Sc.op("scalar", lambda e, par=par, stc=stc: e.activation(
                sb_esc[:, par, 0:256], sb_sc[:, par, 0:256], AF.Exp,
                bias=stc[:, 0:1], accum_out=stc[:, 2:3]), waits=[t_mf])
            t_e2 = Sc.op("scalar", lambda e, par=par, stc=stc: e.activation(
                sb_esc[:, par, 256:512], sb_sc[:, par, 256:512], AF.Exp,
                bias=stc[:, 1:2], accum_out=stc[:, 3:4]), waits=[t_mx, t_e1])
            t_es = Sc.op("scalar", lambda e, cch=cch, par=par, stc=stc, hs=hs: e.activation(
                sb_snk[:, hs, 16 + par:17 + par], sb_snk[:, hs, cch:cch + 1], AF.Exp,
                bias=stc[:, 0:1]), waits=[t_mf] + esg[par])
            t_sa = Sc.op("vector", lambda e, par=par, stc=stc, hs=hs: e.tensor_tensor(
                stc[:, 2:3], stc[:, 2:3], sb_snk[:, hs, 16 + par:17 + par], ALU.add),
                waits=[t_e1, t_es])
            t_rc = Sc.op("vector", lambda e, stc=stc: e.reciprocal(
                stc[:, 4:6], stc[:, 2:4]), waits=[t_sa, t_e2])
            t_a1 = Sc.op("scalar", lambda e, par=par, stc=stc: e.activation(
                sb_attbf[:, par, 0:256], sb_esc[:, par, 0:256], AF.Identity,
                scale=stc[:, 4:5]), waits=[t_rc] + att_g[par])
            t_a2 = Sc.op("scalar", lambda e, par=par, stc=stc: e.activation(
                sb_attbf[:, par, 256:512], sb_esc[:, par, 256:512], AF.Identity,
                scale=stc[:, 5:6]), waits=[t_rc, t_a1])
            t_ak = Sc.op("vector", lambda e, par=par, stc=stc, hs=hs: e.scalar_tensor_tensor(
                sb_ask[:, par:par + 1], sb_snk[:, hs, 16 + par:17 + par], 1.0,
                stc[:, 4:5], ALU.mult, ALU.mult), waits=[t_rc] + askg[par])
            prev_snk_cons[hs].extend([t_mf, t_es, t_sa, t_ak])
            esg[par] = [t_sa]
            tt = None
            for r in range(4):
                tt = Sc.op("tensor", lambda e, par=par, r=r: e.transpose(
                    bkTB.t[:, r * 128:(r + 1) * 128],
                    sb_attbf[:, par, r * 128:(r + 1) * 128], sb_idbf[:]),
                    waits=(bkTB.wr_waits() if r == 0 else []) +
                    ([t_a1] if r < 2 else [t_a2]), sig=(r == 3))
            tt5 = Sc.op("tensor", lambda e, par=par: e.transpose(
                bkTB.t[0:1, 512:640], sb_ask[:, par:par + 1], sb_idbf[:]),
                waits=[t_ak])
            bkTB.set_writer(tt5)
            t_aT = Sc.op("vector", lambda e, par=par: e.tensor_copy(
                sb_attT[:, par, :], bkTB.t[:, 0:512]), waits=bkTB.rd_waits())
            t_aS = Sc.op("vector", lambda e, par=par: e.tensor_copy(
                sb_attS[:, par, :], bkTB.t[0:1, 512:640]), waits=bkTB.rd_waits())
            bkTB.add_reader(t_aT)
            bkTB.add_reader(t_aS)
            att_g[par] = [tt, tt5]
            askg[par] = [tt5]
            c1 = Sc.op("tensor", lambda e, par=par, b=X: e.matmul(
                b.t[:, 0:128], sb_cvT[:, 0, :], sb_attT[:, par, 0:128],
                start=True, stop=False), waits=X.wr_waits() + [t_aT], sig=False)
            c2 = Sc.op("tensor", lambda e, par=par, b=X: e.matmul(
                b.t[:, 0:128], sb_cvT[:, 1, :], sb_attT[:, par, 128:256],
                start=False, stop=False), waits=[], sig=False)
            c3 = Sc.op("tensor", lambda e, par=par, h=h, b=X: e.matmul(
                b.t[:, 0:128], sb_sinkvR[:, h * 128:(h + 1) * 128],
                sb_attS[:, par, :], start=False, stop=False), waits=[t_aS], sig=False)
            l1 = Sc.op("tensor", lambda e, par=par, cch=cch, b=X, hs=hs: e.matmul(
                b.t[:, 128:256], sb_lvT2[:, hs, cch, :], sb_attT[:, par, 256:384],
                start=False, stop=False), waits=lvT_toks, sig=False)
            l2 = Sc.op("tensor", lambda e, par=par, cch=cch, b=X, hs=hs: e.matmul(
                b.t[:, 128:256], sb_lvT2[:, hs, cch + 1, :], sb_attT[:, par, 384:512],
                start=False, stop=True), waits=[])
            X.set_writer(l2)
            prev_lvT_cons[hs].append(l2)
            t_m1 = Sc.op("scalar", lambda e, par=par, b=X: e.activation(
                sb_mtmp[:, par, :], b.t[:, 0:128], AF.Identity,
                scale=sb_cst[:, 3:4]), waits=X.rd_waits() + mtg[par])
            X.add_reader(t_m1)
            t_m2 = Sc.op("vector", lambda e, h=h, q0=q0, par=par, b=X: e.scalar_tensor_tensor(
                sb_merged[:, h, q0:q0 + 128], b.t[:, 128:256], 0.5,
                sb_mtmp[:, par, :], ALU.mult, ALU.add),
                waits=X.rd_waits() + [t_m1])
            X.add_reader(t_m2)
            mtg[par] = [t_m2]
            chain_g[par] = [t_e2, t_aT]
            merged_toks.append(t_m2)

        pend = []
        for cch in range(NCH + 1):
            if cch < NCH:
                pend.append((cch, emit_scores(cch)))
            if len(pend) > (1 if cch < NCH else 0):
                c0_, P0_ = pend.pop(0)
                emit_chain(c0_, P0_)
            gen_next = run_gen(gen_next)
            pump_quant(2)
        while gen_next is not None:
            gen_next = run_gen(gen_next)

    drain_quant()

    BAR2 = Sc.barrier()

    ao_dmas = []
    evf_guard = []
    for m in range(16):
        slab = next_slab()
        t_sl = Sc.dma("aslab", lambda e, m=m, slab=slab: e.dma_start(
            out=sb_wsl[:, slab, :, :],
            in_=wbf_o[:, m * 128:(m + 1) * 128].rearrange("(k p) n -> p k n", p=128)),
            waits=slab_cons[slab] + [qcast_a["o"]] + (BAR2 if m < 4 else []))
        slab_cons[slab] = []
        for th in range(2):
            bank = bkA0 if th == 0 else bkA1
            w0 = bank.wr_waits() + [t_sl] + merged_toks[-1:]
            last = None
            for kd in range(16):
                last = Sc.op("tensor", lambda e, kd=kd, th=th, slab=slab, b=bank: e.matmul(
                    b.t[:, 0:512], sb_wsl[:, slab, kd, :],
                    sb_merged[:, kd, th * 512:(th + 1) * 512],
                    start=(kd == 0), stop=(kd == 15)),
                    waits=w0 if kd == 0 else [], sig=(kd == 15))
            bank.set_writer(last)
            slab_cons[slab].append(last)
            t_ev = Sc.op("scalar", lambda e, m=m, b=bank: e.activation(
                sb_evf[:], b.t[:, 0:512], AF.Identity, bias=sb_boT[:, m:m + 1]),
                waits=bank.rd_waits() + evf_guard)
            bank.add_reader(t_ev)
            t_evb = Sc.op("vector", lambda e: e.tensor_copy(
                sb_dump[:, 0:512], sb_evf[:]), waits=[t_ev] + dump_guard)
            evf_guard = [t_evb]
            dump_guard = []
            for p4 in range(4):
                cch = th * 4 + p4
                tm = Sc.op("tensor", lambda e, p4=p4: e.transpose(
                    bkTB.t[:, 0:128], sb_dump[:, p4 * 128:(p4 + 1) * 128], sb_idbf[:]),
                    waits=bkTB.wr_waits() + [t_evb])
                bkTB.set_writer(tm)
                dump_guard.append(tm)
                te = Sc.op("vector", lambda e: e.tensor_copy(
                    sb_mtmp[:, 0, :], bkTB.t[:, 0:128]), waits=bkTB.rd_waits())
                bkTB.add_reader(te)
                td = Sc.dma("aoscw", lambda e, cch=cch, m=m: e.dma_start(
                    out=aosc[cch, :, m * 128:(m + 1) * 128], in_=sb_mtmp[:, 0, :]),
                    waits=[te])
                bkTB.add_reader(td)
                ao_dmas.append(td)

    x2_cons = [[], []]
    xn2T_last = []
    x2sp_toks = []
    fo_guard = []
    for cch in range(NCH):
        sl = cch % 2
        t_ldao = Sc.dma("x", lambda e, cch=cch: e.dma_start(
            out=sb_fo[:], in_=aosc[cch, :, :]), waits=ao_dmas[-2:] + fo_guard)
        t_x = Sc.dma("x", lambda e, cch=cch, sl=sl: e.dma_start(
            out=sb_xc[:, sl, :], in_=xo[(cch + 1) * 128:(cch + 2) * 128, :]),
            waits=xc_cons[sl])
        xc_cons[sl] = []
        _, st, t7 = emit_ln(sb_fo[:], None, [t_ldao])
        t_nf = Sc.op("vector", lambda e, st=st: e.tensor_scalar(
            sb_fo[:], sb_fo[:], st[:, 7:8], st[:, 8:9], ALU.mult, ALU.add),
            waits=[t7])
        t_x2 = Sc.op("vector", lambda e, sl=sl: e.tensor_tensor(
            sb_x2[:, sl, :], sb_fo[:], sb_xc[:, sl, :], ALU.add),
            waits=[t_nf, t_x] + x2_cons[sl])
        fo_guard = [t_x2]
        x2_cons[sl] = []
        xc_cons[sl].append(t_x2)
        t_sp = Sc.dma("x2spw", lambda e, cch=cch, sl=sl: e.dma_start(
            out=x2sp[cch, :, :], in_=sb_x2[:, sl, :]), waits=[t_x2])
        x2_cons[sl].append(t_sp)
        x2sp_toks.append(t_sp)
        t_xn2, _, _ = emit_ln(sb_x2[:, sl, :], sb_xnbf[:, sl, :],
                              [t_x2] + xn_cons[sl])
        xn_cons[sl] = []
        x2_cons[sl].append(t_xn2)
        for kd in range(16):
            tm = Sc.op("tensor", lambda e, kd=kd, sl=sl: e.transpose(
                bkTB.t[:, 0:128], sb_xnbf[:, sl, kd * 128:(kd + 1) * 128], sb_idbf[:]),
                waits=bkTB.wr_waits() + [t_xn2])
            bkTB.set_writer(tm)
            te = Sc.op("vector", lambda e, kd=kd, cch=cch: e.tensor_copy(
                sb_xn2T[:, kd, cch * 128:(cch + 1) * 128], bkTB.t[:, 0:128]),
                waits=bkTB.rd_waits())
            bkTB.add_reader(te)
            if kd == 15:
                xn_cons[sl].append(tm)
                xn2T_last = [te]
    BAR3 = Sc.barrier(streams=("x2spw", "aoscw"))

    ffn_dmas = []
    gsl_cons = {i: [] for i in range(4)}
    evf_guard2 = []
    for hf in range(2):
        hoff = hf * 512
        h_last = []
        for jj in range(64):
            sl = jj % 2
            jg, ju = jj, jj + 64
            t_sg = Sc.dma("gslab", lambda e, jg=jg, sl=sl: e.dma_start(
                out=sb_wsl[:, sl * 2, :, :],
                in_=whatT_gu[:, :, jg * 128:(jg + 1) * 128].rearrange("k p f -> p k f")),
                waits=gsl_cons[sl * 2] + [wtok[("gu", jg, kk)] for kk in range(16)] +
                (BAR3 if jj < 2 and hf == 0 else []))
            t_su = Sc.dma("gslab", lambda e, ju=ju, sl=sl: e.dma_start(
                out=sb_wsl[:, sl * 2 + 1, :, :],
                in_=whatT_gu[:, :, ju * 128:(ju + 1) * 128].rearrange("k p f -> p k f")),
                waits=gsl_cons[sl * 2 + 1] + [wtok[("gu", ju, kk)] for kk in range(16)])
            gsl_cons[sl * 2] = []
            gsl_cons[sl * 2 + 1] = []
            bg = bkA0 if sl == 0 else bkA1
            bu = bkB0 if sl == 0 else bkB1
            w0 = bg.wr_waits() + [t_sg] + (xn2T_last if jj == 0 else [])
            lastg = None
            for kd in range(16):
                lastg = Sc.op("tensor", lambda e, kd=kd, sl=sl, hoff=hoff, b=bg: e.matmul(
                    b.t[:, 0:512], sb_wsl[:, sl * 2, kd, :],
                    sb_xn2T[:, kd, hoff:hoff + 512],
                    start=(kd == 0), stop=(kd == 15)),
                    waits=w0 if kd == 0 else [], sig=(kd == 15))
            bg.set_writer(lastg)
            gsl_cons[sl * 2].append(lastg)
            w0 = bu.wr_waits() + [t_su]
            lastu = None
            for kd in range(16):
                lastu = Sc.op("tensor", lambda e, kd=kd, sl=sl, hoff=hoff, b=bu: e.matmul(
                    b.t[:, 0:512], sb_wsl[:, sl * 2 + 1, kd, :],
                    sb_xn2T[:, kd, hoff:hoff + 512],
                    start=(kd == 0), stop=(kd == 15)),
                    waits=w0 if kd == 0 else [], sig=(kd == 15))
            bu.set_writer(lastu)
            gsl_cons[sl * 2 + 1].append(lastu)
            t_si = Sc.op("scalar", lambda e, b=bg: e.activation(
                sb_evf[:], b.t[:, 0:512], AF.Silu), waits=bg.rd_waits() + evf_guard2)
            bg.add_reader(t_si)
            t_h = Sc.op("vector", lambda e, jj=jj, b=bu: e.tensor_tensor(
                h_tile(jj), b.t[:, 0:512], sb_evf[:], ALU.mult),
                waits=bu.rd_waits() + [t_si] + (BAR3 if jj < 2 and hf == 0 else []))
            bu.add_reader(t_h)
            evf_guard2 = [t_h]
            h_last = [t_h]
        for m2 in range(16):
            sl = m2 % 2
            osl = 1 - sl
            pq = [wtok[("p", m2 // 8, hb)] for hb in range(64)] if hf == 0 else []
            t_s1 = Sc.dma("pslab", lambda e, m2=m2, sl=sl: e.dma_start(
                out=sb_wsl[:, sl * 2:sl * 2 + 2, :, :].rearrange("p a k n -> p (a k) n"),
                in_=whatT_p[m2, 0:32].rearrange("k p f -> p k f")),
                waits=gsl_cons[sl * 2] + gsl_cons[sl * 2 + 1] + pq)
            t_s2 = Sc.dma("pslab", lambda e, m2=m2, osl=osl: e.dma_start(
                out=sb_wsl[:, osl * 2:osl * 2 + 2, :, :].rearrange("p a k n -> p (a k) n"),
                in_=whatT_p[m2, 32:64].rearrange("k p f -> p k f")),
                waits=gsl_cons[osl * 2] + gsl_cons[osl * 2 + 1])
            for i4 in range(4):
                gsl_cons[i4] = []
            bank = bkA0 if sl == 0 else bkA1
            w0 = bank.wr_waits() + [t_s1, t_s2] + h_last
            last = None
            for kd in range(64):
                wsrc = (sb_wsl[:, sl * 2 + kd // 16, kd % 16, :] if kd < 32 else
                        sb_wsl[:, osl * 2 + (kd - 32) // 16, (kd - 32) % 16, :])
                last = Sc.op("tensor", lambda e, kd=kd, wsrc=wsrc, b=bank: e.matmul(
                    b.t[:, 0:512], wsrc, h_tile(kd), start=(kd == 0), stop=(kd == 63)),
                    waits=w0 if kd == 0 else [], sig=(kd == 63))
            bank.set_writer(last)
            for i4 in range(4):
                gsl_cons[i4].append(last)
            t_ev = Sc.op("scalar", lambda e, b=bank: e.activation(
                sb_evf[:], b.t[:, 0:512], AF.Identity),
                waits=bank.rd_waits() + evf_guard2)
            bank.add_reader(t_ev)
            t_evb = Sc.op("vector", lambda e: e.tensor_copy(
                sb_dump[:, 0:512], sb_evf[:]), waits=[t_ev] + dump_guard)
            evf_guard2 = [t_evb]
            dump_guard = []
            for p4 in range(4):
                cch = hf * 4 + p4
                tm = Sc.op("tensor", lambda e, p4=p4: e.transpose(
                    bkTB.t[:, 0:128], sb_dump[:, p4 * 128:(p4 + 1) * 128], sb_idbf[:]),
                    waits=bkTB.wr_waits() + [t_evb])
                bkTB.set_writer(tm)
                dump_guard.append(tm)
                te = Sc.op("vector", lambda e: e.tensor_copy(
                    sb_mtmp[:, 0, :], bkTB.t[:, 0:128]), waits=bkTB.rd_waits())
                bkTB.add_reader(te)
                td = Sc.dma("ffnw", lambda e, cch=cch, m2=m2: e.dma_start(
                    out=ffnsc[cch, :, m2 * 128:(m2 + 1) * 128], in_=sb_mtmp[:, 0, :]),
                    waits=[te])
                bkTB.add_reader(td)
                ffn_dmas.append(td)

    BAR4 = Sc.barrier(streams=("whatw",))

    out_dmas = []
    fo_cons = []
    for cch in range(NCH):
        sl = cch % 2
        t_f = Sc.dma("x", lambda e, cch=cch: e.dma_start(
            out=sb_fo2[:], in_=ffnsc[cch, :, :]),
            waits=ffn_dmas[-2:] + fo_cons + (BAR4 if cch < 2 else []))
        t_x2l = Sc.dma("x", lambda e, cch=cch, sl=sl: e.dma_start(
            out=sb_x2[:, sl, :], in_=x2sp[cch, :, :]),
            waits=x2_cons[sl] + (BAR4 if cch < 2 else []) + x2sp_toks[-1:])
        x2_cons[sl] = []
        _, st, t7 = emit_ln(sb_fo2[:], None, [t_f])
        t_nf = Sc.op("vector", lambda e, st=st: e.tensor_scalar(
            sb_fo[:], sb_fo2[:], st[:, 7:8], st[:, 8:9], ALU.mult, ALU.add),
            waits=[t7] + (BAR4 if cch == 0 else []))
        fo_cons = [t_nf]
        t_o = Sc.op("vector", lambda e, sl=sl: e.tensor_tensor(
            sb_outst[:, sl, :], sb_fo[:], sb_x2[:, sl, :], ALU.add),
            waits=[t_nf, t_x2l] + xc_cons[sl])
        xc_cons[sl] = []
        x2_cons[sl].append(t_o)
        td = Sc.dma("outw", lambda e, cch=cch, sl=sl: e.dma_start(
            out=out[cch * 128:(cch + 1) * 128, :], in_=sb_outst[:, sl, :]),
            waits=[t_o])
        xc_cons[sl].append(td)
        out_dmas.append(td)

    Sc.op("sync", lambda e: e.nop(), waits=[out_dmas[-1]], sig=False)
    Sc.build()
    n_inst = sum(len(v) for v in Sc.items.values())
    print(f"program instructions (approx): {n_inst}")
    return nc


def _host_inputs(inputs, core):
    b, qv = core // 4, core % 4
    start = qv * TOK
    x = np.asarray(inputs["L0_attn_hidden_states"], dtype=np.float32)[b]
    xb_ = np.ascontiguousarray(x)
    halo = np.zeros((128, D), np.float32) if qv == 0 else x[start - 128:start]
    xo_ = np.ascontiguousarray(np.concatenate([halo, x[start:start + TOK]], axis=0))
    pos = np.maximum(np.arange(start - 128, start + TOK), 0).astype(np.float32)
    inv_freq = 1.0 / 10000.0 ** (np.arange(32, dtype=np.float32) * 2.0 / 64.0)
    ang = pos[None, :] * inv_freq[:, None]
    cos, sin = np.cos(ang), np.sin(ang)
    rtcc_ = np.ascontiguousarray(np.concatenate([cos, cos], 0).astype(np.float32))
    rtns_ = np.ascontiguousarray(np.concatenate([-sin, sin], 0).astype(np.float32))
    bend = (np.arange(C) * R + (R - 1)).astype(np.float32)
    angc = bend[None, :] * inv_freq[:, None]
    cc_, sc_ = np.cos(angc), np.sin(angc)
    cktcc_ = np.ascontiguousarray(np.concatenate([cc_, cc_], 0).astype(np.float32))
    cktns_ = np.ascontiguousarray(np.concatenate([-sc_, sc_], 0).astype(np.float32))
    cmask = np.zeros((NCH, 128, C), np.float32)
    for cch in range(NCH):
        posr = start + cch * 128 + np.arange(128)
        cmask[cch] = np.where(bend[None, :] <= posr[:, None], 0.0, -1e9)
    lmask = np.zeros((2, 128, 256), np.float32)
    i_ = np.arange(128)[:, None]
    j_ = np.arange(256)[None, :]
    band = (j_ >= i_) & (j_ <= i_ + 128)
    lmask[1] = np.where(band, 0.0, -1e9)
    lmask[0] = np.where(band & (j_ >= 128), 0.0, -1e9) if qv == 0 else lmask[1]
    g = lambda k: np.ascontiguousarray(np.asarray(inputs[k], dtype=np.float32))
    return {
        "xb": xb_, "xo": xo_, "rtcc": rtcc_, "rtns": rtns_,
        "cktcc": cktcc_, "cktns": cktns_, "cm": cmask.astype(bfdt),
        "lm": lmask.astype(bfdt),
        "Wq": g("L0_attn_Wq"), "Wk": g("L0_attn_Wk"), "Wv": g("L0_attn_Wv"),
        "Wlk": g("L0_attn_Wlk"), "Wlv": g("L0_attn_Wlv"), "Wo": g("L0_attn_Wo"),
        "Wgu": g("L0_ffn_w_gate_up"), "Wp": g("L0_ffn_w_proj"),
        "bqT": np.ascontiguousarray(g("L0_attn_bq").reshape(16, 128).T),
        "blkT": np.ascontiguousarray(g("L0_attn_blk").reshape(16, 128).T),
        "blvT": np.ascontiguousarray(g("L0_attn_blv").reshape(16, 128).T),
        "boT": np.ascontiguousarray(g("L0_attn_bo").reshape(16, 128).T),
        "bkc": g("L0_attn_bk").reshape(128, 1),
        "bvc": g("L0_attn_bv").reshape(128, 1),
        "bcc": np.ascontiguousarray(np.broadcast_to(
            g("L0_attn_bc").reshape(1, 1), (128, 1))),
        "qnc": g("L0_attn_qn_w").reshape(128, 1),
        "knc": g("L0_attn_kn_w").reshape(128, 1),
        "sinkTf": np.ascontiguousarray(g("L0_attn_sink_k").T),
        "sinkvRf": g("L0_attn_sink_v").reshape(1, NH * HD),
        "wcrepf": np.ascontiguousarray(np.broadcast_to(
            g("L0_attn_Wc").reshape(1, D), (128, D))),
    }


_CACHE = {}


def kernel(**inputs):
    if "nc" not in _CACHE:
        _CACHE["nc"] = build_program()
    nc = _CACHE["nc"]
    in_maps = [_host_inputs(inputs, c) for c in range(8)]
    res = run_bass_kernel_spmd(nc, in_maps, core_ids=list(range(8)))
    outs = [np.asarray(res.results[c]["out"], dtype=np.float32) for c in range(8)]
    return np.stack([np.concatenate(outs[0:4], 0), np.concatenate(outs[4:8], 0)], 0)


if __name__ == "__main__":
    build_program()
    print("BUILD OK")


# version 13
# speedup vs baseline: 1.0235x; 1.0235x over previous
import math

import numpy as np
import ml_dtypes
import concourse.bass as bass
import concourse.mybir as mybir
from concourse.bass_utils import run_bass_kernel_spmd

F32 = mybir.dt.float32
BF16 = mybir.dt.bfloat16
U16 = mybir.dt.uint16
AF = mybir.ActivationFunctionType
ALU = mybir.AluOpType
AX = mybir.AxisListType
bfdt = ml_dtypes.bfloat16

B, S, D = 2, 4096, 2048
NH, HD = 16, 128
R, C, WIN = 16, 256, 128
HID2, H = 16384, 8192
TOK, TEXT, NCH, PFX = 1024, 1152, 8, 32
EPS_LN, EPS_RMS = 1e-5, 1e-6
SCALE = 1.0 / math.sqrt(HD)
SB_BASE = 16512
SB_BUDGET = 229376


class Sched:
    ENGINES = ("sync", "gpsimd", "scalar", "vector", "tensor")
    STREAMS = {
        "setup_s": ("sync", 1), "setup_p": ("gpsimd", 1),
        "x": ("sync", 2), "entw": ("sync", 2), "stk": ("sync", 1),
        "whatw": ("gpsimd", 2),
        "aoscw": ("sync", 2), "ffnw": ("sync", 2), "x2spw": ("sync", 1),
        "outw": ("sync", 1),
        "aslab": ("sync", 2), "qcast_gu": ("gpsimd", 2), "qcast_p": ("gpsimd", 2),
        "qcast_a": ("gpsimd", 2),
        "wt": ("scalar", 3), "gslab": ("scalar", 2), "pslab": ("scalar", 2),
    }

    def __init__(self, nc):
        self.nc = nc
        self.items = {e: [] for e in self.ENGINES}
        self.csem = {e: nc.alloc_semaphore(f"cs_{e}") for e in self.ENGINES}
        self.ccount = {e: 0 for e in self.ENGINES}
        self.qsem = {}
        self.qcount = {}
        self.qrr = {}
        for s, (q, ns) in self.STREAMS.items():
            for i in range(ns):
                self.qsem[(s, i)] = nc.alloc_semaphore(f"dq_{s}{i}")
                self.qcount[(s, i)] = 0
            self.qrr[s] = 0

    def op(self, eng, fn, waits=(), sig=True):
        tok = None
        if sig:
            self.ccount[eng] += 1
            tok = (self.csem[eng], self.ccount[eng])
        wl = [w for w in waits if w is not None]
        self.items[eng].append(("op", fn, wl, tok))
        return tok

    def dma(self, stream, fn, waits=()):
        q, ns = self.STREAMS[stream]
        i = self.qrr[stream]
        self.qrr[stream] = (i + 1) % ns
        key = (stream, i)
        prev = (self.qsem[key], self.qcount[key]) if self.qcount[key] else None
        self.qcount[key] += 16
        tok = (self.qsem[key], self.qcount[key])
        wl = [w for w in list(waits) + [prev] if w is not None]
        self.items[q].append(("dma", fn, wl, tok))
        return tok

    def barrier(self, streams=()):
        toks = []
        for e in ("gpsimd", "scalar", "vector", "tensor"):
            pv = (self.csem[e], self.ccount[e])
            toks.append(self.op(e, lambda eng: eng.nop(), waits=[pv], sig=True))
        for s in streams:
            for i in range(self.STREAMS[s][1]):
                if self.qcount[(s, i)]:
                    toks.append((self.qsem[(s, i)], self.qcount[(s, i)]))
        return toks

    def build(self):
        nc = self.nc
        with nc.Block() as block:
            def make(eng_name):
                def body(eng):
                    seen = {}
                    for kind, fn, waits, tok in self.items[eng_name]:
                        for (sem, val) in waits:
                            key = id(sem)
                            if seen.get(key, 0) < val:
                                eng.wait_ge(sem, val)
                                seen[key] = val
                        inst = fn(eng)
                        if tok is not None:
                            inst.then_inc(tok[0], 16 if kind == "dma" else 1)
                return body
            block.sync(make("sync"))
            block.gpsimd(make("gpsimd"))
            block.scalar(make("scalar"))
            block.vector(make("vector"))
            block.tensor(make("tensor"))


class Bank:
    def __init__(self, tile):
        self.t = tile
        self.readers = []
        self.writer = None

    def wr_waits(self):
        w = list(self.readers) + ([self.writer] if self.writer else [])
        self.readers = []
        return w

    def set_writer(self, tok):
        assert tok is not None
        self.writer = tok

    def rd_waits(self):
        return [self.writer]

    def add_reader(self, tok):
        self.readers.append(tok)


def build_program():
    nc = bass.Bass("TRN2", target_bir_lowering=False)
    Sc = Sched(nc)

    def ext(name, shape, dtype=F32):
        return nc.dram_tensor(name, list(shape), dtype, kind="ExternalInput").ap()

    xb = ext("xb", [S, D])
    xo = ext("xo", [TEXT, D])
    rtcc = ext("rtcc", [64, TEXT]); rtns = ext("rtns", [64, TEXT])
    cktcc = ext("cktcc", [64, C]); cktns = ext("cktns", [64, C])
    cmv = ext("cm", [NCH, 128, C], BF16); lmv = ext("lm", [2, 128, 256], BF16)
    Wq = ext("Wq", [D, D]); Wk = ext("Wk", [D, HD]); Wv = ext("Wv", [D, HD])
    Wlk = ext("Wlk", [D, D]); Wlv = ext("Wlv", [D, D]); Wo = ext("Wo", [D, D])
    Wgu = ext("Wgu", [HID2, D]); Wp = ext("Wp", [D, H])
    bqT = ext("bqT", [128, 16]); blkT = ext("blkT", [128, 16])
    blvT = ext("blvT", [128, 16]); boT = ext("boT", [128, 16])
    bkc = ext("bkc", [128, 1]); bvc = ext("bvc", [128, 1]); bcc = ext("bcc", [128, 1])
    qnc = ext("qnc", [128, 1]); knc = ext("knc", [128, 1])
    sinkTf = ext("sinkTf", [128, 16]); sinkvRf = ext("sinkvRf", [1, NH * HD])
    wcrepf = ext("wcrepf", [128, D])

    out = nc.dram_tensor("out", [TOK, D], F32, kind="ExternalOutput").ap()

    wbf_gu = nc.dram_tensor("wbf_gu", [HID2, D], BF16, kind="Internal").ap()
    wbf_q = nc.dram_tensor("wbf_q", [D, D], BF16, kind="Internal").ap()
    wbf_lk = nc.dram_tensor("wbf_lk", [D, D], BF16, kind="Internal").ap()
    wbf_lv = nc.dram_tensor("wbf_lv", [D, D], BF16, kind="Internal").ap()
    wbf_o = nc.dram_tensor("wbf_o", [D, D], BF16, kind="Internal").ap()
    wbf_p = nc.dram_tensor("wbf_p", [D, H], BF16, kind="Internal").ap()
    whatT_gu = nc.dram_tensor("whatT_gu", [16, 128, HID2], BF16, kind="Internal").ap()
    whatT_p = nc.dram_tensor("whatT_p", [16, 64, 128, 128], BF16, kind="Internal").ap()
    x2sp = nc.dram_tensor("x2sp", [NCH, 128, D], F32, kind="Internal").ap()
    aosc = nc.dram_tensor("aosc", [NCH, 128, D], F32, kind="Internal").ap()
    ffnsc = nc.dram_tensor("ffnsc", [NCH, 128, D], F32, kind="Internal").ap()
    entsc_d = nc.dram_tensor("entsc_d", [PFX, 8, D], F32, kind="Internal").ap()

    idn_bf_c = nc.inline_tensor(np.eye(128, dtype=bfdt), name="idn_bf").ap()
    ones_col_c = nc.inline_tensor(np.ones((128, 1), dtype=bfdt), name="ones_col").ap()
    ones_row_c = nc.inline_tensor(np.ones((1, 128), dtype=np.float32), name="ones_row").ap()
    patt = np.zeros((128, 8), dtype=np.float32)
    for i in range(128):
        patt[i, i // 16] = 1.0
    patt8_c = nc.inline_tensor(patt, name="patt8").ap()
    patt8T_c = nc.inline_tensor(np.ascontiguousarray(patt.T), name="patt8T").ap()
    patt8b_c = nc.inline_tensor(patt.astype(bfdt), name="patt8b").ap()
    i2 = np.zeros((128, 2), dtype=bfdt); i2[:64, 0] = 1.0 / 128; i2[64:, 1] = 1.0 / 128
    I2_c = nc.inline_tensor(i2, name="I2").ap()
    e2 = np.zeros((2, 128), dtype=bfdt); e2[0, :64] = 1; e2[1, 64:] = 1
    E2_c = nc.inline_tensor(e2, name="E2").ap()
    sw = np.zeros((64, 64), dtype=bfdt)
    for i in range(64):
        sw[i, (i + 32) % 64] = 1
    SW64_c = nc.inline_tensor(sw, name="SW64").ap()
    cstv = np.zeros((128, 4), dtype=np.float32)
    cstv[:, 0] = EPS_LN; cstv[:, 1] = EPS_RMS; cstv[:, 2] = SCALE; cstv[:, 3] = 0.5
    cst_c = nc.inline_tensor(cstv, name="cstc").ap()

    cur = [SB_BASE]

    def sba(name, shape, dtype, at=None, track=True):
        if at is None:
            cur[0] = (cur[0] + 63) // 64 * 64
            at = cur[0]
            if track:
                cur[0] = at + int(np.prod(shape[1:])) * mybir.dt.size(dtype)
                assert cur[0] <= SB_BUDGET, f"SBUF overflow at {name}: {cur[0]}"
        return nc.alloc_sbuf_tensor_at(name, list(shape), dtype, offset=at)

    sb_idbf = sba("sb_idbf", [128, 128], BF16)
    sb_onesc = sba("sb_onesc", [128, 1], BF16)
    sb_onesr = sba("sb_onesr", [1, 128], F32)
    sb_patt8 = sba("sb_patt8", [128, 8], F32)
    sb_patt8T = sba("sb_patt8T", [8, 128], F32)
    sb_patt8b = sba("sb_patt8b", [128, 8], BF16)
    sb_I2 = sba("sb_I2", [128, 2], BF16)
    sb_E2 = sba("sb_E2", [2, 128], BF16)
    sb_SW = sba("sb_SW", [64, 64], BF16)
    sb_bqT = sba("sb_bqT", [128, 16], F32)
    sb_blkT = sba("sb_blkT", [128, 16], F32)
    sb_blvT = sba("sb_blvT", [128, 16], F32)
    sb_boT = sba("sb_boT", [128, 16], F32)
    sb_cols = sba("sb_cols", [128, 5], F32)
    sb_sinkT = sba("sb_sinkT", [128, 16], BF16)
    sb_sinkvR = sba("sb_sinkvR", [1, NH * HD], BF16)
    sb_rtcc = sba("sb_rtcc", [64, TEXT], BF16)
    sb_rtns = sba("sb_rtns", [64, TEXT], BF16)
    sb_ckcc = sba("sb_ckcc", [64, C], BF16)
    sb_ckns = sba("sb_ckns", [64, C], BF16)
    sb_ckr = sba("sb_ckr", [128, C], BF16)
    sb_cvT = sba("sb_cvT", [128, 2, 128], BF16)
    sb_cst = sba("sb_cst", [128, 4], F32)
    sb_cmask = sba("sb_cmask", [128, NCH, C], BF16)
    sb_lmask = sba("sb_lmask", [128, 2, 256], BF16)

    r1 = (cur[0] + 63) // 64 * 64
    cur[0] = r1
    sb_xnT = sba("sb_xnT", [128, 16, TEXT], BF16)
    r1b = (cur[0] + 63) // 64 * 64
    cur[0] = r1b
    sb_merged = sba("sb_merged", [128, 16, TOK], BF16)
    r1_end = cur[0]
    sb_wcrep = sba("sb_wcrep", [128, D], BF16, at=r1)
    sb_x2 = sba("sb_x2", [128, 2, D], F32, at=r1)
    sb_outst = sba("sb_outst", [128, 2, D], F32, at=r1 + 16384)
    sb_fo = sba("sb_fo", [128, D], F32, at=r1b)
    sb_fo2 = sba("sb_fo2", [128, D], F32, at=r1b + 8192)
    sb_ent = sba("sb_ent", [128, 2, D], F32, at=r1b)
    sb_entbf = sba("sb_entbf", [128, 2, D], BF16, at=r1b + 16384)
    sb_entsc = sba("sb_entsc", [8, D], F32, at=r1b + 24576)
    sb_hh = sba("sb_hh", [128, 64, 512], BF16, at=r1)

    def h_tile(j):
        return sb_hh[:, j, :]

    r2 = (r1_end + 63) // 64 * 64
    cur[0] = r2
    sb_xn2T = sba("sb_xn2T", [128, 16, TOK], BF16)
    r2_end = cur[0]
    a = r2
    sb_qf = sba("sb_qf", [128, TEXT], F32, at=a); a += TEXT * 4
    sb_rsqb = sba("sb_rsqb", [128, TEXT], F32, at=a); a += TEXT * 4
    sb_row = sba("sb_row", [1, TEXT], F32, at=a); a += TEXT * 4
    sb_qs = sba("sb_qs", [128, TEXT], BF16, at=a); a += TEXT * 2
    sb_usw = sba("sb_usw", [64, TEXT], BF16, at=a); a += TEXT * 2
    sb_u = sba("sb_u", [64, TEXT], BF16, at=a); a += TEXT * 2
    sb_v = sba("sb_v", [64, TEXT], BF16, at=a); a += TEXT * 2
    sb_lv = sba("sb_lv", [128, TEXT], BF16, at=a); a += TEXT * 2
    sb_qr2 = sba("sb_qr2", [128, 2, TOK], BF16, at=a); a += 2 * TOK * 2
    assert a <= r2_end, (a, r2_end)

    cur[0] = r2_end
    sb_xc = sba("sb_xc", [128, 2, D], F32)
    sb_xnbf = sba("sb_xnbf", [128, 2, D], BF16)
    sb_dump = sba("sb_dump", [128, D], BF16)
    sb_stat = sba("sb_stat", [128, 26], F32)
    sc_at = (cur[0] + 63) // 64 * 64
    sb_sc = sba("sb_sc", [128, 2, 512], F32)
    sb_evf = sba("sb_evf", [128, 512], F32, at=sc_at)
    sb_esc = sba("sb_esc", [128, 2, 512], BF16)
    sb_attbf = sba("sb_attbf", [128, 2, 512], BF16)
    sb_attT = sba("sb_attT", [128, 2, 512], BF16)
    sb_attS = sba("sb_attS", [1, 2, 128], BF16)
    sb_mtmp = sba("sb_mtmp", [128, 2, 128], F32)
    sb_snk = sba("sb_snk", [128, 2, 20], F32)
    sb_lkr2 = sba("sb_lkr2", [128, 2, TEXT], BF16)
    sb_lvT2 = sba("sb_lvT2", [128, 2, 9, 128], BF16)
    sb_ask = sba("sb_ask", [128, 2], BF16)
    sb_ckf = sba("sb_ckf", [128, C], F32, at=r2)
    sb_ckq = sba("sb_ckq", [128, C], BF16, at=r2 + 1024)
    sb_wsl = sba("sb_wsl", [128, 4, 16, 128], BF16)
    sb_wt = sba("sb_wt", [128, 2, 1024], BF16)
    sb_absq = sba("sb_absq", [128, 2, 1024], BF16)
    sb_shs = sba("sb_shs", [2, 4, 512], BF16)
    sb_shf = sba("sb_shf", [128, 4, 512], BF16)
    sb_mq = sba("sb_mq", [128, 1, 512], BF16)
    sb_sgq = sba("sb_sgq", [128, 1, 512], U16)
    sb_ssq = sba("sb_ssq", [128, 1, 512], U16)
    sb_what = sba("sb_what", [128, 3, 1024], BF16)

    assert cur[0] <= SB_BUDGET, cur[0]

    bkA0 = Bank(nc.alloc_psum_tensor("pbA0", [128, 512], F32))
    bkA1 = Bank(nc.alloc_psum_tensor("pbA1", [128, 512], F32))
    bkB0 = Bank(nc.alloc_psum_tensor("pbB0", [128, 512], F32))
    bkB1 = Bank(nc.alloc_psum_tensor("pbB1", [128, 512], F32))
    bkTB = Bank(nc.alloc_psum_tensor("ptrb", [128, 1024], BF16))
    bkTF = Bank(nc.alloc_psum_tensor("ptrf", [128, 512], F32))
    bkSC = Bank(nc.alloc_psum_tensor("psc", [128, 512], F32))
    bkMS = Bank(nc.alloc_psum_tensor("pmisc", [128, 512], F32))

    for dst, src in ((sb_idbf, idn_bf_c), (sb_onesc, ones_col_c), (sb_onesr, ones_row_c),
                     (sb_patt8, patt8_c), (sb_patt8T, patt8T_c), (sb_patt8b, patt8b_c),
                     (sb_I2, I2_c), (sb_E2, E2_c), (sb_SW, SW64_c), (sb_cst, cst_c),
                     (sb_bqT, bqT), (sb_blkT, blkT), (sb_blvT, blvT), (sb_boT, boT)):
        Sc.dma("setup_s", lambda e, d=dst, s=src: e.dma_start(out=d[:], in_=s))
    for i, src in enumerate((bkc, bvc, bcc, qnc, knc)):
        Sc.dma("setup_s", lambda e, i=i, s=src: e.dma_start(out=sb_cols[:, i:i + 1], in_=s))
    Sc.dma("setup_s", lambda e: e.dma_start(out=sb_cmask[:], in_=cmv.rearrange("c p n -> p c n")))
    Sc.dma("setup_s", lambda e: e.dma_start(out=sb_lmask[:], in_=lmv.rearrange("c p n -> p c n")))
    for dst, src in ((sb_sinkT, sinkTf), (sb_sinkvR, sinkvRf), (sb_wcrep, wcrepf),
                     (sb_rtcc, rtcc), (sb_rtns, rtns), (sb_ckcc, cktcc), (sb_ckns, cktns)):
        Sc.dma("setup_p", lambda e, d=dst, s=src: e.dma_start(out=d[:], in_=s))
    Sc.dma("setup_p", lambda e: e.dma_start(
        out=sb_wsl[:, 2, :, :], in_=Wk.rearrange("(k p) n -> p k n", p=128)))
    Sc.dma("setup_p", lambda e: e.dma_start(
        out=sb_wsl[:, 3, :, :], in_=Wv.rearrange("(k p) n -> p k n", p=128)))
    qcast_gu = [Sc.dma("qcast_gu", lambda e, rr=rr: e.dma_start(
        out=wbf_gu[rr * 4096:(rr + 1) * 4096, :], in_=Wgu[rr * 4096:(rr + 1) * 4096, :]))
        for rr in range(4)]
    qcast_p = [Sc.dma("qcast_p", lambda e, rr=rr: e.dma_start(
        out=wbf_p[rr * 512:(rr + 1) * 512, :], in_=Wp[rr * 512:(rr + 1) * 512, :]))
        for rr in range(4)]
    qcast_a = {}
    for nm, (dst, srcw) in {"q": (wbf_q, Wq), "lk": (wbf_lk, Wlk),
                            "lv": (wbf_lv, Wlv), "o": (wbf_o, Wo)}.items():
        qcast_a[nm] = Sc.dma("qcast_a", lambda e, d=dst, s=srcw: e.dma_start(
            out=d, in_=s))
    BAR0 = Sc.barrier(streams=("setup_s", "setup_p"))

    qslot = [0]
    qcons = [[], []]
    qring = [[], [], [], []]
    gsring = [[], [], [], []]
    shf_cons = [[], [], [], []]
    qstate = {}
    wtok = {}
    bguard = [[], []]
    wslot = [0]
    whatg = [[], [], []]

    def emit_qA(which, k, hb):
        sl = qslot[0]; qslot[0] ^= 1
        if which == "gu":
            src = wbf_gu[hb * 1024:(hb + 1) * 1024, k * 128:(k + 1) * 128]
            cast_tok = [qcast_gu[hb // 4]]
        else:
            src = wbf_p[k * 1024:(k + 1) * 1024, hb * 128:(hb + 1) * 128]
            cast_tok = [qcast_p[2 * k], qcast_p[2 * k + 1]]
        t_tr = Sc.dma("wt", lambda e, src=src, sl=sl: e.dma_start(
            out=sb_wt[:, sl, :], in_=src, transpose=True),
            waits=cast_tok + qcons[sl] + BAR0)
        qcons[sl] = []
        t_abs = Sc.op("vector", lambda e, sl=sl: e.tensor_scalar(
            sb_absq[:, sl, :].bitcast(U16), sb_wt[:, sl, :].bitcast(U16), 0x7FFF,
            None, ALU.bitwise_and), waits=[t_tr])
        qcons[sl].append(t_abs)
        tshf2 = []
        for q2 in range(2):
            fs = q2 * 512
            ssl = sl * 2 + q2
            tm = Sc.op("tensor", lambda e, fs=fs, sl=sl: e.matmul(
                bkMS.t[0:2, 0:512], sb_I2[:], sb_absq[:, sl, fs:fs + 512],
                start=True, stop=True), waits=bkMS.wr_waits() + [t_abs])
            bkMS.set_writer(tm)
            tse = Sc.op("scalar", lambda e, ssl=ssl: e.activation(
                sb_shs[:, ssl, :], bkMS.t[0:2, 0:512], AF.Identity),
                waits=bkMS.rd_waits() + gsring[ssl])
            bkMS.add_reader(tse)
            tex = Sc.op("tensor", lambda e, ssl=ssl: e.matmul(
                bkTF.t[:, 0:512], sb_E2[:], sb_shs[:, ssl, :], start=True, stop=True),
                waits=bkTF.wr_waits() + [tse])
            gsring[ssl] = [tex]
            bkTF.set_writer(tex)
            tshf = Sc.op("scalar", lambda e, ssl=ssl: e.activation(
                sb_shf[:, ssl, :], bkTF.t[:, 0:512], AF.Identity),
                waits=bkTF.rd_waits() + shf_cons[ssl])
            shf_cons[ssl] = []
            bkTF.add_reader(tshf)
            tshf2.append(tshf)
        qstate[sl] = (which, k, hb, tshf2)

    def emit_qB(sl):
        which, k, hb, tshf2 = qstate[sl]
        w3 = wslot[0]
        wslot[0] = (w3 + 1) % 3
        outs = []
        for q2 in range(2):
            fs = q2 * 512
            ssl = sl * 2 + q2
            tmq = Sc.op("vector", lambda e, fs=fs, sl=sl, ssl=ssl: e.tensor_tensor(
                sb_mq[:, 0, :], sb_absq[:, sl, fs:fs + 512], sb_shf[:, ssl, :],
                ALU.is_gt), waits=[tshf2[q2]] + qring[ssl] + bguard[0])
            tsg = Sc.op("vector", lambda e, fs=fs, sl=sl, q2=q2: e.tensor_scalar(
                sb_sgq[:, 0, :], sb_wt[:, sl, fs:fs + 512].bitcast(U16), 0x8000,
                None, ALU.bitwise_and), waits=qring[ssl] + bguard[1])
            tss = Sc.op("vector", lambda e, q2=q2, ssl=ssl: e.tensor_tensor(
                sb_ssq[:, 0, :], sb_sgq[:, 0, :], sb_shf[:, ssl, :].bitcast(U16),
                ALU.bitwise_or), waits=[tsg, tmq] + bguard[0])
            shf_cons[ssl].append(tss)
            twh = Sc.op("vector", lambda e, fs=fs, w3=w3, q2=q2: e.scalar_tensor_tensor(
                sb_what[:, w3, fs:fs + 512], sb_mq[:, 0, :], 2.0,
                sb_ssq[:, 0, :].bitcast(BF16), ALU.mult, ALU.mult),
                waits=[tss] + (whatg[w3] if q2 == 0 else []))
            bguard[0] = [twh]
            bguard[1] = [tss]
            qring[ssl] = [twh]
            outs.append(twh)
            qcons[sl].append(tmq)
            qcons[sl].append(tsg)
        if which == "gu":
            dst = whatT_gu[k, :, hb * 1024:(hb + 1) * 1024]
        else:
            dst = whatT_p[k * 8:(k + 1) * 8, hb, :, :].rearrange("m p f -> p m f")
        t_out = Sc.dma("whatw", lambda e, dst=dst, w3=w3: e.dma_start(
            out=dst, in_=sb_what[:, w3, :].rearrange("p (j f) -> p j f", f=128)),
            waits=outs)
        whatg[w3] = [t_out]
        if which == "gu":
            for j in range(hb * 8, hb * 8 + 8):
                wtok[("gu", j, k)] = t_out
        else:
            wtok[("p", k, hb)] = t_out

    lnst = {"sl": 0, "cons": [[], []]}
    dump_guard = []

    def emit_ln(x_ap, out_bf_ap, deps):
        nonlocal dump_guard
        sl = lnst["sl"]; lnst["sl"] ^= 1
        st = sb_stat[:, sl * 10:sl * 10 + 10]
        cons = lnst["cons"][sl]
        t_sum = Sc.op("vector", lambda e: e.reduce_sum(st[:, 0:1], x_ap, AX.X),
                      waits=deps + cons)
        t_sq = Sc.op("scalar", lambda e: e.activation(
            sb_dump[:], x_ap, AF.Square, accum_out=st[:, 1:2]),
            waits=deps + cons + dump_guard)
        dump_guard = [t_sq]
        lnst["cons"][sl] = []
        cons = lnst["cons"][sl]
        t1 = Sc.op("vector", lambda e: e.tensor_scalar(
            st[:, 2:3], st[:, 0:1], 1.0 / D, None, ALU.mult), waits=[t_sum])
        t2 = Sc.op("vector", lambda e: e.tensor_scalar(
            st[:, 3:4], st[:, 1:2], 1.0 / D, None, ALU.mult), waits=[t_sq])
        t3 = Sc.op("vector", lambda e: e.tensor_tensor(
            st[:, 4:5], st[:, 2:3], st[:, 2:3], ALU.mult), waits=[t1])
        t4 = Sc.op("vector", lambda e: e.tensor_tensor(
            st[:, 5:6], st[:, 3:4], st[:, 4:5], ALU.subtract), waits=[t2, t3])
        t5 = Sc.op("scalar", lambda e: e.activation(
            st[:, 6:7], st[:, 5:6], AF.Sqrt, bias=sb_cst[:, 0:1]), waits=[t4])
        t6 = Sc.op("vector", lambda e: e.reciprocal(st[:, 7:8], st[:, 6:7]), waits=[t5])
        t7 = Sc.op("vector", lambda e: e.scalar_tensor_tensor(
            st[:, 8:9], st[:, 2:3], -1.0, st[:, 7:8], ALU.mult, ALU.mult), waits=[t6])
        tn = None
        if out_bf_ap is not None:
            tn = Sc.op("scalar", lambda e: e.activation(
                out_bf_ap, x_ap, AF.Identity, bias=st[:, 8:9], scale=st[:, 7:8]),
                waits=[t7])
            cons.append(tn)
        return tn, st, t7

    quant_iter = ([("gu", k, hb) for hb in range(16) for k in range(16)] +
                  [("p", k2, hb) for k2 in range(2) for hb in range(64)])
    qi = [0]
    qpend = []

    def pump_quant(n):
        for _ in range(n):
            if qi[0] < len(quant_iter):
                sl_next = qslot[0]
                emit_qA(*quant_iter[qi[0]])
                qi[0] += 1
                qpend.append(sl_next)
            if len(qpend) > 1:
                emit_qB(qpend.pop(0))

    def drain_quant():
        while qpend:
            emit_qB(qpend.pop(0))

    xc_cons = [[], []]
    xn_cons = [[], []]
    ent_dmas = []
    t_blk_g = [None]
    x_pend = {}

    def issue_x(i):
        sl = i % 2
        x_pend[i] = Sc.dma("x", lambda e, i=i, sl=sl: e.dma_start(
            out=sb_xc[:, sl, :], in_=xb[i * 128:(i + 1) * 128, :]),
            waits=xc_cons[sl] + BAR0)
        xc_cons[sl] = []

    issue_x(0)
    for i in range(PFX):
        sl = i % 2
        t_x = x_pend.pop(i)
        if i + 1 < PFX:
            issue_x(i + 1)
        t_xn, _, _ = emit_ln(sb_xc[:, sl, :], sb_xnbf[:, sl, :], [t_x] + xn_cons[sl])
        xn_cons[sl] = []
        xc_cons[sl].append(t_xn)
        t_wc = Sc.op("vector", lambda e, sl=sl: e.scalar_tensor_tensor(
            sb_dump[:], sb_xnbf[:, sl, :], 1.0, sb_wcrep[:], ALU.mult, ALU.mult,
            accum_out=sb_stat[:, 20:21]), waits=[t_xn] + dump_guard)
        dump_guard = [t_wc]
        if i >= 10:
            pump_quant(3)
        t_ew = Sc.op("scalar", lambda e: e.activation(
            sb_stat[:, 21:22], sb_stat[:, 20:21], AF.Exp, bias=sb_cols[:, 2:3]),
            waits=[t_wc])
        t_sm = Sc.op("tensor", lambda e: e.matmul(
            bkMS.t[0:8, 0:1], sb_patt8[:], sb_stat[:, 21:22], start=True, stop=True),
            waits=bkMS.wr_waits() + [t_ew])
        bkMS.set_writer(t_sm)
        t_ev = Sc.op("vector", lambda e: e.tensor_copy(
            sb_stat[0:8, 22:23], bkMS.t[0:8, 0:1]), waits=bkMS.rd_waits())
        bkMS.add_reader(t_ev)
        t_rc = Sc.op("vector", lambda e: e.reciprocal(
            sb_stat[0:8, 23:24], sb_stat[0:8, 22:23]), waits=[t_ev])
        t_blk = Sc.op("vector", lambda e: e.tensor_scalar(
            sb_attbf[:, 0, 1:9], sb_patt8b[:], sb_stat[:, 21:22], None, ALU.mult),
            waits=[t_ew, t_blk_g[0]] if t_blk_g[0] else [t_ew])
        t_blk_g[0] = None
        if i >= 10:
            pump_quant(3)
        ev_toks = []
        tm = None
        for q4 in range(4):
            bnk = bkSC if q4 % 2 == 0 else bkB0
            tm = Sc.op("tensor", lambda e, q4=q4, sl=sl, b=bnk: e.matmul(
                b.t[0:8, 0:512], sb_attbf[:, 0, 1:9],
                sb_xnbf[:, sl, q4 * 512:(q4 + 1) * 512], start=True, stop=True),
                waits=bnk.wr_waits() + [t_blk])
            bnk.set_writer(tm)
            te = Sc.op("vector", lambda e, q4=q4, b=bnk: e.tensor_scalar(
                sb_entsc[:, q4 * 512:(q4 + 1) * 512], b.t[0:8, 0:512],
                sb_stat[0:8, 23:24], None, ALU.mult),
                waits=bnk.rd_waits() + [t_rc] + (ent_dmas[-2:] if q4 == 0 else []))
            bnk.add_reader(te)
            ev_toks.append(te)
            t_blk_g[0] = tm
        xn_cons[sl] = [tm, t_wc]
        t_ed = Sc.dma("entw", lambda e, i=i: e.dma_start(
            out=entsc_d[i, :, :], in_=sb_entsc[:]), waits=ev_toks)
        ent_dmas.append(t_ed)
        xn_cons[sl].append(t_ed)
        if i >= 10:
            pump_quant(6)

    t_stk = Sc.dma("stk", lambda e: e.dma_start(
        out=sb_ent[:],
        in_=entsc_d.rearrange("c e d -> (c e) d").rearrange("(t p) d -> p t d", p=128)),
        waits=ent_dmas)
    t_eb = [Sc.op("vector", lambda e, tt=tt: e.tensor_copy(
        sb_entbf[:, tt, :], sb_ent[:, tt, :]), waits=[t_stk]) for tt in range(2)]
    entT_toks = []
    for kd in range(16):
        for tt in range(2):
            tm = Sc.op("tensor", lambda e, kd=kd, tt=tt: e.transpose(
                bkTB.t[:, 0:128], sb_entbf[:, tt, kd * 128:(kd + 1) * 128], sb_idbf[:]),
                waits=bkTB.wr_waits() + t_eb)
            bkTB.set_writer(tm)
            te = Sc.op("vector", lambda e, kd=kd, tt=tt: e.tensor_copy(
                sb_xnT[:, kd, tt * 128:(tt + 1) * 128], bkTB.t[:, 0:128]),
                waits=bkTB.rd_waits())
            bkTB.add_reader(te)
            entT_toks.append(te)

    def emit_ckcv(is_ck):
        slab = 2 if is_ck else 3
        w0 = bkSC.wr_waits() + entT_toks + BAR0
        last = None
        for kd in range(16):
            last = Sc.op("tensor", lambda e, kd=kd, slab=slab: e.matmul(
                bkSC.t[:, 0:256], sb_wsl[:, slab, kd, :], sb_xnT[:, kd, 0:256],
                start=(kd == 0), stop=(kd == 15)),
                waits=w0 if kd == 0 else [], sig=(kd == 15))
        bkSC.set_writer(last)
        bias = sb_cols[:, 0:1] if is_ck else sb_cols[:, 1:2]
        t_ev = Sc.op("scalar", lambda e, bias=bias: e.activation(
            sb_ckf[:], bkSC.t[:, 0:256], AF.Identity, bias=bias),
            waits=bkSC.rd_waits())
        bkSC.add_reader(t_ev)
        return t_ev, last

    t_ckf, ck_mm = emit_ckcv(True)
    t_cq = Sc.op("scalar", lambda e: e.activation(sb_ckq[:], sb_ckf[:], AF.Square),
                 waits=[t_ckf])
    t_cs = Sc.op("tensor", lambda e: e.matmul(
        bkMS.t[0:1, 0:256], sb_onesc[:], sb_ckq[:], start=True, stop=True),
        waits=bkMS.wr_waits() + [t_cq])
    bkMS.set_writer(t_cs)
    t_ce = Sc.op("vector", lambda e: e.tensor_scalar(
        sb_row[0:1, 0:256], bkMS.t[0:1, 0:256], 1.0 / HD, None, ALU.mult),
        waits=bkMS.rd_waits())
    bkMS.add_reader(t_ce)
    t_cs2 = Sc.op("scalar", lambda e: e.activation(
        sb_row[0:1, 256:512], sb_row[0:1, 0:256], AF.Sqrt, bias=sb_cst[0:1, 1:2]),
        waits=[t_ce])
    t_cr = Sc.op("vector", lambda e: e.reciprocal(
        sb_row[0:1, 512:768], sb_row[0:1, 256:512]), waits=[t_cs2])
    t_cb = Sc.op("tensor", lambda e: e.matmul(
        bkTF.t[:, 0:256], sb_onesr[:], sb_row[0:1, 512:768], start=True, stop=True),
        waits=bkTF.wr_waits() + [t_cr])
    bkTF.set_writer(t_cb)
    t_cbe = Sc.op("vector", lambda e: e.tensor_copy(sb_rsqb[:, 0:256], bkTF.t[:, 0:256]),
                  waits=bkTF.rd_waits())
    bkTF.add_reader(t_cbe)
    t_ct = Sc.op("vector", lambda e: e.scalar_tensor_tensor(
        sb_ckr[:], sb_ckf[:], sb_cols[:, 4:5], sb_rsqb[:, 0:256], ALU.mult, ALU.mult),
        waits=[t_cbe])
    t_csw = Sc.op("tensor", lambda e: e.matmul(
        bkMS.t[0:64, 0:256], sb_SW[:], sb_ckr[0:64, :], start=True, stop=True),
        waits=bkMS.wr_waits() + [t_ct])
    bkMS.set_writer(t_csw)
    t_cswe = Sc.op("vector", lambda e: e.tensor_copy(sb_usw[:, 0:256], bkMS.t[0:64, 0:256]),
                   waits=bkMS.rd_waits())
    bkMS.add_reader(t_cswe)
    t_cu = Sc.op("vector", lambda e: e.tensor_tensor(
        sb_u[:, 0:256], sb_ckr[0:64, :], sb_ckcc[:], ALU.mult), waits=[t_ct])
    t_cv2 = Sc.op("vector", lambda e: e.tensor_tensor(
        sb_v[:, 0:256], sb_usw[:, 0:256], sb_ckns[:], ALU.mult), waits=[t_cswe])
    t_ckdone = Sc.op("vector", lambda e: e.tensor_tensor(
        sb_ckr[0:64, :], sb_u[:, 0:256], sb_v[:, 0:256], ALU.add),
        waits=[t_cu, t_cv2])
    t_cvf, cv_mm = emit_ckcv(False)
    t_cvb = Sc.op("vector", lambda e: e.tensor_copy(sb_ckq[:], sb_ckf[:]), waits=[t_cvf])
    cvT_toks = []
    for tt in range(2):
        tm = Sc.op("tensor", lambda e, tt=tt: e.transpose(
            bkTB.t[:, 0:128], sb_ckq[:, tt * 128:(tt + 1) * 128], sb_idbf[:]),
            waits=bkTB.wr_waits() + [t_cvb])
        bkTB.set_writer(tm)
        te = Sc.op("vector", lambda e, tt=tt: e.tensor_copy(
            sb_cvT[:, tt, :], bkTB.t[:, 0:128]), waits=bkTB.rd_waits())
        bkTB.add_reader(te)
        cvT_toks.append(te)
    ckcv_toks = [t_ckdone] + cvT_toks
    BARP = Sc.barrier(streams=("entw", "stk"))
    slab_seed = {2: [ck_mm], 3: [cv_mm]}

    xnT_last = []
    x2_pend = {}

    def issue_x2(i):
        sl = i % 2
        x2_pend[i] = Sc.dma("x", lambda e, i=i, sl=sl: e.dma_start(
            out=sb_xc[:, sl, :], in_=xo[i * 128:(i + 1) * 128, :]),
            waits=xc_cons[sl] + (BARP if i < 2 else []))
        xc_cons[sl] = []

    issue_x2(0)
    for i in range(9):
        sl = i % 2
        t_x = x2_pend.pop(i)
        if i + 1 < 9:
            issue_x2(i + 1)
        t_xn, _, _ = emit_ln(sb_xc[:, sl, :], sb_xnbf[:, sl, :], [t_x] + xn_cons[sl])
        xn_cons[sl] = []
        xc_cons[sl].append(t_xn)
        for kd in range(16):
            tm = Sc.op("tensor", lambda e, kd=kd, sl=sl: e.transpose(
                bkTB.t[:, 0:128], sb_xnbf[:, sl, kd * 128:(kd + 1) * 128], sb_idbf[:]),
                waits=bkTB.wr_waits() + [t_xn] + (BARP if i == 0 else []))
            bkTB.set_writer(tm)
            te = Sc.op("vector", lambda e, kd=kd, i=i: e.tensor_copy(
                sb_xnT[:, kd, i * 128:(i + 1) * 128], bkTB.t[:, 0:128]),
                waits=bkTB.rd_waits())
            bkTB.add_reader(te)
            if kd == 7:
                pump_quant(4)
            if kd == 15:
                xn_cons[sl].append(tm)
                xnT_last = [te]
        pump_quant(4)
    BAR1 = Sc.barrier()

    slab_cons = {i: slab_seed.get(i, []) for i in range(4)}
    slab_rot = [0]
    merged_toks = []
    def next_slab():
        s = slab_rot[0]
        slab_rot[0] = (slab_rot[0] + 1) % 4
        return s

    def emit_proj(Wsrc, cast_tok, h, bias_col, c0, ncol, out_f32, deps, out_bf=None,
                  bank_fixed=None):
        slab = next_slab()
        t_sl = Sc.dma("aslab", lambda e, slab=slab: e.dma_start(
            out=sb_wsl[:, slab, :, :],
            in_=Wsrc[:, h * 128:(h + 1) * 128].rearrange("(k p) n -> p k n", p=128)),
            waits=slab_cons[slab] + [cast_tok])
        slab_cons[slab] = []
        ev_toks = []
        for nb in range((ncol + 511) // 512):
            f0 = nb * 512
            cw = min(512, ncol - f0)
            bank = bank_fixed or (bkA0 if nb % 2 == 0 else bkA1)
            w0 = bank.wr_waits() + [t_sl] + deps
            last = None
            for kd in range(16):
                last = Sc.op("tensor", lambda e, kd=kd, f0=f0, cw=cw, slab=slab, b=bank: e.matmul(
                    b.t[:, 0:cw], sb_wsl[:, slab, kd, :],
                    sb_xnT[:, kd, c0 + f0:c0 + f0 + cw],
                    start=(kd == 0), stop=(kd == 15)),
                    waits=w0 if kd == 0 else [], sig=(kd == 15))
            bank.set_writer(last)
            slab_cons[slab].append(last)
            dst = out_f32 if out_bf is None else out_bf
            tev = Sc.op("scalar", lambda e, f0=f0, cw=cw, dst=dst, b=bank: e.activation(
                dst[:, f0:f0 + cw], b.t[:, 0:cw], AF.Identity, bias=bias_col),
                waits=bank.rd_waits())
            bank.add_reader(tev)
            ev_toks.append(tev)
        return ev_toks

    def emit_rms_rope(ncol, toff, gain_col, out_ap, deps):
        nonlocal dump_guard
        t_sq = Sc.op("scalar", lambda e: e.activation(
            sb_dump[:, 0:ncol], sb_qf[:, 0:ncol], AF.Square),
            waits=deps + dump_guard)
        dump_guard = []
        rs_toks = []
        for nb in range((ncol + 511) // 512):
            f0 = nb * 512; cw = min(512, ncol - f0)
            tm = Sc.op("tensor", lambda e, f0=f0, cw=cw: e.matmul(
                bkMS.t[0:1, 0:cw], sb_onesc[:], sb_dump[:, f0:f0 + cw],
                start=True, stop=True), waits=bkMS.wr_waits() + [t_sq])
            bkMS.set_writer(tm)
            dump_guard.append(tm)
            te = Sc.op("vector", lambda e, f0=f0, cw=cw: e.tensor_scalar(
                sb_row[0:1, f0:f0 + cw], bkMS.t[0:1, 0:cw], 1.0 / HD, None, ALU.mult),
                waits=bkMS.rd_waits())
            bkMS.add_reader(te)
            rs_toks.append(te)
        t_rc2 = Sc.op("scalar", lambda e: e.activation(
            sb_row[0:1, 0:ncol], sb_row[0:1, 0:ncol], AF.Sqrt, bias=sb_cst[0:1, 1:2]),
            waits=rs_toks)
        bc_toks = []
        for nb in range((ncol + 511) // 512):
            f0 = nb * 512; cw = min(512, ncol - f0)
            tm = Sc.op("tensor", lambda e, f0=f0, cw=cw: e.matmul(
                bkTF.t[:, 0:cw], sb_onesr[:], sb_row[0:1, f0:f0 + cw],
                start=True, stop=True), waits=bkTF.wr_waits() + [t_rc2])
            bkTF.set_writer(tm)
            te = Sc.op("vector", lambda e, f0=f0, cw=cw: e.reciprocal(
                sb_rsqb[:, f0:f0 + cw], bkTF.t[:, 0:cw]), waits=bkTF.rd_waits())
            bkTF.add_reader(te)
            bc_toks.append(te)
        t_t = Sc.op("vector", lambda e: e.scalar_tensor_tensor(
            sb_qs[:, 0:ncol], sb_qf[:, 0:ncol], gain_col, sb_rsqb[:, 0:ncol],
            ALU.mult, ALU.mult), waits=bc_toks + deps)
        sw_toks = []
        for nb in range((ncol + 511) // 512):
            f0 = nb * 512; cw = min(512, ncol - f0)
            tm = Sc.op("tensor", lambda e, f0=f0, cw=cw: e.matmul(
                bkMS.t[0:64, 0:cw], sb_SW[:], sb_qs[0:64, f0:f0 + cw],
                start=True, stop=True), waits=bkMS.wr_waits() + [t_t])
            bkMS.set_writer(tm)
            te = Sc.op("vector", lambda e, f0=f0, cw=cw: e.tensor_copy(
                sb_usw[:, f0:f0 + cw], bkMS.t[0:64, 0:cw]), waits=bkMS.rd_waits())
            bkMS.add_reader(te)
            sw_toks.append(te)
        t_u = Sc.op("vector", lambda e: e.tensor_tensor(
            sb_u[:, 0:ncol], sb_qs[0:64, 0:ncol], sb_rtcc[:, toff:toff + ncol],
            ALU.mult), waits=[t_t])
        t_v = Sc.op("vector", lambda e: e.tensor_tensor(
            sb_v[:, 0:ncol], sb_usw[:, 0:ncol], sb_rtns[:, toff:toff + ncol],
            ALU.mult), waits=sw_toks)
        t_o1 = Sc.op("vector", lambda e: e.tensor_tensor(
            out_ap[0:64, 0:ncol], sb_u[:, 0:ncol], sb_v[:, 0:ncol], ALU.add),
            waits=[t_u, t_v])
        t_rest = Sc.op("vector", lambda e: e.tensor_copy(
            out_ap[64:128, 0:ncol], sb_qs[64:128, 0:ncol]), waits=[t_t])
        return [t_o1, t_rest]

    prev_qr_cons = [[], []]
    prev_lkr_cons = [[], []]
    prev_lv_cons = [[], []]
    prev_lvT_cons = [[], []]
    prev_snk_cons = [[], []]
    ropeg = {"qs": [], "u": [], "row": []}
    front_state = [None, None]

    def front_gen(h):
        hs = h % 2
        deps0 = (BAR1 if h == 0 else []) + prev_qr_cons[hs]
        prev_qr_cons[hs] = []
        if h > 0:
            deps0 = deps0 + front_state[1 - hs]["qf_free"]
        q_ev = emit_proj(wbf_q, qcast_a["q"], h, sb_bqT[:, h:h + 1], 128, TOK,
                         sb_qf, deps0, bank_fixed=bkA0)
        yield
        qr_toks = emit_rms_rope(TOK, 128, sb_cols[:, 3:4], sb_qr2[:, hs, :], q_ev)
        yield
        lk_ev = emit_proj(wbf_lk, qcast_a["lk"], h, sb_blkT[:, h:h + 1], 0, TEXT,
                          sb_qf, qr_toks + prev_lkr_cons[hs], bank_fixed=bkA0)
        prev_lkr_cons[hs] = []
        yield
        lkr_toks = emit_rms_rope(TEXT, 0, sb_cols[:, 4:5], sb_lkr2[:, hs, :], lk_ev)
        yield
        lv_ev = emit_proj(wbf_lv, qcast_a["lv"], h, sb_blvT[:, h:h + 1], 0, TEXT,
                          None, lkr_toks + prev_lv_cons[hs], out_bf=sb_lv,
                          bank_fixed=bkA0)
        prev_lv_cons[hs] = []
        yield
        lvT_toks = []
        for j in range(9):
            tm = Sc.op("tensor", lambda e, j=j: e.transpose(
                bkTB.t[:, 0:128], sb_lv[:, j * 128:(j + 1) * 128], sb_idbf[:]),
                waits=bkTB.wr_waits() + lv_ev)
            bkTB.set_writer(tm)
            te = Sc.op("vector", lambda e, j=j, hs=hs: e.tensor_copy(
                sb_lvT2[:, hs, j, :], bkTB.t[:, 0:128]),
                waits=bkTB.rd_waits() + (prev_lvT_cons[hs] if j == 0 else []))
            bkTB.add_reader(te)
            lvT_toks.append(te)
            prev_lv_cons[hs].append(tm)
        prev_lvT_cons[hs] = []
        yield
        t_sr0 = Sc.op("tensor", lambda e, h=h, hs=hs: e.matmul(
            bkMS.t[0:1, 0:512], sb_sinkT[:, h:h + 1], sb_qr2[:, hs, 0:512],
            start=True, stop=True), waits=bkMS.wr_waits() + qr_toks)
        bkMS.set_writer(t_sr0)
        t_se0 = Sc.op("vector", lambda e: e.tensor_scalar(
            sb_row[0:1, 0:512], bkMS.t[0:1, 0:512], float(SCALE), None, ALU.mult),
            waits=bkMS.rd_waits() + ropeg["row"])
        bkMS.add_reader(t_se0)
        t_sr1 = Sc.op("tensor", lambda e, h=h, hs=hs: e.matmul(
            bkMS.t[0:1, 0:512], sb_sinkT[:, h:h + 1], sb_qr2[:, hs, 512:1024],
            start=True, stop=True), waits=bkMS.wr_waits())
        bkMS.set_writer(t_sr1)
        t_se1 = Sc.op("vector", lambda e: e.tensor_scalar(
            sb_row[0:1, 512:1024], bkMS.t[0:1, 0:512], float(SCALE), None, ALU.mult),
            waits=bkMS.rd_waits())
        bkMS.add_reader(t_se1)
        t_sbf = Sc.op("vector", lambda e: e.tensor_copy(
            sb_u[0:1, 0:TOK], sb_row[0:1, 0:TOK]),
            waits=[t_se0, t_se1] + ropeg["u"] + lkr_toks)
        snk_evs = []
        tmx = None
        for cch in range(NCH):
            tmx = Sc.op("tensor", lambda e, cch=cch: e.transpose(
                bkTB.t[:, 896:897], sb_u[0:1, cch * 128:(cch + 1) * 128],
                sb_idbf[0:1, 0:1]),
                waits=bkTB.wr_waits() + [t_sbf] + snk_evs[-1:])
            bkTB.set_writer(tmx)
            tev = Sc.op("vector", lambda e, cch=cch, hs=hs: e.tensor_copy(
                sb_snk[:, hs, cch:cch + 1], bkTB.t[:, 896:897]),
                waits=bkTB.rd_waits() + (prev_snk_cons[hs] if cch == 0 else []))
            bkTB.add_reader(tev)
            snk_evs.append(tev)
        prev_snk_cons[hs] = []
        t_nsk = Sc.op("vector", lambda e, hs=hs: e.tensor_scalar(
            sb_snk[:, hs, 8:16], sb_snk[:, hs, 0:8], -1.0, None, ALU.mult),
            waits=snk_evs)
        ropeg["row"] = [t_sbf]
        ropeg["u"] = [tmx]
        front_state[hs] = {"qr": qr_toks, "lkr": lkr_toks, "lvT": lvT_toks,
                           "nsk": [t_nsk], "qf_free": lkr_toks}
        yield

    def run_gen(g):
        if g is not None:
            try:
                next(g)
                return g
            except StopIteration:
                return None
        return None

    merged_toks = merged_toks if False else merged_toks
    chain_g = [[], []]
    esg = [[], []]
    att_g = [[], []]
    askg = [[], []]
    mtg = [[], []]
    g0 = front_gen(0)
    for _ in range(10):
        g0 = run_gen(g0)
        if g0 is None:
            break

    for h in range(NH):
        hs = h % 2
        fs = front_state[hs]
        qr_toks = fs["qr"]; lkr_toks = fs["lkr"]; lvT_toks = fs["lvT"]
        nsk_toks = fs["nsk"]
        gen_next = front_gen(h + 1) if h + 1 < NH else None

        def emit_scores(cch):
            q0 = cch * 128
            par = cch % 2
            P = bkSC if par == 0 else bkA1
            mm1 = Sc.op("tensor", lambda e, q0=q0, b=P, hs=hs: e.matmul(
                b.t[:, 0:256], sb_qr2[:, hs, q0:q0 + 128], sb_ckr[:],
                start=True, stop=False),
                waits=P.wr_waits() + qr_toks + ckcv_toks + lkr_toks, sig=False)
            mm2 = Sc.op("tensor", lambda e, q0=q0, b=P, hs=hs: e.matmul(
                b.t[:, 256:512], sb_qr2[:, hs, q0:q0 + 128],
                sb_lkr2[:, hs, q0:q0 + 256], start=False, stop=True), waits=[])
            P.set_writer(mm2)
            prev_qr_cons[hs].append(mm2)
            prev_lkr_cons[hs].append(mm2)
            return P

        def emit_chain(cch, P):
            q0 = cch * 128
            par = cch % 2
            X = bkB0 if par == 0 else bkB1
            stc = sb_stat[:, 14 + par * 6: 20 + par * 6]
            midx = 0 if cch == 0 else 1
            t_ev0 = Sc.op("vector", lambda e, cch=cch, par=par, b=P: e.scalar_tensor_tensor(
                sb_sc[:, par, 0:256], b.t[:, 0:256], float(SCALE), sb_cmask[:, cch, :],
                ALU.mult, ALU.add), waits=P.rd_waits() + chain_g[par])
            t_ev = Sc.op("vector", lambda e, midx=midx, par=par, b=P: e.scalar_tensor_tensor(
                sb_sc[:, par, 256:512], b.t[:, 256:512], float(SCALE),
                sb_lmask[:, midx, :], ALU.mult, ALU.add), waits=[t_ev0])
            P.add_reader(t_ev)
            t_mx = Sc.op("vector", lambda e, par=par, stc=stc: e.reduce_max(
                stc[:, 0:2], sb_sc[:, par, :].rearrange("p (g w) -> p g w", w=256),
                AX.X, negate=True), waits=[t_ev])
            t_mf = Sc.op("vector", lambda e, cch=cch, stc=stc, hs=hs: e.tensor_scalar(
                stc[:, 0:1], stc[:, 0:1], sb_snk[:, hs, 8 + cch:9 + cch], None,
                ALU.min), waits=[t_mx] + nsk_toks)
            t_e1 = Sc.op("scalar", lambda e, par=par, stc=stc: e.activation(
                sb_esc[:, par, 0:256], sb_sc[:, par, 0:256], AF.Exp,
                bias=stc[:, 0:1], accum_out=stc[:, 2:3]), waits=[t_mf])
            t_e2 = Sc.op("scalar", lambda e, par=par, stc=stc: e.activation(
                sb_esc[:, par, 256:512], sb_sc[:, par, 256:512], AF.Exp,
                bias=stc[:, 1:2], accum_out=stc[:, 3:4]), waits=[t_mx, t_e1])
            t_es = Sc.op("scalar", lambda e, cch=cch, par=par, stc=stc, hs=hs: e.activation(
                sb_snk[:, hs, 16 + par:17 + par], sb_snk[:, hs, cch:cch + 1], AF.Exp,
                bias=stc[:, 0:1]), waits=[t_mf] + esg[par])
            t_sa = Sc.op("vector", lambda e, par=par, stc=stc, hs=hs: e.tensor_tensor(
                stc[:, 2:3], stc[:, 2:3], sb_snk[:, hs, 16 + par:17 + par], ALU.add),
                waits=[t_e1, t_es])
            t_rc = Sc.op("vector", lambda e, stc=stc: e.reciprocal(
                stc[:, 4:6], stc[:, 2:4]), waits=[t_sa, t_e2])
            t_a1 = Sc.op("scalar", lambda e, par=par, stc=stc: e.activation(
                sb_attbf[:, par, 0:256], sb_esc[:, par, 0:256], AF.Identity,
                scale=stc[:, 4:5]), waits=[t_rc] + att_g[par])
            t_a2 = Sc.op("scalar", lambda e, par=par, stc=stc: e.activation(
                sb_attbf[:, par, 256:512], sb_esc[:, par, 256:512], AF.Identity,
                scale=stc[:, 5:6]), waits=[t_rc, t_a1])
            t_ak = Sc.op("vector", lambda e, par=par, stc=stc, hs=hs: e.scalar_tensor_tensor(
                sb_ask[:, par:par + 1], sb_snk[:, hs, 16 + par:17 + par], 1.0,
                stc[:, 4:5], ALU.mult, ALU.mult), waits=[t_rc] + askg[par])
            prev_snk_cons[hs].extend([t_mf, t_es, t_sa, t_ak])
            esg[par] = [t_sa]
            tt = None
            for r in range(4):
                tt = Sc.op("tensor", lambda e, par=par, r=r: e.transpose(
                    bkTB.t[:, r * 128:(r + 1) * 128],
                    sb_attbf[:, par, r * 128:(r + 1) * 128], sb_idbf[:]),
                    waits=(bkTB.wr_waits() if r == 0 else []) +
                    ([t_a1] if r < 2 else [t_a2]), sig=(r == 3))
            tt5 = Sc.op("tensor", lambda e, par=par: e.transpose(
                bkTB.t[0:1, 512:640], sb_ask[:, par:par + 1], sb_idbf[:]),
                waits=[t_ak])
            bkTB.set_writer(tt5)
            t_aT = Sc.op("vector", lambda e, par=par: e.tensor_copy(
                sb_attT[:, par, :], bkTB.t[:, 0:512]), waits=bkTB.rd_waits())
            t_aS = Sc.op("vector", lambda e, par=par: e.tensor_copy(
                sb_attS[:, par, :], bkTB.t[0:1, 512:640]), waits=bkTB.rd_waits())
            bkTB.add_reader(t_aT)
            bkTB.add_reader(t_aS)
            att_g[par] = [tt, tt5]
            askg[par] = [tt5]
            c1 = Sc.op("tensor", lambda e, par=par, b=X: e.matmul(
                b.t[:, 0:128], sb_cvT[:, 0, :], sb_attT[:, par, 0:128],
                start=True, stop=False), waits=X.wr_waits() + [t_aT], sig=False)
            c2 = Sc.op("tensor", lambda e, par=par, b=X: e.matmul(
                b.t[:, 0:128], sb_cvT[:, 1, :], sb_attT[:, par, 128:256],
                start=False, stop=False), waits=[], sig=False)
            c3 = Sc.op("tensor", lambda e, par=par, h=h, b=X: e.matmul(
                b.t[:, 0:128], sb_sinkvR[:, h * 128:(h + 1) * 128],
                sb_attS[:, par, :], start=False, stop=False), waits=[t_aS], sig=False)
            l1 = Sc.op("tensor", lambda e, par=par, cch=cch, b=X, hs=hs: e.matmul(
                b.t[:, 128:256], sb_lvT2[:, hs, cch, :], sb_attT[:, par, 256:384],
                start=False, stop=False), waits=lvT_toks, sig=False)
            l2 = Sc.op("tensor", lambda e, par=par, cch=cch, b=X, hs=hs: e.matmul(
                b.t[:, 128:256], sb_lvT2[:, hs, cch + 1, :], sb_attT[:, par, 384:512],
                start=False, stop=True), waits=[])
            X.set_writer(l2)
            prev_lvT_cons[hs].append(l2)
            t_m1 = Sc.op("scalar", lambda e, par=par, b=X: e.activation(
                sb_mtmp[:, par, :], b.t[:, 0:128], AF.Identity,
                scale=sb_cst[:, 3:4]), waits=X.rd_waits() + mtg[par])
            X.add_reader(t_m1)
            t_m2 = Sc.op("vector", lambda e, h=h, q0=q0, par=par, b=X: e.scalar_tensor_tensor(
                sb_merged[:, h, q0:q0 + 128], b.t[:, 128:256], 0.5,
                sb_mtmp[:, par, :], ALU.mult, ALU.add),
                waits=X.rd_waits() + [t_m1])
            X.add_reader(t_m2)
            mtg[par] = [t_m2]
            chain_g[par] = [t_e2, t_aT]
            merged_toks.append(t_m2)

        pend = []
        for cch in range(NCH + 1):
            if cch < NCH:
                pend.append((cch, emit_scores(cch)))
            if len(pend) > (1 if cch < NCH else 0):
                c0_, P0_ = pend.pop(0)
                emit_chain(c0_, P0_)
            gen_next = run_gen(gen_next)
            pump_quant(2)
        while gen_next is not None:
            gen_next = run_gen(gen_next)

    drain_quant()

    BAR2 = Sc.barrier()

    ao_dmas = []
    evf_guard = []
    for m in range(16):
        slab = next_slab()
        t_sl = Sc.dma("aslab", lambda e, m=m, slab=slab: e.dma_start(
            out=sb_wsl[:, slab, :, :],
            in_=wbf_o[:, m * 128:(m + 1) * 128].rearrange("(k p) n -> p k n", p=128)),
            waits=slab_cons[slab] + [qcast_a["o"]] + (BAR2 if m < 4 else []))
        slab_cons[slab] = []
        for th in range(2):
            bank = bkA0 if th == 0 else bkA1
            w0 = bank.wr_waits() + [t_sl] + merged_toks[-1:]
            last = None
            for kd in range(16):
                last = Sc.op("tensor", lambda e, kd=kd, th=th, slab=slab, b=bank: e.matmul(
                    b.t[:, 0:512], sb_wsl[:, slab, kd, :],
                    sb_merged[:, kd, th * 512:(th + 1) * 512],
                    start=(kd == 0), stop=(kd == 15)),
                    waits=w0 if kd == 0 else [], sig=(kd == 15))
            bank.set_writer(last)
            slab_cons[slab].append(last)
            t_ev = Sc.op("scalar", lambda e, m=m, b=bank: e.activation(
                sb_evf[:], b.t[:, 0:512], AF.Identity, bias=sb_boT[:, m:m + 1]),
                waits=bank.rd_waits() + evf_guard)
            bank.add_reader(t_ev)
            t_evb = Sc.op("vector", lambda e: e.tensor_copy(
                sb_dump[:, 0:512], sb_evf[:]), waits=[t_ev] + dump_guard)
            evf_guard = [t_evb]
            dump_guard = []
            for p4 in range(4):
                cch = th * 4 + p4
                tm = Sc.op("tensor", lambda e, p4=p4: e.transpose(
                    bkTB.t[:, 0:128], sb_dump[:, p4 * 128:(p4 + 1) * 128], sb_idbf[:]),
                    waits=bkTB.wr_waits() + [t_evb])
                bkTB.set_writer(tm)
                dump_guard.append(tm)
                te = Sc.op("vector", lambda e: e.tensor_copy(
                    sb_mtmp[:, 0, :], bkTB.t[:, 0:128]), waits=bkTB.rd_waits())
                bkTB.add_reader(te)
                td = Sc.dma("aoscw", lambda e, cch=cch, m=m: e.dma_start(
                    out=aosc[cch, :, m * 128:(m + 1) * 128], in_=sb_mtmp[:, 0, :]),
                    waits=[te])
                bkTB.add_reader(td)
                ao_dmas.append(td)

    x2_cons = [[], []]
    xn2T_last = []
    x2sp_toks = []
    fo_guard = []
    for cch in range(NCH):
        sl = cch % 2
        t_ldao = Sc.dma("x", lambda e, cch=cch: e.dma_start(
            out=sb_fo[:], in_=aosc[cch, :, :]), waits=ao_dmas[-2:] + fo_guard)
        t_x = Sc.dma("x", lambda e, cch=cch, sl=sl: e.dma_start(
            out=sb_xc[:, sl, :], in_=xo[(cch + 1) * 128:(cch + 2) * 128, :]),
            waits=xc_cons[sl])
        xc_cons[sl] = []
        _, st, t7 = emit_ln(sb_fo[:], None, [t_ldao])
        t_nf = Sc.op("vector", lambda e, st=st: e.tensor_scalar(
            sb_fo[:], sb_fo[:], st[:, 7:8], st[:, 8:9], ALU.mult, ALU.add),
            waits=[t7])
        t_x2 = Sc.op("vector", lambda e, sl=sl: e.tensor_tensor(
            sb_x2[:, sl, :], sb_fo[:], sb_xc[:, sl, :], ALU.add),
            waits=[t_nf, t_x] + x2_cons[sl])
        fo_guard = [t_x2]
        x2_cons[sl] = []
        xc_cons[sl].append(t_x2)
        t_sp = Sc.dma("x2spw", lambda e, cch=cch, sl=sl: e.dma_start(
            out=x2sp[cch, :, :], in_=sb_x2[:, sl, :]), waits=[t_x2])
        x2_cons[sl].append(t_sp)
        x2sp_toks.append(t_sp)
        t_xn2, _, _ = emit_ln(sb_x2[:, sl, :], sb_xnbf[:, sl, :],
                              [t_x2] + xn_cons[sl])
        xn_cons[sl] = []
        x2_cons[sl].append(t_xn2)
        for kd in range(16):
            tm = Sc.op("tensor", lambda e, kd=kd, sl=sl: e.transpose(
                bkTB.t[:, 0:128], sb_xnbf[:, sl, kd * 128:(kd + 1) * 128], sb_idbf[:]),
                waits=bkTB.wr_waits() + [t_xn2])
            bkTB.set_writer(tm)
            te = Sc.op("vector", lambda e, kd=kd, cch=cch: e.tensor_copy(
                sb_xn2T[:, kd, cch * 128:(cch + 1) * 128], bkTB.t[:, 0:128]),
                waits=bkTB.rd_waits())
            bkTB.add_reader(te)
            if kd == 15:
                xn_cons[sl].append(tm)
                xn2T_last = [te]
    BAR3 = Sc.barrier(streams=("x2spw", "aoscw"))

    ffn_dmas = []
    gsl_cons = {i: [] for i in range(4)}
    evf_guard2 = []
    for hf in range(2):
        hoff = hf * 512
        h_last = []
        for jj in range(64):
            sl = jj % 2
            jg, ju = jj, jj + 64
            t_sg = Sc.dma("gslab", lambda e, jg=jg, sl=sl: e.dma_start(
                out=sb_wsl[:, sl * 2, :, :],
                in_=whatT_gu[:, :, jg * 128:(jg + 1) * 128].rearrange("k p f -> p k f")),
                waits=gsl_cons[sl * 2] + [wtok[("gu", jg, kk)] for kk in range(16)] +
                (BAR3 if jj < 2 and hf == 0 else []))
            t_su = Sc.dma("gslab", lambda e, ju=ju, sl=sl: e.dma_start(
                out=sb_wsl[:, sl * 2 + 1, :, :],
                in_=whatT_gu[:, :, ju * 128:(ju + 1) * 128].rearrange("k p f -> p k f")),
                waits=gsl_cons[sl * 2 + 1] + [wtok[("gu", ju, kk)] for kk in range(16)])
            gsl_cons[sl * 2] = []
            gsl_cons[sl * 2 + 1] = []
            bg = bkA0 if sl == 0 else bkA1
            bu = bkB0 if sl == 0 else bkB1
            w0 = bg.wr_waits() + [t_sg] + (xn2T_last if jj == 0 else [])
            lastg = None
            for kd in range(16):
                lastg = Sc.op("tensor", lambda e, kd=kd, sl=sl, hoff=hoff, b=bg: e.matmul(
                    b.t[:, 0:512], sb_wsl[:, sl * 2, kd, :],
                    sb_xn2T[:, kd, hoff:hoff + 512],
                    start=(kd == 0), stop=(kd == 15)),
                    waits=w0 if kd == 0 else [], sig=(kd == 15))
            bg.set_writer(lastg)
            gsl_cons[sl * 2].append(lastg)
            w0 = bu.wr_waits() + [t_su]
            lastu = None
            for kd in range(16):
                lastu = Sc.op("tensor", lambda e, kd=kd, sl=sl, hoff=hoff, b=bu: e.matmul(
                    b.t[:, 0:512], sb_wsl[:, sl * 2 + 1, kd, :],
                    sb_xn2T[:, kd, hoff:hoff + 512],
                    start=(kd == 0), stop=(kd == 15)),
                    waits=w0 if kd == 0 else [], sig=(kd == 15))
            bu.set_writer(lastu)
            gsl_cons[sl * 2 + 1].append(lastu)
            t_si = Sc.op("scalar", lambda e, b=bg: e.activation(
                sb_evf[:], b.t[:, 0:512], AF.Silu), waits=bg.rd_waits() + evf_guard2)
            bg.add_reader(t_si)
            t_h = Sc.op("vector", lambda e, jj=jj, b=bu: e.tensor_tensor(
                h_tile(jj), b.t[:, 0:512], sb_evf[:], ALU.mult),
                waits=bu.rd_waits() + [t_si] + (BAR3 if jj < 2 and hf == 0 else []))
            bu.add_reader(t_h)
            evf_guard2 = [t_h]
            h_last = [t_h]
        for m2 in range(16):
            sl = m2 % 2
            osl = 1 - sl
            pq = [wtok[("p", m2 // 8, hb)] for hb in range(64)] if hf == 0 else []
            t_s1 = Sc.dma("pslab", lambda e, m2=m2, sl=sl: e.dma_start(
                out=sb_wsl[:, sl * 2:sl * 2 + 2, :, :].rearrange("p a k n -> p (a k) n"),
                in_=whatT_p[m2, 0:32].rearrange("k p f -> p k f")),
                waits=gsl_cons[sl * 2] + gsl_cons[sl * 2 + 1] + pq)
            t_s2 = Sc.dma("pslab", lambda e, m2=m2, osl=osl: e.dma_start(
                out=sb_wsl[:, osl * 2:osl * 2 + 2, :, :].rearrange("p a k n -> p (a k) n"),
                in_=whatT_p[m2, 32:64].rearrange("k p f -> p k f")),
                waits=gsl_cons[osl * 2] + gsl_cons[osl * 2 + 1])
            for i4 in range(4):
                gsl_cons[i4] = []
            bank = bkA0 if sl == 0 else bkA1
            w0 = bank.wr_waits() + [t_s1, t_s2] + h_last
            last = None
            for kd in range(64):
                wsrc = (sb_wsl[:, sl * 2 + kd // 16, kd % 16, :] if kd < 32 else
                        sb_wsl[:, osl * 2 + (kd - 32) // 16, (kd - 32) % 16, :])
                last = Sc.op("tensor", lambda e, kd=kd, wsrc=wsrc, b=bank: e.matmul(
                    b.t[:, 0:512], wsrc, h_tile(kd), start=(kd == 0), stop=(kd == 63)),
                    waits=w0 if kd == 0 else [], sig=(kd == 63))
            bank.set_writer(last)
            for i4 in range(4):
                gsl_cons[i4].append(last)
            t_ev = Sc.op("scalar", lambda e, b=bank: e.activation(
                sb_evf[:], b.t[:, 0:512], AF.Identity),
                waits=bank.rd_waits() + evf_guard2)
            bank.add_reader(t_ev)
            t_evb = Sc.op("vector", lambda e: e.tensor_copy(
                sb_dump[:, 0:512], sb_evf[:]), waits=[t_ev] + dump_guard)
            evf_guard2 = [t_evb]
            dump_guard = []
            for p4 in range(4):
                cch = hf * 4 + p4
                tm = Sc.op("tensor", lambda e, p4=p4: e.transpose(
                    bkTB.t[:, 0:128], sb_dump[:, p4 * 128:(p4 + 1) * 128], sb_idbf[:]),
                    waits=bkTB.wr_waits() + [t_evb])
                bkTB.set_writer(tm)
                dump_guard.append(tm)
                te = Sc.op("vector", lambda e: e.tensor_copy(
                    sb_mtmp[:, 0, :], bkTB.t[:, 0:128]), waits=bkTB.rd_waits())
                bkTB.add_reader(te)
                td = Sc.dma("ffnw", lambda e, cch=cch, m2=m2: e.dma_start(
                    out=ffnsc[cch, :, m2 * 128:(m2 + 1) * 128], in_=sb_mtmp[:, 0, :]),
                    waits=[te])
                bkTB.add_reader(td)
                ffn_dmas.append(td)

    BAR4 = Sc.barrier(streams=("whatw",))

    out_dmas = []
    fo_cons = []
    for cch in range(NCH):
        sl = cch % 2
        t_f = Sc.dma("x", lambda e, cch=cch: e.dma_start(
            out=sb_fo2[:], in_=ffnsc[cch, :, :]),
            waits=ffn_dmas[-2:] + fo_cons + (BAR4 if cch < 2 else []))
        t_x2l = Sc.dma("x", lambda e, cch=cch, sl=sl: e.dma_start(
            out=sb_x2[:, sl, :], in_=x2sp[cch, :, :]),
            waits=x2_cons[sl] + (BAR4 if cch < 2 else []) + x2sp_toks[-1:])
        x2_cons[sl] = []
        _, st, t7 = emit_ln(sb_fo2[:], None, [t_f])
        t_nf = Sc.op("vector", lambda e, st=st: e.tensor_scalar(
            sb_fo[:], sb_fo2[:], st[:, 7:8], st[:, 8:9], ALU.mult, ALU.add),
            waits=[t7] + (BAR4 if cch == 0 else []))
        fo_cons = [t_nf]
        t_o = Sc.op("vector", lambda e, sl=sl: e.tensor_tensor(
            sb_outst[:, sl, :], sb_fo[:], sb_x2[:, sl, :], ALU.add),
            waits=[t_nf, t_x2l] + xc_cons[sl])
        xc_cons[sl] = []
        x2_cons[sl].append(t_o)
        td = Sc.dma("outw", lambda e, cch=cch, sl=sl: e.dma_start(
            out=out[cch * 128:(cch + 1) * 128, :], in_=sb_outst[:, sl, :]),
            waits=[t_o])
        xc_cons[sl].append(td)
        out_dmas.append(td)

    Sc.op("sync", lambda e: e.nop(), waits=[out_dmas[-1]], sig=False)
    Sc.build()
    n_inst = sum(len(v) for v in Sc.items.values())
    print(f"program instructions (approx): {n_inst}")
    return nc


def _host_inputs(inputs, core):
    b, qv = core // 4, core % 4
    start = qv * TOK
    x = np.asarray(inputs["L0_attn_hidden_states"], dtype=np.float32)[b]
    xb_ = np.ascontiguousarray(x)
    halo = np.zeros((128, D), np.float32) if qv == 0 else x[start - 128:start]
    xo_ = np.ascontiguousarray(np.concatenate([halo, x[start:start + TOK]], axis=0))
    pos = np.maximum(np.arange(start - 128, start + TOK), 0).astype(np.float32)
    inv_freq = 1.0 / 10000.0 ** (np.arange(32, dtype=np.float32) * 2.0 / 64.0)
    ang = pos[None, :] * inv_freq[:, None]
    cos, sin = np.cos(ang), np.sin(ang)
    rtcc_ = np.ascontiguousarray(np.concatenate([cos, cos], 0).astype(np.float32))
    rtns_ = np.ascontiguousarray(np.concatenate([-sin, sin], 0).astype(np.float32))
    bend = (np.arange(C) * R + (R - 1)).astype(np.float32)
    angc = bend[None, :] * inv_freq[:, None]
    cc_, sc_ = np.cos(angc), np.sin(angc)
    cktcc_ = np.ascontiguousarray(np.concatenate([cc_, cc_], 0).astype(np.float32))
    cktns_ = np.ascontiguousarray(np.concatenate([-sc_, sc_], 0).astype(np.float32))
    cmask = np.zeros((NCH, 128, C), np.float32)
    for cch in range(NCH):
        posr = start + cch * 128 + np.arange(128)
        cmask[cch] = np.where(bend[None, :] <= posr[:, None], 0.0, -1e9)
    lmask = np.zeros((2, 128, 256), np.float32)
    i_ = np.arange(128)[:, None]
    j_ = np.arange(256)[None, :]
    band = (j_ >= i_) & (j_ <= i_ + 128)
    lmask[1] = np.where(band, 0.0, -1e9)
    lmask[0] = np.where(band & (j_ >= 128), 0.0, -1e9) if qv == 0 else lmask[1]
    g = lambda k: np.ascontiguousarray(np.asarray(inputs[k], dtype=np.float32))
    return {
        "xb": xb_, "xo": xo_, "rtcc": rtcc_, "rtns": rtns_,
        "cktcc": cktcc_, "cktns": cktns_, "cm": cmask.astype(bfdt),
        "lm": lmask.astype(bfdt),
        "Wq": g("L0_attn_Wq"), "Wk": g("L0_attn_Wk"), "Wv": g("L0_attn_Wv"),
        "Wlk": g("L0_attn_Wlk"), "Wlv": g("L0_attn_Wlv"), "Wo": g("L0_attn_Wo"),
        "Wgu": g("L0_ffn_w_gate_up"), "Wp": g("L0_ffn_w_proj"),
        "bqT": np.ascontiguousarray(g("L0_attn_bq").reshape(16, 128).T),
        "blkT": np.ascontiguousarray(g("L0_attn_blk").reshape(16, 128).T),
        "blvT": np.ascontiguousarray(g("L0_attn_blv").reshape(16, 128).T),
        "boT": np.ascontiguousarray(g("L0_attn_bo").reshape(16, 128).T),
        "bkc": g("L0_attn_bk").reshape(128, 1),
        "bvc": g("L0_attn_bv").reshape(128, 1),
        "bcc": np.ascontiguousarray(np.broadcast_to(
            g("L0_attn_bc").reshape(1, 1), (128, 1))),
        "qnc": g("L0_attn_qn_w").reshape(128, 1),
        "knc": g("L0_attn_kn_w").reshape(128, 1),
        "sinkTf": np.ascontiguousarray(g("L0_attn_sink_k").T),
        "sinkvRf": g("L0_attn_sink_v").reshape(1, NH * HD),
        "wcrepf": np.ascontiguousarray(np.broadcast_to(
            g("L0_attn_Wc").reshape(1, D), (128, D))),
    }


_CACHE = {}


def kernel(**inputs):
    if "nc" not in _CACHE:
        _CACHE["nc"] = build_program()
    nc = _CACHE["nc"]
    in_maps = [_host_inputs(inputs, c) for c in range(8)]
    res = run_bass_kernel_spmd(nc, in_maps, core_ids=list(range(8)))
    outs = [np.asarray(res.results[c]["out"], dtype=np.float32) for c in range(8)]
    return np.stack([np.concatenate(outs[0:4], 0), np.concatenate(outs[4:8], 0)], 0)


if __name__ == "__main__":
    build_program()
    print("BUILD OK")


# version 14
# speedup vs baseline: 1.0245x; 1.0010x over previous
import math

import numpy as np
import ml_dtypes
import concourse.bass as bass
import concourse.mybir as mybir
from concourse.bass_utils import run_bass_kernel_spmd

F32 = mybir.dt.float32
BF16 = mybir.dt.bfloat16
U16 = mybir.dt.uint16
AF = mybir.ActivationFunctionType
ALU = mybir.AluOpType
AX = mybir.AxisListType
bfdt = ml_dtypes.bfloat16

B, S, D = 2, 4096, 2048
NH, HD = 16, 128
R, C, WIN = 16, 256, 128
HID2, H = 16384, 8192
TOK, TEXT, NCH, PFX = 1024, 1152, 8, 32
EPS_LN, EPS_RMS = 1e-5, 1e-6
SCALE = 1.0 / math.sqrt(HD)
SB_BASE = 16512
SB_BUDGET = 229376


class Sched:
    ENGINES = ("sync", "gpsimd", "scalar", "vector", "tensor")
    STREAMS = {
        "setup_s": ("sync", 1), "setup_p": ("gpsimd", 1),
        "x": ("sync", 2), "entw": ("sync", 2), "stk": ("sync", 1),
        "whatw": ("gpsimd", 2),
        "aoscw": ("sync", 2), "ffnw": ("sync", 2), "x2spw": ("sync", 1),
        "outw": ("sync", 1),
        "aslab": ("sync", 2), "qcast_gu": ("gpsimd", 2), "qcast_p": ("gpsimd", 2),
        "qcast_a": ("gpsimd", 2),
        "wt": ("scalar", 3), "gslab": ("scalar", 2), "pslab": ("scalar", 2),
    }

    def __init__(self, nc):
        self.nc = nc
        self.items = {e: [] for e in self.ENGINES}
        self.csem = {e: nc.alloc_semaphore(f"cs_{e}") for e in self.ENGINES}
        self.ccount = {e: 0 for e in self.ENGINES}
        self.qsem = {}
        self.qcount = {}
        self.qrr = {}
        for s, (q, ns) in self.STREAMS.items():
            for i in range(ns):
                self.qsem[(s, i)] = nc.alloc_semaphore(f"dq_{s}{i}")
                self.qcount[(s, i)] = 0
            self.qrr[s] = 0

    def op(self, eng, fn, waits=(), sig=True):
        tok = None
        if sig:
            self.ccount[eng] += 1
            tok = (self.csem[eng], self.ccount[eng])
        wl = [w for w in waits if w is not None]
        self.items[eng].append(("op", fn, wl, tok))
        return tok

    def dma(self, stream, fn, waits=()):
        q, ns = self.STREAMS[stream]
        i = self.qrr[stream]
        self.qrr[stream] = (i + 1) % ns
        key = (stream, i)
        prev = (self.qsem[key], self.qcount[key]) if self.qcount[key] else None
        self.qcount[key] += 16
        tok = (self.qsem[key], self.qcount[key])
        wl = [w for w in list(waits) + [prev] if w is not None]
        self.items[q].append(("dma", fn, wl, tok))
        return tok

    def barrier(self, streams=()):
        toks = []
        for e in ("gpsimd", "scalar", "vector", "tensor"):
            pv = (self.csem[e], self.ccount[e])
            toks.append(self.op(e, lambda eng: eng.nop(), waits=[pv], sig=True))
        for s in streams:
            for i in range(self.STREAMS[s][1]):
                if self.qcount[(s, i)]:
                    toks.append((self.qsem[(s, i)], self.qcount[(s, i)]))
        return toks

    def build(self):
        nc = self.nc
        with nc.Block() as block:
            def make(eng_name):
                def body(eng):
                    seen = {}
                    for kind, fn, waits, tok in self.items[eng_name]:
                        for (sem, val) in waits:
                            key = id(sem)
                            if seen.get(key, 0) < val:
                                eng.wait_ge(sem, val)
                                seen[key] = val
                        inst = fn(eng)
                        if tok is not None:
                            inst.then_inc(tok[0], 16 if kind == "dma" else 1)
                return body
            block.sync(make("sync"))
            block.gpsimd(make("gpsimd"))
            block.scalar(make("scalar"))
            block.vector(make("vector"))
            block.tensor(make("tensor"))


class Bank:
    def __init__(self, tile):
        self.t = tile
        self.readers = []
        self.writer = None

    def wr_waits(self):
        w = list(self.readers) + ([self.writer] if self.writer else [])
        self.readers = []
        return w

    def set_writer(self, tok):
        assert tok is not None
        self.writer = tok

    def rd_waits(self):
        return [self.writer]

    def add_reader(self, tok):
        self.readers.append(tok)


def build_program():
    nc = bass.Bass("TRN2", target_bir_lowering=False)
    Sc = Sched(nc)

    def ext(name, shape, dtype=F32):
        return nc.dram_tensor(name, list(shape), dtype, kind="ExternalInput").ap()

    xb = ext("xb", [S, D])
    xo = ext("xo", [TEXT, D])
    rtcc = ext("rtcc", [64, TEXT]); rtns = ext("rtns", [64, TEXT])
    cktcc = ext("cktcc", [64, C]); cktns = ext("cktns", [64, C])
    cmv = ext("cm", [NCH, 128, C], BF16); lmv = ext("lm", [2, 128, 256], BF16)
    Wq = ext("Wq", [D, D]); Wk = ext("Wk", [D, HD]); Wv = ext("Wv", [D, HD])
    Wlk = ext("Wlk", [D, D]); Wlv = ext("Wlv", [D, D]); Wo = ext("Wo", [D, D])
    Wgu = ext("Wgu", [HID2, D]); Wp = ext("Wp", [D, H])
    bqT = ext("bqT", [128, 16]); blkT = ext("blkT", [128, 16])
    blvT = ext("blvT", [128, 16]); boT = ext("boT", [128, 16])
    bkc = ext("bkc", [128, 1]); bvc = ext("bvc", [128, 1]); bcc = ext("bcc", [128, 1])
    qnc = ext("qnc", [128, 1]); knc = ext("knc", [128, 1])
    sinkTf = ext("sinkTf", [128, 16]); sinkvRf = ext("sinkvRf", [1, NH * HD])
    wcrepf = ext("wcrepf", [128, D])

    out = nc.dram_tensor("out", [TOK, D], F32, kind="ExternalOutput").ap()

    wbf_gu = nc.dram_tensor("wbf_gu", [HID2, D], BF16, kind="Internal").ap()
    wbf_q = nc.dram_tensor("wbf_q", [D, D], BF16, kind="Internal").ap()
    wbf_lk = nc.dram_tensor("wbf_lk", [D, D], BF16, kind="Internal").ap()
    wbf_lv = nc.dram_tensor("wbf_lv", [D, D], BF16, kind="Internal").ap()
    wbf_o = nc.dram_tensor("wbf_o", [D, D], BF16, kind="Internal").ap()
    wbf_p = nc.dram_tensor("wbf_p", [D, H], BF16, kind="Internal").ap()
    whatT_gu = nc.dram_tensor("whatT_gu", [16, 128, HID2], BF16, kind="Internal").ap()
    whatT_p = nc.dram_tensor("whatT_p", [16, 64, 128, 128], BF16, kind="Internal").ap()
    x2sp = nc.dram_tensor("x2sp", [NCH, 128, D], F32, kind="Internal").ap()
    aosc = nc.dram_tensor("aosc", [NCH, 128, D], F32, kind="Internal").ap()
    ffnsc = nc.dram_tensor("ffnsc", [NCH, 128, D], F32, kind="Internal").ap()
    entsc_d = nc.dram_tensor("entsc_d", [PFX, 8, D], F32, kind="Internal").ap()

    idn_bf_c = nc.inline_tensor(np.eye(128, dtype=bfdt), name="idn_bf").ap()
    ones_col_c = nc.inline_tensor(np.ones((128, 1), dtype=bfdt), name="ones_col").ap()
    ones_row_c = nc.inline_tensor(np.ones((1, 128), dtype=np.float32), name="ones_row").ap()
    patt = np.zeros((128, 8), dtype=np.float32)
    for i in range(128):
        patt[i, i // 16] = 1.0
    patt8_c = nc.inline_tensor(patt, name="patt8").ap()
    patt8T_c = nc.inline_tensor(np.ascontiguousarray(patt.T), name="patt8T").ap()
    patt8b_c = nc.inline_tensor(patt.astype(bfdt), name="patt8b").ap()
    i2 = np.zeros((128, 2), dtype=bfdt); i2[:64, 0] = 1.0 / 128; i2[64:, 1] = 1.0 / 128
    I2_c = nc.inline_tensor(i2, name="I2").ap()
    e2 = np.zeros((2, 128), dtype=bfdt); e2[0, :64] = 1; e2[1, 64:] = 1
    E2_c = nc.inline_tensor(e2, name="E2").ap()
    sw = np.zeros((64, 64), dtype=bfdt)
    for i in range(64):
        sw[i, (i + 32) % 64] = 1
    SW64_c = nc.inline_tensor(sw, name="SW64").ap()
    cstv = np.zeros((128, 4), dtype=np.float32)
    cstv[:, 0] = EPS_LN; cstv[:, 1] = EPS_RMS; cstv[:, 2] = SCALE; cstv[:, 3] = 0.5
    cst_c = nc.inline_tensor(cstv, name="cstc").ap()

    cur = [SB_BASE]

    def sba(name, shape, dtype, at=None, track=True):
        if at is None:
            cur[0] = (cur[0] + 63) // 64 * 64
            at = cur[0]
            if track:
                cur[0] = at + int(np.prod(shape[1:])) * mybir.dt.size(dtype)
                assert cur[0] <= SB_BUDGET, f"SBUF overflow at {name}: {cur[0]}"
        return nc.alloc_sbuf_tensor_at(name, list(shape), dtype, offset=at)

    sb_idbf = sba("sb_idbf", [128, 128], BF16)
    sb_onesc = sba("sb_onesc", [128, 1], BF16)
    sb_onesr = sba("sb_onesr", [1, 128], F32)
    sb_patt8 = sba("sb_patt8", [128, 8], F32)
    sb_patt8T = sba("sb_patt8T", [8, 128], F32)
    sb_patt8b = sba("sb_patt8b", [128, 8], BF16)
    sb_I2 = sba("sb_I2", [128, 2], BF16)
    sb_E2 = sba("sb_E2", [2, 128], BF16)
    sb_SW = sba("sb_SW", [64, 64], BF16)
    sb_bqT = sba("sb_bqT", [128, 16], F32)
    sb_blkT = sba("sb_blkT", [128, 16], F32)
    sb_blvT = sba("sb_blvT", [128, 16], F32)
    sb_boT = sba("sb_boT", [128, 16], F32)
    sb_cols = sba("sb_cols", [128, 5], F32)
    sb_sinkT = sba("sb_sinkT", [128, 16], BF16)
    sb_sinkvR = sba("sb_sinkvR", [1, NH * HD], BF16)
    sb_rtcc = sba("sb_rtcc", [64, TEXT], BF16)
    sb_rtns = sba("sb_rtns", [64, TEXT], BF16)
    sb_ckcc = sba("sb_ckcc", [64, C], BF16)
    sb_ckns = sba("sb_ckns", [64, C], BF16)
    sb_ckr = sba("sb_ckr", [128, C], BF16)
    sb_cvT = sba("sb_cvT", [128, 2, 128], BF16)
    sb_cst = sba("sb_cst", [128, 4], F32)
    sb_cmask = sba("sb_cmask", [128, NCH, C], BF16)
    sb_lmask = sba("sb_lmask", [128, 2, 256], BF16)

    r1 = (cur[0] + 63) // 64 * 64
    cur[0] = r1
    sb_xnT = sba("sb_xnT", [128, 16, TEXT], BF16)
    r1b = (cur[0] + 63) // 64 * 64
    cur[0] = r1b
    sb_merged = sba("sb_merged", [128, 16, TOK], BF16)
    r1_end = cur[0]
    sb_wcrep = sba("sb_wcrep", [128, D], BF16, at=r1)
    sb_x2 = sba("sb_x2", [128, 2, D], F32, at=r1)
    sb_outst = sba("sb_outst", [128, 2, D], F32, at=r1 + 16384)
    sb_fo = sba("sb_fo", [128, D], F32, at=r1b)
    sb_fo2 = sba("sb_fo2", [128, D], F32, at=r1b + 8192)
    sb_ent = sba("sb_ent", [128, 2, D], F32, at=r1b)
    sb_entbf = sba("sb_entbf", [128, 2, D], BF16, at=r1b + 16384)
    sb_entsc = sba("sb_entsc", [8, D], F32, at=r1b + 24576)
    sb_hh = sba("sb_hh", [128, 64, 512], BF16, at=r1)

    def h_tile(j):
        return sb_hh[:, j, :]

    r2 = (r1_end + 63) // 64 * 64
    cur[0] = r2
    sb_xn2T = sba("sb_xn2T", [128, 16, TOK], BF16)
    r2_end = cur[0]
    a = r2
    sb_qf = sba("sb_qf", [128, TEXT], F32, at=a); a += TEXT * 4
    sb_rsqb = sba("sb_rsqb", [128, TEXT], F32, at=a); a += TEXT * 4
    sb_row = sba("sb_row", [1, TEXT], F32, at=a); a += TEXT * 4
    sb_qs = sba("sb_qs", [128, TEXT], BF16, at=a); a += TEXT * 2
    sb_usw = sba("sb_usw", [64, TEXT], BF16, at=a); a += TEXT * 2
    sb_u = sba("sb_u", [64, TEXT], BF16, at=a); a += TEXT * 2
    sb_v = sba("sb_v", [64, TEXT], BF16, at=a); a += TEXT * 2
    sb_lv = sba("sb_lv", [128, TEXT], BF16, at=a); a += TEXT * 2
    sb_qr2 = sba("sb_qr2", [128, 2, TOK], BF16, at=a); a += 2 * TOK * 2
    sb_mq = sba("sb_mq", [128, 1024], BF16, at=a); a += 1024 * 2
    assert a <= r2_end, (a, r2_end)

    cur[0] = r2_end
    sb_xc = sba("sb_xc", [128, 2, D], F32)
    sb_xnbf = sba("sb_xnbf", [128, 2, D], BF16)
    sb_dump = sba("sb_dump", [128, D], BF16)
    sb_stat = sba("sb_stat", [128, 26], F32)
    sc_at = (cur[0] + 63) // 64 * 64
    sb_sc = sba("sb_sc", [128, 2, 512], F32)
    sb_evf = sba("sb_evf", [128, 512], F32, at=sc_at)
    sb_esc = sba("sb_esc", [128, 2, 512], BF16)
    sb_attbf = sba("sb_attbf", [128, 2, 512], BF16)
    sb_attT = sba("sb_attT", [128, 2, 512], BF16)
    sb_attS = sba("sb_attS", [1, 2, 128], BF16)
    sb_mtmp = sba("sb_mtmp", [128, 2, 128], F32)
    sb_snk = sba("sb_snk", [128, 2, 20], F32)
    sb_lkr2 = sba("sb_lkr2", [128, 2, TEXT], BF16)
    sb_lvT2 = sba("sb_lvT2", [128, 2, 9, 128], BF16)
    sb_ask = sba("sb_ask", [128, 2], BF16)
    sb_ckf = sba("sb_ckf", [128, C], F32, at=r2)
    sb_ckq = sba("sb_ckq", [128, C], BF16, at=r2 + 1024)
    sb_wsl = sba("sb_wsl", [128, 4, 16, 128], BF16)
    sb_wt = sba("sb_wt", [128, 2, 1024], BF16)
    sb_absq = sba("sb_absq", [128, 2, 1024], BF16)
    sb_shs = sba("sb_shs", [2, 4, 512], BF16)
    sb_shf = sba("sb_shf", [128, 2, 1024], BF16)
    sb_sgq = sba("sb_sgq", [128, 1024], U16)
    sb_what = sba("sb_what", [128, 3, 1024], BF16)

    assert cur[0] <= SB_BUDGET, cur[0]

    bkA0 = Bank(nc.alloc_psum_tensor("pbA0", [128, 512], F32))
    bkA1 = Bank(nc.alloc_psum_tensor("pbA1", [128, 512], F32))
    bkB0 = Bank(nc.alloc_psum_tensor("pbB0", [128, 512], F32))
    bkB1 = Bank(nc.alloc_psum_tensor("pbB1", [128, 512], F32))
    bkTB = Bank(nc.alloc_psum_tensor("ptrb", [128, 1024], BF16))
    bkTF = Bank(nc.alloc_psum_tensor("ptrf", [128, 512], F32))
    bkSC = Bank(nc.alloc_psum_tensor("psc", [128, 512], F32))
    bkMS = Bank(nc.alloc_psum_tensor("pmisc", [128, 512], F32))

    for dst, src in ((sb_idbf, idn_bf_c), (sb_onesc, ones_col_c), (sb_onesr, ones_row_c),
                     (sb_patt8, patt8_c), (sb_patt8T, patt8T_c), (sb_patt8b, patt8b_c),
                     (sb_I2, I2_c), (sb_E2, E2_c), (sb_SW, SW64_c), (sb_cst, cst_c),
                     (sb_bqT, bqT), (sb_blkT, blkT), (sb_blvT, blvT), (sb_boT, boT)):
        Sc.dma("setup_s", lambda e, d=dst, s=src: e.dma_start(out=d[:], in_=s))
    for i, src in enumerate((bkc, bvc, bcc, qnc, knc)):
        Sc.dma("setup_s", lambda e, i=i, s=src: e.dma_start(out=sb_cols[:, i:i + 1], in_=s))
    Sc.dma("setup_s", lambda e: e.dma_start(out=sb_cmask[:], in_=cmv.rearrange("c p n -> p c n")))
    Sc.dma("setup_s", lambda e: e.dma_start(out=sb_lmask[:], in_=lmv.rearrange("c p n -> p c n")))
    for dst, src in ((sb_sinkT, sinkTf), (sb_sinkvR, sinkvRf), (sb_wcrep, wcrepf),
                     (sb_rtcc, rtcc), (sb_rtns, rtns), (sb_ckcc, cktcc), (sb_ckns, cktns)):
        Sc.dma("setup_p", lambda e, d=dst, s=src: e.dma_start(out=d[:], in_=s))
    Sc.dma("setup_p", lambda e: e.dma_start(
        out=sb_wsl[:, 2, :, :], in_=Wk.rearrange("(k p) n -> p k n", p=128)))
    Sc.dma("setup_p", lambda e: e.dma_start(
        out=sb_wsl[:, 3, :, :], in_=Wv.rearrange("(k p) n -> p k n", p=128)))
    qcast_gu = [Sc.dma("qcast_gu", lambda e, rr=rr: e.dma_start(
        out=wbf_gu[rr * 4096:(rr + 1) * 4096, :], in_=Wgu[rr * 4096:(rr + 1) * 4096, :]))
        for rr in range(4)]
    qcast_p = [Sc.dma("qcast_p", lambda e, rr=rr: e.dma_start(
        out=wbf_p[rr * 512:(rr + 1) * 512, :], in_=Wp[rr * 512:(rr + 1) * 512, :]))
        for rr in range(4)]
    qcast_a = {}
    for nm, (dst, srcw) in {"q": (wbf_q, Wq), "lk": (wbf_lk, Wlk),
                            "lv": (wbf_lv, Wlv), "o": (wbf_o, Wo)}.items():
        qcast_a[nm] = Sc.dma("qcast_a", lambda e, d=dst, s=srcw: e.dma_start(
            out=d, in_=s))
    BAR0 = Sc.barrier(streams=("setup_s", "setup_p"))

    qslot = [0]
    qcons = [[], []]
    gsring = [[], [], [], []]
    shf_cons = [[], []]
    qstate = {}
    wtok = {}
    bguard = [[], []]
    wslot = [0]
    whatg = [[], [], []]

    def emit_qA(which, k, hb):
        sl = qslot[0]; qslot[0] ^= 1
        if which == "gu":
            src = wbf_gu[hb * 1024:(hb + 1) * 1024, k * 128:(k + 1) * 128]
            cast_tok = [qcast_gu[hb // 4]]
        else:
            src = wbf_p[k * 1024:(k + 1) * 1024, hb * 128:(hb + 1) * 128]
            cast_tok = [qcast_p[2 * k], qcast_p[2 * k + 1]]
        t_tr = Sc.dma("wt", lambda e, src=src, sl=sl: e.dma_start(
            out=sb_wt[:, sl, :], in_=src, transpose=True),
            waits=cast_tok + qcons[sl] + BAR0)
        qcons[sl] = []
        t_abs = Sc.op("vector", lambda e, sl=sl: e.tensor_scalar(
            sb_absq[:, sl, :].bitcast(U16), sb_wt[:, sl, :].bitcast(U16), 0x7FFF,
            None, ALU.bitwise_and), waits=[t_tr])
        qcons[sl].append(t_abs)
        tshf2 = []
        for q2 in range(2):
            fs = q2 * 512
            ssl = sl * 2 + q2
            tm = Sc.op("tensor", lambda e, fs=fs, sl=sl: e.matmul(
                bkMS.t[0:2, 0:512], sb_I2[:], sb_absq[:, sl, fs:fs + 512],
                start=True, stop=True), waits=bkMS.wr_waits() + [t_abs])
            bkMS.set_writer(tm)
            tse = Sc.op("scalar", lambda e, ssl=ssl: e.activation(
                sb_shs[:, ssl, :], bkMS.t[0:2, 0:512], AF.Identity),
                waits=bkMS.rd_waits() + gsring[ssl])
            bkMS.add_reader(tse)
            tex = Sc.op("tensor", lambda e, ssl=ssl: e.matmul(
                bkTF.t[:, 0:512], sb_E2[:], sb_shs[:, ssl, :], start=True, stop=True),
                waits=bkTF.wr_waits() + [tse])
            gsring[ssl] = [tex]
            bkTF.set_writer(tex)
            tshf = Sc.op("scalar", lambda e, sl=sl, fs=fs: e.activation(
                sb_shf[:, sl, fs:fs + 512], bkTF.t[:, 0:512], AF.Identity),
                waits=bkTF.rd_waits() + (shf_cons[sl] if q2 == 0 else [])),
            shf_cons[sl] = [] if q2 == 0 else shf_cons[sl]
            tshf = tshf[0]
            bkTF.add_reader(tshf)
            tshf2.append(tshf)
        qstate[sl] = (which, k, hb, tshf2)

    def emit_qB(sl):
        which, k, hb, tshf2 = qstate[sl]
        w3 = wslot[0]
        wslot[0] = (w3 + 1) % 3
        tmq = Sc.op("vector", lambda e, sl=sl: e.tensor_tensor(
            sb_mq[:], sb_absq[:, sl, :], sb_shf[:, sl, :], ALU.is_gt),
            waits=tshf2 + bguard[0])
        tsg = Sc.op("vector", lambda e, sl=sl: e.tensor_scalar(
            sb_sgq[:], sb_wt[:, sl, :].bitcast(U16), 0x8000, None,
            ALU.bitwise_and), waits=bguard[1])
        tss = Sc.op("vector", lambda e, sl=sl: e.tensor_tensor(
            sb_absq[:, sl, :].bitcast(U16), sb_sgq[:], sb_shf[:, sl, :].bitcast(U16),
            ALU.bitwise_or), waits=[tsg, tmq])
        shf_cons[sl] = [tmq, tss]
        twh = Sc.op("vector", lambda e, sl=sl, w3=w3: e.scalar_tensor_tensor(
            sb_what[:, w3, :], sb_mq[:], 2.0, sb_absq[:, sl, :],
            ALU.mult, ALU.mult), waits=[tss] + whatg[w3])
        bguard[0] = [twh]
        bguard[1] = [tss]
        outs = [twh]
        qcons[sl].extend([tmq, tsg, tss, twh])
        if which == "gu":
            dst = whatT_gu[k, :, hb * 1024:(hb + 1) * 1024]
        else:
            dst = whatT_p[k * 8:(k + 1) * 8, hb, :, :].rearrange("m p f -> p m f")
        t_out = Sc.dma("whatw", lambda e, dst=dst, w3=w3: e.dma_start(
            out=dst, in_=sb_what[:, w3, :].rearrange("p (j f) -> p j f", f=128)),
            waits=outs)
        whatg[w3] = [t_out]
        if which == "gu":
            for j in range(hb * 8, hb * 8 + 8):
                wtok[("gu", j, k)] = t_out
        else:
            wtok[("p", k, hb)] = t_out

    lnst = {"sl": 0, "cons": [[], []]}
    dump_guard = []

    def emit_ln(x_ap, out_bf_ap, deps):
        nonlocal dump_guard
        sl = lnst["sl"]; lnst["sl"] ^= 1
        st = sb_stat[:, sl * 10:sl * 10 + 10]
        cons = lnst["cons"][sl]
        t_sum = Sc.op("vector", lambda e: e.reduce_sum(st[:, 0:1], x_ap, AX.X),
                      waits=deps + cons)
        t_sq = Sc.op("scalar", lambda e: e.activation(
            sb_dump[:], x_ap, AF.Square, accum_out=st[:, 1:2]),
            waits=deps + cons + dump_guard)
        dump_guard = [t_sq]
        lnst["cons"][sl] = []
        cons = lnst["cons"][sl]
        t1 = Sc.op("vector", lambda e: e.tensor_scalar(
            st[:, 2:3], st[:, 0:1], 1.0 / D, None, ALU.mult), waits=[t_sum])
        t2 = Sc.op("vector", lambda e: e.tensor_scalar(
            st[:, 3:4], st[:, 1:2], 1.0 / D, None, ALU.mult), waits=[t_sq])
        t3 = Sc.op("vector", lambda e: e.tensor_tensor(
            st[:, 4:5], st[:, 2:3], st[:, 2:3], ALU.mult), waits=[t1])
        t4 = Sc.op("vector", lambda e: e.tensor_tensor(
            st[:, 5:6], st[:, 3:4], st[:, 4:5], ALU.subtract), waits=[t2, t3])
        t5 = Sc.op("scalar", lambda e: e.activation(
            st[:, 6:7], st[:, 5:6], AF.Sqrt, bias=sb_cst[:, 0:1]), waits=[t4])
        t6 = Sc.op("vector", lambda e: e.reciprocal(st[:, 7:8], st[:, 6:7]), waits=[t5])
        t7 = Sc.op("vector", lambda e: e.scalar_tensor_tensor(
            st[:, 8:9], st[:, 2:3], -1.0, st[:, 7:8], ALU.mult, ALU.mult), waits=[t6])
        tn = None
        if out_bf_ap is not None:
            tn = Sc.op("scalar", lambda e: e.activation(
                out_bf_ap, x_ap, AF.Identity, bias=st[:, 8:9], scale=st[:, 7:8]),
                waits=[t7])
            cons.append(tn)
        return tn, st, t7

    quant_iter = ([("gu", k, hb) for hb in range(16) for k in range(16)] +
                  [("p", k2, hb) for k2 in range(2) for hb in range(64)])
    qi = [0]
    qpend = []

    def pump_quant(n):
        for _ in range(n):
            if qi[0] < len(quant_iter):
                sl_next = qslot[0]
                emit_qA(*quant_iter[qi[0]])
                qi[0] += 1
                qpend.append(sl_next)
            if len(qpend) > 1:
                emit_qB(qpend.pop(0))

    def drain_quant():
        while qpend:
            emit_qB(qpend.pop(0))

    xc_cons = [[], []]
    xn_cons = [[], []]
    ent_dmas = []
    t_blk_g = [None]
    x_pend = {}

    def issue_x(i):
        sl = i % 2
        x_pend[i] = Sc.dma("x", lambda e, i=i, sl=sl: e.dma_start(
            out=sb_xc[:, sl, :], in_=xb[i * 128:(i + 1) * 128, :]),
            waits=xc_cons[sl] + BAR0)
        xc_cons[sl] = []

    issue_x(0)
    for i in range(PFX):
        sl = i % 2
        t_x = x_pend.pop(i)
        if i + 1 < PFX:
            issue_x(i + 1)
        t_xn, _, _ = emit_ln(sb_xc[:, sl, :], sb_xnbf[:, sl, :], [t_x] + xn_cons[sl])
        xn_cons[sl] = []
        xc_cons[sl].append(t_xn)
        t_wc = Sc.op("vector", lambda e, sl=sl: e.scalar_tensor_tensor(
            sb_dump[:], sb_xnbf[:, sl, :], 1.0, sb_wcrep[:], ALU.mult, ALU.mult,
            accum_out=sb_stat[:, 20:21]), waits=[t_xn] + dump_guard)
        dump_guard = [t_wc]
        if i >= 10:
            pump_quant(3)
        t_ew = Sc.op("scalar", lambda e: e.activation(
            sb_stat[:, 21:22], sb_stat[:, 20:21], AF.Exp, bias=sb_cols[:, 2:3]),
            waits=[t_wc])
        t_sm = Sc.op("tensor", lambda e: e.matmul(
            bkMS.t[0:8, 0:1], sb_patt8[:], sb_stat[:, 21:22], start=True, stop=True),
            waits=bkMS.wr_waits() + [t_ew])
        bkMS.set_writer(t_sm)
        t_ev = Sc.op("vector", lambda e: e.tensor_copy(
            sb_stat[0:8, 22:23], bkMS.t[0:8, 0:1]), waits=bkMS.rd_waits())
        bkMS.add_reader(t_ev)
        t_rc = Sc.op("vector", lambda e: e.reciprocal(
            sb_stat[0:8, 23:24], sb_stat[0:8, 22:23]), waits=[t_ev])
        t_blk = Sc.op("vector", lambda e: e.tensor_scalar(
            sb_attbf[:, 0, 1:9], sb_patt8b[:], sb_stat[:, 21:22], None, ALU.mult),
            waits=[t_ew, t_blk_g[0]] if t_blk_g[0] else [t_ew])
        t_blk_g[0] = None
        if i >= 10:
            pump_quant(3)
        ev_toks = []
        tm = None
        for q4 in range(4):
            bnk = bkSC if q4 % 2 == 0 else bkB0
            tm = Sc.op("tensor", lambda e, q4=q4, sl=sl, b=bnk: e.matmul(
                b.t[0:8, 0:512], sb_attbf[:, 0, 1:9],
                sb_xnbf[:, sl, q4 * 512:(q4 + 1) * 512], start=True, stop=True),
                waits=bnk.wr_waits() + [t_blk])
            bnk.set_writer(tm)
            te = Sc.op("vector", lambda e, q4=q4, b=bnk: e.tensor_scalar(
                sb_entsc[:, q4 * 512:(q4 + 1) * 512], b.t[0:8, 0:512],
                sb_stat[0:8, 23:24], None, ALU.mult),
                waits=bnk.rd_waits() + [t_rc] + (ent_dmas[-2:] if q4 == 0 else []))
            bnk.add_reader(te)
            ev_toks.append(te)
            t_blk_g[0] = tm
        xn_cons[sl] = [tm, t_wc]
        t_ed = Sc.dma("entw", lambda e, i=i: e.dma_start(
            out=entsc_d[i, :, :], in_=sb_entsc[:]), waits=ev_toks)
        ent_dmas.append(t_ed)
        xn_cons[sl].append(t_ed)
        if i >= 10:
            pump_quant(6)

    t_stk = Sc.dma("stk", lambda e: e.dma_start(
        out=sb_ent[:],
        in_=entsc_d.rearrange("c e d -> (c e) d").rearrange("(t p) d -> p t d", p=128)),
        waits=ent_dmas)
    t_eb = [Sc.op("vector", lambda e, tt=tt: e.tensor_copy(
        sb_entbf[:, tt, :], sb_ent[:, tt, :]), waits=[t_stk]) for tt in range(2)]
    entT_toks = []
    for kd in range(16):
        for tt in range(2):
            tm = Sc.op("tensor", lambda e, kd=kd, tt=tt: e.transpose(
                bkTB.t[:, 0:128], sb_entbf[:, tt, kd * 128:(kd + 1) * 128], sb_idbf[:]),
                waits=bkTB.wr_waits() + t_eb)
            bkTB.set_writer(tm)
            te = Sc.op("vector", lambda e, kd=kd, tt=tt: e.tensor_copy(
                sb_xnT[:, kd, tt * 128:(tt + 1) * 128], bkTB.t[:, 0:128]),
                waits=bkTB.rd_waits())
            bkTB.add_reader(te)
            entT_toks.append(te)

    def emit_ckcv(is_ck):
        slab = 2 if is_ck else 3
        w0 = bkSC.wr_waits() + entT_toks + BAR0
        last = None
        for kd in range(16):
            last = Sc.op("tensor", lambda e, kd=kd, slab=slab: e.matmul(
                bkSC.t[:, 0:256], sb_wsl[:, slab, kd, :], sb_xnT[:, kd, 0:256],
                start=(kd == 0), stop=(kd == 15)),
                waits=w0 if kd == 0 else [], sig=(kd == 15))
        bkSC.set_writer(last)
        bias = sb_cols[:, 0:1] if is_ck else sb_cols[:, 1:2]
        t_ev = Sc.op("scalar", lambda e, bias=bias: e.activation(
            sb_ckf[:], bkSC.t[:, 0:256], AF.Identity, bias=bias),
            waits=bkSC.rd_waits())
        bkSC.add_reader(t_ev)
        return t_ev, last

    t_ckf, ck_mm = emit_ckcv(True)
    t_cq = Sc.op("scalar", lambda e: e.activation(sb_ckq[:], sb_ckf[:], AF.Square),
                 waits=[t_ckf])
    t_cs = Sc.op("tensor", lambda e: e.matmul(
        bkMS.t[0:1, 0:256], sb_onesc[:], sb_ckq[:], start=True, stop=True),
        waits=bkMS.wr_waits() + [t_cq])
    bkMS.set_writer(t_cs)
    t_ce = Sc.op("vector", lambda e: e.tensor_scalar(
        sb_row[0:1, 0:256], bkMS.t[0:1, 0:256], 1.0 / HD, None, ALU.mult),
        waits=bkMS.rd_waits())
    bkMS.add_reader(t_ce)
    t_cs2 = Sc.op("scalar", lambda e: e.activation(
        sb_row[0:1, 256:512], sb_row[0:1, 0:256], AF.Sqrt, bias=sb_cst[0:1, 1:2]),
        waits=[t_ce])
    t_cr = Sc.op("vector", lambda e: e.reciprocal(
        sb_row[0:1, 512:768], sb_row[0:1, 256:512]), waits=[t_cs2])
    t_cb = Sc.op("tensor", lambda e: e.matmul(
        bkTF.t[:, 0:256], sb_onesr[:], sb_row[0:1, 512:768], start=True, stop=True),
        waits=bkTF.wr_waits() + [t_cr])
    bkTF.set_writer(t_cb)
    t_cbe = Sc.op("vector", lambda e: e.tensor_copy(sb_rsqb[:, 0:256], bkTF.t[:, 0:256]),
                  waits=bkTF.rd_waits())
    bkTF.add_reader(t_cbe)
    t_ct = Sc.op("vector", lambda e: e.scalar_tensor_tensor(
        sb_ckr[:], sb_ckf[:], sb_cols[:, 4:5], sb_rsqb[:, 0:256], ALU.mult, ALU.mult),
        waits=[t_cbe])
    t_csw = Sc.op("tensor", lambda e: e.matmul(
        bkMS.t[0:64, 0:256], sb_SW[:], sb_ckr[0:64, :], start=True, stop=True),
        waits=bkMS.wr_waits() + [t_ct])
    bkMS.set_writer(t_csw)
    t_cswe = Sc.op("vector", lambda e: e.tensor_copy(sb_usw[:, 0:256], bkMS.t[0:64, 0:256]),
                   waits=bkMS.rd_waits())
    bkMS.add_reader(t_cswe)
    t_cu = Sc.op("vector", lambda e: e.tensor_tensor(
        sb_u[:, 0:256], sb_ckr[0:64, :], sb_ckcc[:], ALU.mult), waits=[t_ct])
    t_cv2 = Sc.op("vector", lambda e: e.tensor_tensor(
        sb_v[:, 0:256], sb_usw[:, 0:256], sb_ckns[:], ALU.mult), waits=[t_cswe])
    t_ckdone = Sc.op("vector", lambda e: e.tensor_tensor(
        sb_ckr[0:64, :], sb_u[:, 0:256], sb_v[:, 0:256], ALU.add),
        waits=[t_cu, t_cv2])
    t_cvf, cv_mm = emit_ckcv(False)
    t_cvb = Sc.op("vector", lambda e: e.tensor_copy(sb_ckq[:], sb_ckf[:]), waits=[t_cvf])
    cvT_toks = []
    for tt in range(2):
        tm = Sc.op("tensor", lambda e, tt=tt: e.transpose(
            bkTB.t[:, 0:128], sb_ckq[:, tt * 128:(tt + 1) * 128], sb_idbf[:]),
            waits=bkTB.wr_waits() + [t_cvb])
        bkTB.set_writer(tm)
        te = Sc.op("vector", lambda e, tt=tt: e.tensor_copy(
            sb_cvT[:, tt, :], bkTB.t[:, 0:128]), waits=bkTB.rd_waits())
        bkTB.add_reader(te)
        cvT_toks.append(te)
    ckcv_toks = [t_ckdone] + cvT_toks
    BARP = Sc.barrier(streams=("entw", "stk"))
    slab_seed = {2: [ck_mm], 3: [cv_mm]}

    xnT_last = []
    x2_pend = {}

    def issue_x2(i):
        sl = i % 2
        x2_pend[i] = Sc.dma("x", lambda e, i=i, sl=sl: e.dma_start(
            out=sb_xc[:, sl, :], in_=xo[i * 128:(i + 1) * 128, :]),
            waits=xc_cons[sl] + (BARP if i < 2 else []))
        xc_cons[sl] = []

    issue_x2(0)
    for i in range(9):
        sl = i % 2
        t_x = x2_pend.pop(i)
        if i + 1 < 9:
            issue_x2(i + 1)
        t_xn, _, _ = emit_ln(sb_xc[:, sl, :], sb_xnbf[:, sl, :], [t_x] + xn_cons[sl])
        xn_cons[sl] = []
        xc_cons[sl].append(t_xn)
        for kd in range(16):
            tm = Sc.op("tensor", lambda e, kd=kd, sl=sl: e.transpose(
                bkTB.t[:, 0:128], sb_xnbf[:, sl, kd * 128:(kd + 1) * 128], sb_idbf[:]),
                waits=bkTB.wr_waits() + [t_xn] + (BARP if i == 0 else []))
            bkTB.set_writer(tm)
            te = Sc.op("vector", lambda e, kd=kd, i=i: e.tensor_copy(
                sb_xnT[:, kd, i * 128:(i + 1) * 128], bkTB.t[:, 0:128]),
                waits=bkTB.rd_waits())
            bkTB.add_reader(te)
            if kd == 7:
                pump_quant(4)
            if kd == 15:
                xn_cons[sl].append(tm)
                xnT_last = [te]
        pump_quant(4)
    BAR1 = Sc.barrier()

    slab_cons = {i: slab_seed.get(i, []) for i in range(4)}
    slab_rot = [0]
    merged_toks = []
    def next_slab():
        s = slab_rot[0]
        slab_rot[0] = (slab_rot[0] + 1) % 4
        return s

    def emit_proj(Wsrc, cast_tok, h, bias_col, c0, ncol, out_f32, deps, out_bf=None,
                  bank_fixed=None):
        slab = next_slab()
        t_sl = Sc.dma("aslab", lambda e, slab=slab: e.dma_start(
            out=sb_wsl[:, slab, :, :],
            in_=Wsrc[:, h * 128:(h + 1) * 128].rearrange("(k p) n -> p k n", p=128)),
            waits=slab_cons[slab] + [cast_tok])
        slab_cons[slab] = []
        ev_toks = []
        for nb in range((ncol + 511) // 512):
            f0 = nb * 512
            cw = min(512, ncol - f0)
            bank = bank_fixed or (bkA0 if nb % 2 == 0 else bkA1)
            w0 = bank.wr_waits() + [t_sl] + deps
            last = None
            for kd in range(16):
                last = Sc.op("tensor", lambda e, kd=kd, f0=f0, cw=cw, slab=slab, b=bank: e.matmul(
                    b.t[:, 0:cw], sb_wsl[:, slab, kd, :],
                    sb_xnT[:, kd, c0 + f0:c0 + f0 + cw],
                    start=(kd == 0), stop=(kd == 15)),
                    waits=w0 if kd == 0 else [], sig=(kd == 15))
            bank.set_writer(last)
            slab_cons[slab].append(last)
            dst = out_f32 if out_bf is None else out_bf
            tev = Sc.op("scalar", lambda e, f0=f0, cw=cw, dst=dst, b=bank: e.activation(
                dst[:, f0:f0 + cw], b.t[:, 0:cw], AF.Identity, bias=bias_col),
                waits=bank.rd_waits())
            bank.add_reader(tev)
            ev_toks.append(tev)
        return ev_toks

    def emit_rms_rope(ncol, toff, gain_col, out_ap, deps):
        nonlocal dump_guard
        t_sq = Sc.op("scalar", lambda e: e.activation(
            sb_dump[:, 0:ncol], sb_qf[:, 0:ncol], AF.Square),
            waits=deps + dump_guard)
        dump_guard = []
        rs_toks = []
        for nb in range((ncol + 511) // 512):
            f0 = nb * 512; cw = min(512, ncol - f0)
            tm = Sc.op("tensor", lambda e, f0=f0, cw=cw: e.matmul(
                bkMS.t[0:1, 0:cw], sb_onesc[:], sb_dump[:, f0:f0 + cw],
                start=True, stop=True), waits=bkMS.wr_waits() + [t_sq])
            bkMS.set_writer(tm)
            dump_guard.append(tm)
            te = Sc.op("vector", lambda e, f0=f0, cw=cw: e.tensor_scalar(
                sb_row[0:1, f0:f0 + cw], bkMS.t[0:1, 0:cw], 1.0 / HD, None, ALU.mult),
                waits=bkMS.rd_waits())
            bkMS.add_reader(te)
            rs_toks.append(te)
        t_rc2 = Sc.op("scalar", lambda e: e.activation(
            sb_row[0:1, 0:ncol], sb_row[0:1, 0:ncol], AF.Sqrt, bias=sb_cst[0:1, 1:2]),
            waits=rs_toks)
        bc_toks = []
        for nb in range((ncol + 511) // 512):
            f0 = nb * 512; cw = min(512, ncol - f0)
            tm = Sc.op("tensor", lambda e, f0=f0, cw=cw: e.matmul(
                bkTF.t[:, 0:cw], sb_onesr[:], sb_row[0:1, f0:f0 + cw],
                start=True, stop=True), waits=bkTF.wr_waits() + [t_rc2])
            bkTF.set_writer(tm)
            te = Sc.op("vector", lambda e, f0=f0, cw=cw: e.reciprocal(
                sb_rsqb[:, f0:f0 + cw], bkTF.t[:, 0:cw]), waits=bkTF.rd_waits())
            bkTF.add_reader(te)
            bc_toks.append(te)
        t_t = Sc.op("vector", lambda e: e.scalar_tensor_tensor(
            sb_qs[:, 0:ncol], sb_qf[:, 0:ncol], gain_col, sb_rsqb[:, 0:ncol],
            ALU.mult, ALU.mult), waits=bc_toks + deps)
        sw_toks = []
        for nb in range((ncol + 511) // 512):
            f0 = nb * 512; cw = min(512, ncol - f0)
            tm = Sc.op("tensor", lambda e, f0=f0, cw=cw: e.matmul(
                bkMS.t[0:64, 0:cw], sb_SW[:], sb_qs[0:64, f0:f0 + cw],
                start=True, stop=True), waits=bkMS.wr_waits() + [t_t])
            bkMS.set_writer(tm)
            te = Sc.op("vector", lambda e, f0=f0, cw=cw: e.tensor_copy(
                sb_usw[:, f0:f0 + cw], bkMS.t[0:64, 0:cw]), waits=bkMS.rd_waits())
            bkMS.add_reader(te)
            sw_toks.append(te)
        t_u = Sc.op("vector", lambda e: e.tensor_tensor(
            sb_u[:, 0:ncol], sb_qs[0:64, 0:ncol], sb_rtcc[:, toff:toff + ncol],
            ALU.mult), waits=[t_t])
        t_v = Sc.op("vector", lambda e: e.tensor_tensor(
            sb_v[:, 0:ncol], sb_usw[:, 0:ncol], sb_rtns[:, toff:toff + ncol],
            ALU.mult), waits=sw_toks)
        t_o1 = Sc.op("vector", lambda e: e.tensor_tensor(
            out_ap[0:64, 0:ncol], sb_u[:, 0:ncol], sb_v[:, 0:ncol], ALU.add),
            waits=[t_u, t_v])
        t_rest = Sc.op("vector", lambda e: e.tensor_copy(
            out_ap[64:128, 0:ncol], sb_qs[64:128, 0:ncol]), waits=[t_t])
        return [t_o1, t_rest]

    prev_qr_cons = [[], []]
    prev_lkr_cons = [[], []]
    prev_lv_cons = [[], []]
    prev_lvT_cons = [[], []]
    prev_snk_cons = [[], []]
    ropeg = {"qs": [], "u": [], "row": []}
    front_state = [None, None]

    def front_gen(h):
        hs = h % 2
        deps0 = (BAR1 if h == 0 else []) + prev_qr_cons[hs]
        prev_qr_cons[hs] = []
        if h > 0:
            deps0 = deps0 + front_state[1 - hs]["qf_free"]
        q_ev = emit_proj(wbf_q, qcast_a["q"], h, sb_bqT[:, h:h + 1], 128, TOK,
                         sb_qf, deps0, bank_fixed=bkA0)
        yield
        qr_toks = emit_rms_rope(TOK, 128, sb_cols[:, 3:4], sb_qr2[:, hs, :], q_ev)
        yield
        lk_ev = emit_proj(wbf_lk, qcast_a["lk"], h, sb_blkT[:, h:h + 1], 0, TEXT,
                          sb_qf, qr_toks + prev_lkr_cons[hs], bank_fixed=bkA0)
        prev_lkr_cons[hs] = []
        yield
        lkr_toks = emit_rms_rope(TEXT, 0, sb_cols[:, 4:5], sb_lkr2[:, hs, :], lk_ev)
        yield
        lv_ev = emit_proj(wbf_lv, qcast_a["lv"], h, sb_blvT[:, h:h + 1], 0, TEXT,
                          None, lkr_toks + prev_lv_cons[hs], out_bf=sb_lv,
                          bank_fixed=bkA0)
        prev_lv_cons[hs] = []
        yield
        lvT_toks = []
        for j in range(9):
            tm = Sc.op("tensor", lambda e, j=j: e.transpose(
                bkTB.t[:, 0:128], sb_lv[:, j * 128:(j + 1) * 128], sb_idbf[:]),
                waits=bkTB.wr_waits() + lv_ev)
            bkTB.set_writer(tm)
            te = Sc.op("vector", lambda e, j=j, hs=hs: e.tensor_copy(
                sb_lvT2[:, hs, j, :], bkTB.t[:, 0:128]),
                waits=bkTB.rd_waits() + (prev_lvT_cons[hs] if j == 0 else []))
            bkTB.add_reader(te)
            lvT_toks.append(te)
            prev_lv_cons[hs].append(tm)
        prev_lvT_cons[hs] = []
        yield
        t_sr0 = Sc.op("tensor", lambda e, h=h, hs=hs: e.matmul(
            bkMS.t[0:1, 0:512], sb_sinkT[:, h:h + 1], sb_qr2[:, hs, 0:512],
            start=True, stop=True), waits=bkMS.wr_waits() + qr_toks)
        bkMS.set_writer(t_sr0)
        t_se0 = Sc.op("vector", lambda e: e.tensor_scalar(
            sb_row[0:1, 0:512], bkMS.t[0:1, 0:512], float(SCALE), None, ALU.mult),
            waits=bkMS.rd_waits() + ropeg["row"])
        bkMS.add_reader(t_se0)
        t_sr1 = Sc.op("tensor", lambda e, h=h, hs=hs: e.matmul(
            bkMS.t[0:1, 0:512], sb_sinkT[:, h:h + 1], sb_qr2[:, hs, 512:1024],
            start=True, stop=True), waits=bkMS.wr_waits())
        bkMS.set_writer(t_sr1)
        t_se1 = Sc.op("vector", lambda e: e.tensor_scalar(
            sb_row[0:1, 512:1024], bkMS.t[0:1, 0:512], float(SCALE), None, ALU.mult),
            waits=bkMS.rd_waits())
        bkMS.add_reader(t_se1)
        t_sbf = Sc.op("vector", lambda e: e.tensor_copy(
            sb_u[0:1, 0:TOK], sb_row[0:1, 0:TOK]),
            waits=[t_se0, t_se1] + ropeg["u"] + lkr_toks)
        snk_evs = []
        tmx = None
        for cch in range(NCH):
            tmx = Sc.op("tensor", lambda e, cch=cch: e.transpose(
                bkTB.t[:, 896:897], sb_u[0:1, cch * 128:(cch + 1) * 128],
                sb_idbf[0:1, 0:1]),
                waits=bkTB.wr_waits() + [t_sbf] + snk_evs[-1:])
            bkTB.set_writer(tmx)
            tev = Sc.op("vector", lambda e, cch=cch, hs=hs: e.tensor_copy(
                sb_snk[:, hs, cch:cch + 1], bkTB.t[:, 896:897]),
                waits=bkTB.rd_waits() + (prev_snk_cons[hs] if cch == 0 else []))
            bkTB.add_reader(tev)
            snk_evs.append(tev)
        prev_snk_cons[hs] = []
        t_nsk = Sc.op("vector", lambda e, hs=hs: e.tensor_scalar(
            sb_snk[:, hs, 8:16], sb_snk[:, hs, 0:8], -1.0, None, ALU.mult),
            waits=snk_evs)
        ropeg["row"] = [t_sbf]
        ropeg["u"] = [tmx]
        front_state[hs] = {"qr": qr_toks, "lkr": lkr_toks, "lvT": lvT_toks,
                           "nsk": [t_nsk], "qf_free": lkr_toks}
        yield

    def run_gen(g):
        if g is not None:
            try:
                next(g)
                return g
            except StopIteration:
                return None
        return None

    merged_toks = merged_toks if False else merged_toks
    chain_g = [[], []]
    esg = [[], []]
    att_g = [[], []]
    askg = [[], []]
    mtg = [[], []]
    g0 = front_gen(0)
    for _ in range(10):
        g0 = run_gen(g0)
        if g0 is None:
            break

    for h in range(NH):
        hs = h % 2
        fs = front_state[hs]
        qr_toks = fs["qr"]; lkr_toks = fs["lkr"]; lvT_toks = fs["lvT"]
        nsk_toks = fs["nsk"]
        gen_next = front_gen(h + 1) if h + 1 < NH else None

        def emit_scores(cch):
            q0 = cch * 128
            par = cch % 2
            P = bkSC if par == 0 else bkA1
            mm1 = Sc.op("tensor", lambda e, q0=q0, b=P, hs=hs: e.matmul(
                b.t[:, 0:256], sb_qr2[:, hs, q0:q0 + 128], sb_ckr[:],
                start=True, stop=False),
                waits=P.wr_waits() + qr_toks + ckcv_toks + lkr_toks, sig=False)
            mm2 = Sc.op("tensor", lambda e, q0=q0, b=P, hs=hs: e.matmul(
                b.t[:, 256:512], sb_qr2[:, hs, q0:q0 + 128],
                sb_lkr2[:, hs, q0:q0 + 256], start=False, stop=True), waits=[])
            P.set_writer(mm2)
            prev_qr_cons[hs].append(mm2)
            prev_lkr_cons[hs].append(mm2)
            return P

        def emit_chain(cch, P):
            q0 = cch * 128
            par = cch % 2
            X = bkB0 if par == 0 else bkB1
            stc = sb_stat[:, 14 + par * 6: 20 + par * 6]
            midx = 0 if cch == 0 else 1
            t_ev0 = Sc.op("vector", lambda e, cch=cch, par=par, b=P: e.scalar_tensor_tensor(
                sb_sc[:, par, 0:256], b.t[:, 0:256], float(SCALE), sb_cmask[:, cch, :],
                ALU.mult, ALU.add), waits=P.rd_waits() + chain_g[par])
            t_ev = Sc.op("vector", lambda e, midx=midx, par=par, b=P: e.scalar_tensor_tensor(
                sb_sc[:, par, 256:512], b.t[:, 256:512], float(SCALE),
                sb_lmask[:, midx, :], ALU.mult, ALU.add), waits=[t_ev0])
            P.add_reader(t_ev)
            t_mx = Sc.op("vector", lambda e, par=par, stc=stc: e.reduce_max(
                stc[:, 0:2], sb_sc[:, par, :].rearrange("p (g w) -> p g w", w=256),
                AX.X, negate=True), waits=[t_ev])
            t_mf = Sc.op("vector", lambda e, cch=cch, stc=stc, hs=hs: e.tensor_scalar(
                stc[:, 0:1], stc[:, 0:1], sb_snk[:, hs, 8 + cch:9 + cch], None,
                ALU.min), waits=[t_mx] + nsk_toks)
            t_e1 = Sc.op("scalar", lambda e, par=par, stc=stc: e.activation(
                sb_esc[:, par, 0:256], sb_sc[:, par, 0:256], AF.Exp,
                bias=stc[:, 0:1], accum_out=stc[:, 2:3]), waits=[t_mf])
            t_e2 = Sc.op("scalar", lambda e, par=par, stc=stc: e.activation(
                sb_esc[:, par, 256:512], sb_sc[:, par, 256:512], AF.Exp,
                bias=stc[:, 1:2], accum_out=stc[:, 3:4]), waits=[t_mx, t_e1])
            t_es = Sc.op("scalar", lambda e, cch=cch, par=par, stc=stc, hs=hs: e.activation(
                sb_snk[:, hs, 16 + par:17 + par], sb_snk[:, hs, cch:cch + 1], AF.Exp,
                bias=stc[:, 0:1]), waits=[t_mf] + esg[par])
            t_sa = Sc.op("vector", lambda e, par=par, stc=stc, hs=hs: e.tensor_tensor(
                stc[:, 2:3], stc[:, 2:3], sb_snk[:, hs, 16 + par:17 + par], ALU.add),
                waits=[t_e1, t_es])
            t_rc = Sc.op("vector", lambda e, stc=stc: e.reciprocal(
                stc[:, 4:6], stc[:, 2:4]), waits=[t_sa, t_e2])
            t_a1 = Sc.op("scalar", lambda e, par=par, stc=stc: e.activation(
                sb_attbf[:, par, 0:256], sb_esc[:, par, 0:256], AF.Identity,
                scale=stc[:, 4:5]), waits=[t_rc] + att_g[par])
            t_a2 = Sc.op("scalar", lambda e, par=par, stc=stc: e.activation(
                sb_attbf[:, par, 256:512], sb_esc[:, par, 256:512], AF.Identity,
                scale=stc[:, 5:6]), waits=[t_rc, t_a1])
            t_ak = Sc.op("vector", lambda e, par=par, stc=stc, hs=hs: e.scalar_tensor_tensor(
                sb_ask[:, par:par + 1], sb_snk[:, hs, 16 + par:17 + par], 1.0,
                stc[:, 4:5], ALU.mult, ALU.mult), waits=[t_rc] + askg[par])
            prev_snk_cons[hs].extend([t_mf, t_es, t_sa, t_ak])
            esg[par] = [t_sa]
            tt = None
            for r in range(4):
                tt = Sc.op("tensor", lambda e, par=par, r=r: e.transpose(
                    bkTB.t[:, r * 128:(r + 1) * 128],
                    sb_attbf[:, par, r * 128:(r + 1) * 128], sb_idbf[:]),
                    waits=(bkTB.wr_waits() if r == 0 else []) +
                    ([t_a1] if r < 2 else [t_a2]), sig=(r == 3))
            tt5 = Sc.op("tensor", lambda e, par=par: e.transpose(
                bkTB.t[0:1, 512:640], sb_ask[:, par:par + 1], sb_idbf[:]),
                waits=[t_ak])
            bkTB.set_writer(tt5)
            t_aT = Sc.op("vector", lambda e, par=par: e.tensor_copy(
                sb_attT[:, par, :], bkTB.t[:, 0:512]), waits=bkTB.rd_waits())
            t_aS = Sc.op("vector", lambda e, par=par: e.tensor_copy(
                sb_attS[:, par, :], bkTB.t[0:1, 512:640]), waits=bkTB.rd_waits())
            bkTB.add_reader(t_aT)
            bkTB.add_reader(t_aS)
            att_g[par] = [tt, tt5]
            askg[par] = [tt5]
            c1 = Sc.op("tensor", lambda e, par=par, b=X: e.matmul(
                b.t[:, 0:128], sb_cvT[:, 0, :], sb_attT[:, par, 0:128],
                start=True, stop=False), waits=X.wr_waits() + [t_aT], sig=False)
            c2 = Sc.op("tensor", lambda e, par=par, b=X: e.matmul(
                b.t[:, 0:128], sb_cvT[:, 1, :], sb_attT[:, par, 128:256],
                start=False, stop=False), waits=[], sig=False)
            c3 = Sc.op("tensor", lambda e, par=par, h=h, b=X: e.matmul(
                b.t[:, 0:128], sb_sinkvR[:, h * 128:(h + 1) * 128],
                sb_attS[:, par, :], start=False, stop=False), waits=[t_aS], sig=False)
            l1 = Sc.op("tensor", lambda e, par=par, cch=cch, b=X, hs=hs: e.matmul(
                b.t[:, 128:256], sb_lvT2[:, hs, cch, :], sb_attT[:, par, 256:384],
                start=False, stop=False), waits=lvT_toks, sig=False)
            l2 = Sc.op("tensor", lambda e, par=par, cch=cch, b=X, hs=hs: e.matmul(
                b.t[:, 128:256], sb_lvT2[:, hs, cch + 1, :], sb_attT[:, par, 384:512],
                start=False, stop=True), waits=[])
            X.set_writer(l2)
            prev_lvT_cons[hs].append(l2)
            t_m1 = Sc.op("scalar", lambda e, par=par, b=X: e.activation(
                sb_mtmp[:, par, :], b.t[:, 0:128], AF.Identity,
                scale=sb_cst[:, 3:4]), waits=X.rd_waits() + mtg[par])
            X.add_reader(t_m1)
            t_m2 = Sc.op("vector", lambda e, h=h, q0=q0, par=par, b=X: e.scalar_tensor_tensor(
                sb_merged[:, h, q0:q0 + 128], b.t[:, 128:256], 0.5,
                sb_mtmp[:, par, :], ALU.mult, ALU.add),
                waits=X.rd_waits() + [t_m1])
            X.add_reader(t_m2)
            mtg[par] = [t_m2]
            chain_g[par] = [t_e2, t_aT]
            merged_toks.append(t_m2)

        pend = []
        for cch in range(NCH + 1):
            if cch < NCH:
                pend.append((cch, emit_scores(cch)))
            if len(pend) > (1 if cch < NCH else 0):
                c0_, P0_ = pend.pop(0)
                emit_chain(c0_, P0_)
            gen_next = run_gen(gen_next)
            pump_quant(2)
        while gen_next is not None:
            gen_next = run_gen(gen_next)

    drain_quant()

    BAR2 = Sc.barrier()

    ao_dmas = []
    evf_guard = []
    for m in range(16):
        slab = next_slab()
        t_sl = Sc.dma("aslab", lambda e, m=m, slab=slab: e.dma_start(
            out=sb_wsl[:, slab, :, :],
            in_=wbf_o[:, m * 128:(m + 1) * 128].rearrange("(k p) n -> p k n", p=128)),
            waits=slab_cons[slab] + [qcast_a["o"]])
        slab_cons[slab] = []
        for th in range(2):
            bank = bkA0 if th == 0 else bkA1
            w0 = bank.wr_waits() + [t_sl] + merged_toks[-1:]
            last = None
            for kd in range(16):
                last = Sc.op("tensor", lambda e, kd=kd, th=th, slab=slab, b=bank: e.matmul(
                    b.t[:, 0:512], sb_wsl[:, slab, kd, :],
                    sb_merged[:, kd, th * 512:(th + 1) * 512],
                    start=(kd == 0), stop=(kd == 15)),
                    waits=w0 if kd == 0 else [], sig=(kd == 15))
            bank.set_writer(last)
            slab_cons[slab].append(last)
            t_ev = Sc.op("scalar", lambda e, m=m, b=bank: e.activation(
                sb_evf[:], b.t[:, 0:512], AF.Identity, bias=sb_boT[:, m:m + 1]),
                waits=bank.rd_waits() + evf_guard)
            bank.add_reader(t_ev)
            t_evb = Sc.op("vector", lambda e: e.tensor_copy(
                sb_dump[:, 0:512], sb_evf[:]), waits=[t_ev] + dump_guard)
            evf_guard = [t_evb]
            dump_guard = []
            for p4 in range(4):
                cch = th * 4 + p4
                tm = Sc.op("tensor", lambda e, p4=p4: e.transpose(
                    bkTB.t[:, 0:128], sb_dump[:, p4 * 128:(p4 + 1) * 128], sb_idbf[:]),
                    waits=bkTB.wr_waits() + [t_evb])
                bkTB.set_writer(tm)
                dump_guard.append(tm)
                te = Sc.op("vector", lambda e: e.tensor_copy(
                    sb_mtmp[:, 0, :], bkTB.t[:, 0:128]), waits=bkTB.rd_waits())
                bkTB.add_reader(te)
                td = Sc.dma("aoscw", lambda e, cch=cch, m=m: e.dma_start(
                    out=aosc[cch, :, m * 128:(m + 1) * 128], in_=sb_mtmp[:, 0, :]),
                    waits=[te])
                bkTB.add_reader(td)
                ao_dmas.append(td)

    x2_cons = [[], []]
    xn2T_last = []
    x2sp_toks = []
    fo_guard = []
    for cch in range(NCH):
        sl = cch % 2
        t_ldao = Sc.dma("x", lambda e, cch=cch: e.dma_start(
            out=sb_fo[:], in_=aosc[cch, :, :]), waits=ao_dmas[-2:] + fo_guard)
        t_x = Sc.dma("x", lambda e, cch=cch, sl=sl: e.dma_start(
            out=sb_xc[:, sl, :], in_=xo[(cch + 1) * 128:(cch + 2) * 128, :]),
            waits=xc_cons[sl])
        xc_cons[sl] = []
        _, st, t7 = emit_ln(sb_fo[:], None, [t_ldao])
        t_nf = Sc.op("vector", lambda e, st=st: e.tensor_scalar(
            sb_fo[:], sb_fo[:], st[:, 7:8], st[:, 8:9], ALU.mult, ALU.add),
            waits=[t7])
        t_x2 = Sc.op("vector", lambda e, sl=sl: e.tensor_tensor(
            sb_x2[:, sl, :], sb_fo[:], sb_xc[:, sl, :], ALU.add),
            waits=[t_nf, t_x] + x2_cons[sl])
        fo_guard = [t_x2]
        x2_cons[sl] = []
        xc_cons[sl].append(t_x2)
        t_sp = Sc.dma("x2spw", lambda e, cch=cch, sl=sl: e.dma_start(
            out=x2sp[cch, :, :], in_=sb_x2[:, sl, :]), waits=[t_x2])
        x2_cons[sl].append(t_sp)
        x2sp_toks.append(t_sp)
        t_xn2, _, _ = emit_ln(sb_x2[:, sl, :], sb_xnbf[:, sl, :],
                              [t_x2] + xn_cons[sl])
        xn_cons[sl] = []
        x2_cons[sl].append(t_xn2)
        for kd in range(16):
            tm = Sc.op("tensor", lambda e, kd=kd, sl=sl: e.transpose(
                bkTB.t[:, 0:128], sb_xnbf[:, sl, kd * 128:(kd + 1) * 128], sb_idbf[:]),
                waits=bkTB.wr_waits() + [t_xn2])
            bkTB.set_writer(tm)
            te = Sc.op("vector", lambda e, kd=kd, cch=cch: e.tensor_copy(
                sb_xn2T[:, kd, cch * 128:(cch + 1) * 128], bkTB.t[:, 0:128]),
                waits=bkTB.rd_waits())
            bkTB.add_reader(te)
            if kd == 15:
                xn_cons[sl].append(tm)
                xn2T_last = [te]
    BAR3 = Sc.barrier(streams=("x2spw", "aoscw"))

    ffn_dmas = []
    gsl_cons = {i: [] for i in range(4)}
    evf_guard2 = []
    for hf in range(2):
        hoff = hf * 512
        h_last = []
        for jj in range(64):
            sl = jj % 2
            jg, ju = jj, jj + 64
            t_sg = Sc.dma("gslab", lambda e, jg=jg, sl=sl: e.dma_start(
                out=sb_wsl[:, sl * 2, :, :],
                in_=whatT_gu[:, :, jg * 128:(jg + 1) * 128].rearrange("k p f -> p k f")),
                waits=gsl_cons[sl * 2] + [wtok[("gu", jg, kk)] for kk in range(16)] +
                (BAR3 if jj < 2 and hf == 0 else []))
            t_su = Sc.dma("gslab", lambda e, ju=ju, sl=sl: e.dma_start(
                out=sb_wsl[:, sl * 2 + 1, :, :],
                in_=whatT_gu[:, :, ju * 128:(ju + 1) * 128].rearrange("k p f -> p k f")),
                waits=gsl_cons[sl * 2 + 1] + [wtok[("gu", ju, kk)] for kk in range(16)])
            gsl_cons[sl * 2] = []
            gsl_cons[sl * 2 + 1] = []
            bg = bkA0 if sl == 0 else bkA1
            bu = bkB0 if sl == 0 else bkB1
            w0 = bg.wr_waits() + [t_sg] + (xn2T_last if jj == 0 else [])
            lastg = None
            for kd in range(16):
                lastg = Sc.op("tensor", lambda e, kd=kd, sl=sl, hoff=hoff, b=bg: e.matmul(
                    b.t[:, 0:512], sb_wsl[:, sl * 2, kd, :],
                    sb_xn2T[:, kd, hoff:hoff + 512],
                    start=(kd == 0), stop=(kd == 15)),
                    waits=w0 if kd == 0 else [], sig=(kd == 15))
            bg.set_writer(lastg)
            gsl_cons[sl * 2].append(lastg)
            w0 = bu.wr_waits() + [t_su]
            lastu = None
            for kd in range(16):
                lastu = Sc.op("tensor", lambda e, kd=kd, sl=sl, hoff=hoff, b=bu: e.matmul(
                    b.t[:, 0:512], sb_wsl[:, sl * 2 + 1, kd, :],
                    sb_xn2T[:, kd, hoff:hoff + 512],
                    start=(kd == 0), stop=(kd == 15)),
                    waits=w0 if kd == 0 else [], sig=(kd == 15))
            bu.set_writer(lastu)
            gsl_cons[sl * 2 + 1].append(lastu)
            t_si = Sc.op("scalar", lambda e, b=bg: e.activation(
                sb_evf[:], b.t[:, 0:512], AF.Silu), waits=bg.rd_waits() + evf_guard2)
            bg.add_reader(t_si)
            t_h = Sc.op("vector", lambda e, jj=jj, b=bu: e.tensor_tensor(
                h_tile(jj), b.t[:, 0:512], sb_evf[:], ALU.mult),
                waits=bu.rd_waits() + [t_si] + (BAR3 if jj < 2 and hf == 0 else []))
            bu.add_reader(t_h)
            evf_guard2 = [t_h]
            h_last = [t_h]
        for m2 in range(16):
            sl = m2 % 2
            osl = 1 - sl
            pq = [wtok[("p", m2 // 8, hb)] for hb in range(64)] if hf == 0 else []
            t_s1 = Sc.dma("pslab", lambda e, m2=m2, sl=sl: e.dma_start(
                out=sb_wsl[:, sl * 2:sl * 2 + 2, :, :].rearrange("p a k n -> p (a k) n"),
                in_=whatT_p[m2, 0:32].rearrange("k p f -> p k f")),
                waits=gsl_cons[sl * 2] + gsl_cons[sl * 2 + 1] + pq)
            t_s2 = Sc.dma("pslab", lambda e, m2=m2, osl=osl: e.dma_start(
                out=sb_wsl[:, osl * 2:osl * 2 + 2, :, :].rearrange("p a k n -> p (a k) n"),
                in_=whatT_p[m2, 32:64].rearrange("k p f -> p k f")),
                waits=gsl_cons[osl * 2] + gsl_cons[osl * 2 + 1])
            for i4 in range(4):
                gsl_cons[i4] = []
            bank = bkA0 if sl == 0 else bkA1
            w0 = bank.wr_waits() + [t_s1, t_s2] + h_last
            last = None
            for kd in range(64):
                wsrc = (sb_wsl[:, sl * 2 + kd // 16, kd % 16, :] if kd < 32 else
                        sb_wsl[:, osl * 2 + (kd - 32) // 16, (kd - 32) % 16, :])
                last = Sc.op("tensor", lambda e, kd=kd, wsrc=wsrc, b=bank: e.matmul(
                    b.t[:, 0:512], wsrc, h_tile(kd), start=(kd == 0), stop=(kd == 63)),
                    waits=w0 if kd == 0 else [], sig=(kd == 63))
            bank.set_writer(last)
            for i4 in range(4):
                gsl_cons[i4].append(last)
            t_ev = Sc.op("scalar", lambda e, b=bank: e.activation(
                sb_evf[:], b.t[:, 0:512], AF.Identity),
                waits=bank.rd_waits() + evf_guard2)
            bank.add_reader(t_ev)
            t_evb = Sc.op("vector", lambda e: e.tensor_copy(
                sb_dump[:, 0:512], sb_evf[:]), waits=[t_ev] + dump_guard)
            evf_guard2 = [t_evb]
            dump_guard = []
            for p4 in range(4):
                cch = hf * 4 + p4
                tm = Sc.op("tensor", lambda e, p4=p4: e.transpose(
                    bkTB.t[:, 0:128], sb_dump[:, p4 * 128:(p4 + 1) * 128], sb_idbf[:]),
                    waits=bkTB.wr_waits() + [t_evb])
                bkTB.set_writer(tm)
                dump_guard.append(tm)
                te = Sc.op("vector", lambda e: e.tensor_copy(
                    sb_mtmp[:, 0, :], bkTB.t[:, 0:128]), waits=bkTB.rd_waits())
                bkTB.add_reader(te)
                td = Sc.dma("ffnw", lambda e, cch=cch, m2=m2: e.dma_start(
                    out=ffnsc[cch, :, m2 * 128:(m2 + 1) * 128], in_=sb_mtmp[:, 0, :]),
                    waits=[te])
                bkTB.add_reader(td)
                ffn_dmas.append(td)

    BAR4 = Sc.barrier(streams=("whatw",))

    out_dmas = []
    fo_cons = []
    for cch in range(NCH):
        sl = cch % 2
        t_f = Sc.dma("x", lambda e, cch=cch: e.dma_start(
            out=sb_fo2[:], in_=ffnsc[cch, :, :]),
            waits=ffn_dmas[-2:] + fo_cons + (BAR4 if cch < 2 else []))
        t_x2l = Sc.dma("x", lambda e, cch=cch, sl=sl: e.dma_start(
            out=sb_x2[:, sl, :], in_=x2sp[cch, :, :]),
            waits=x2_cons[sl] + (BAR4 if cch < 2 else []) + x2sp_toks[-1:])
        x2_cons[sl] = []
        _, st, t7 = emit_ln(sb_fo2[:], None, [t_f])
        t_nf = Sc.op("vector", lambda e, st=st: e.tensor_scalar(
            sb_fo[:], sb_fo2[:], st[:, 7:8], st[:, 8:9], ALU.mult, ALU.add),
            waits=[t7] + (BAR4 if cch == 0 else []))
        fo_cons = [t_nf]
        t_o = Sc.op("vector", lambda e, sl=sl: e.tensor_tensor(
            sb_outst[:, sl, :], sb_fo[:], sb_x2[:, sl, :], ALU.add),
            waits=[t_nf, t_x2l] + xc_cons[sl])
        xc_cons[sl] = []
        x2_cons[sl].append(t_o)
        td = Sc.dma("outw", lambda e, cch=cch, sl=sl: e.dma_start(
            out=out[cch * 128:(cch + 1) * 128, :], in_=sb_outst[:, sl, :]),
            waits=[t_o])
        xc_cons[sl].append(td)
        out_dmas.append(td)

    Sc.op("sync", lambda e: e.nop(), waits=[out_dmas[-1]], sig=False)
    Sc.build()
    n_inst = sum(len(v) for v in Sc.items.values())
    print(f"program instructions (approx): {n_inst}")
    return nc


def _host_inputs(inputs, core):
    b, qv = core // 4, core % 4
    start = qv * TOK
    x = np.asarray(inputs["L0_attn_hidden_states"], dtype=np.float32)[b]
    xb_ = np.ascontiguousarray(x)
    halo = np.zeros((128, D), np.float32) if qv == 0 else x[start - 128:start]
    xo_ = np.ascontiguousarray(np.concatenate([halo, x[start:start + TOK]], axis=0))
    pos = np.maximum(np.arange(start - 128, start + TOK), 0).astype(np.float32)
    inv_freq = 1.0 / 10000.0 ** (np.arange(32, dtype=np.float32) * 2.0 / 64.0)
    ang = pos[None, :] * inv_freq[:, None]
    cos, sin = np.cos(ang), np.sin(ang)
    rtcc_ = np.ascontiguousarray(np.concatenate([cos, cos], 0).astype(np.float32))
    rtns_ = np.ascontiguousarray(np.concatenate([-sin, sin], 0).astype(np.float32))
    bend = (np.arange(C) * R + (R - 1)).astype(np.float32)
    angc = bend[None, :] * inv_freq[:, None]
    cc_, sc_ = np.cos(angc), np.sin(angc)
    cktcc_ = np.ascontiguousarray(np.concatenate([cc_, cc_], 0).astype(np.float32))
    cktns_ = np.ascontiguousarray(np.concatenate([-sc_, sc_], 0).astype(np.float32))
    cmask = np.zeros((NCH, 128, C), np.float32)
    for cch in range(NCH):
        posr = start + cch * 128 + np.arange(128)
        cmask[cch] = np.where(bend[None, :] <= posr[:, None], 0.0, -1e9)
    lmask = np.zeros((2, 128, 256), np.float32)
    i_ = np.arange(128)[:, None]
    j_ = np.arange(256)[None, :]
    band = (j_ >= i_) & (j_ <= i_ + 128)
    lmask[1] = np.where(band, 0.0, -1e9)
    lmask[0] = np.where(band & (j_ >= 128), 0.0, -1e9) if qv == 0 else lmask[1]
    g = lambda k: np.ascontiguousarray(np.asarray(inputs[k], dtype=np.float32))
    return {
        "xb": xb_, "xo": xo_, "rtcc": rtcc_, "rtns": rtns_,
        "cktcc": cktcc_, "cktns": cktns_, "cm": cmask.astype(bfdt),
        "lm": lmask.astype(bfdt),
        "Wq": g("L0_attn_Wq"), "Wk": g("L0_attn_Wk"), "Wv": g("L0_attn_Wv"),
        "Wlk": g("L0_attn_Wlk"), "Wlv": g("L0_attn_Wlv"), "Wo": g("L0_attn_Wo"),
        "Wgu": g("L0_ffn_w_gate_up"), "Wp": g("L0_ffn_w_proj"),
        "bqT": np.ascontiguousarray(g("L0_attn_bq").reshape(16, 128).T),
        "blkT": np.ascontiguousarray(g("L0_attn_blk").reshape(16, 128).T),
        "blvT": np.ascontiguousarray(g("L0_attn_blv").reshape(16, 128).T),
        "boT": np.ascontiguousarray(g("L0_attn_bo").reshape(16, 128).T),
        "bkc": g("L0_attn_bk").reshape(128, 1),
        "bvc": g("L0_attn_bv").reshape(128, 1),
        "bcc": np.ascontiguousarray(np.broadcast_to(
            g("L0_attn_bc").reshape(1, 1), (128, 1))),
        "qnc": g("L0_attn_qn_w").reshape(128, 1),
        "knc": g("L0_attn_kn_w").reshape(128, 1),
        "sinkTf": np.ascontiguousarray(g("L0_attn_sink_k").T),
        "sinkvRf": g("L0_attn_sink_v").reshape(1, NH * HD),
        "wcrepf": np.ascontiguousarray(np.broadcast_to(
            g("L0_attn_Wc").reshape(1, D), (128, D))),
    }


_CACHE = {}


def kernel(**inputs):
    if "nc" not in _CACHE:
        _CACHE["nc"] = build_program()
    nc = _CACHE["nc"]
    in_maps = [_host_inputs(inputs, c) for c in range(8)]
    res = run_bass_kernel_spmd(nc, in_maps, core_ids=list(range(8)))
    outs = [np.asarray(res.results[c]["out"], dtype=np.float32) for c in range(8)]
    return np.stack([np.concatenate(outs[0:4], 0), np.concatenate(outs[4:8], 0)], 0)


if __name__ == "__main__":
    build_program()
    print("BUILD OK")


# version 15
# speedup vs baseline: 1.0518x; 1.0267x over previous
import math

import numpy as np
import ml_dtypes
import concourse.bass as bass
import concourse.mybir as mybir
from concourse.bass_utils import run_bass_kernel_spmd

F32 = mybir.dt.float32
BF16 = mybir.dt.bfloat16
U16 = mybir.dt.uint16
AF = mybir.ActivationFunctionType
ALU = mybir.AluOpType
AX = mybir.AxisListType
bfdt = ml_dtypes.bfloat16

B, S, D = 2, 4096, 2048
NH, HD = 16, 128
R, C, WIN = 16, 256, 128
HID2, H = 16384, 8192
TOK, TEXT, NCH, PFX = 1024, 1152, 8, 32
EPS_LN, EPS_RMS = 1e-5, 1e-6
SCALE = 1.0 / math.sqrt(HD)
SB_BASE = 16512
SB_BUDGET = 229376


class Sched:
    ENGINES = ("sync", "gpsimd", "scalar", "vector", "tensor")
    STREAMS = {
        "setup_s": ("sync", 1), "setup_p": ("gpsimd", 1),
        "x": ("sync", 2), "entw": ("sync", 2), "stk": ("sync", 1),
        "whatw": ("gpsimd", 2),
        "aoscw": ("sync", 2), "ffnw": ("sync", 2), "x2spw": ("sync", 1),
        "outw": ("sync", 1),
        "aslab": ("sync", 2), "qcast_gu": ("gpsimd", 2), "qcast_p": ("gpsimd", 2),
        "qcast_a": ("gpsimd", 2),
        "wt": ("scalar", 3), "gslab": ("scalar", 2), "pslab": ("scalar", 2),
    }

    def __init__(self, nc):
        self.nc = nc
        self.items = {e: [] for e in self.ENGINES}
        self.csem = {e: nc.alloc_semaphore(f"cs_{e}") for e in self.ENGINES}
        self.ccount = {e: 0 for e in self.ENGINES}
        self.qsem = {}
        self.qcount = {}
        self.qrr = {}
        for s, (q, ns) in self.STREAMS.items():
            for i in range(ns):
                self.qsem[(s, i)] = nc.alloc_semaphore(f"dq_{s}{i}")
                self.qcount[(s, i)] = 0
            self.qrr[s] = 0

    def op(self, eng, fn, waits=(), sig=True):
        tok = None
        if sig:
            self.ccount[eng] += 1
            tok = (self.csem[eng], self.ccount[eng])
        wl = [w for w in waits if w is not None]
        self.items[eng].append(("op", fn, wl, tok))
        return tok

    def dma(self, stream, fn, waits=()):
        q, ns = self.STREAMS[stream]
        i = self.qrr[stream]
        self.qrr[stream] = (i + 1) % ns
        key = (stream, i)
        prev = (self.qsem[key], self.qcount[key]) if self.qcount[key] else None
        self.qcount[key] += 16
        tok = (self.qsem[key], self.qcount[key])
        wl = [w for w in list(waits) + [prev] if w is not None]
        self.items[q].append(("dma", fn, wl, tok))
        return tok

    def barrier(self, streams=()):
        toks = []
        for e in ("gpsimd", "scalar", "vector", "tensor"):
            pv = (self.csem[e], self.ccount[e])
            toks.append(self.op(e, lambda eng: eng.nop(), waits=[pv], sig=True))
        for s in streams:
            for i in range(self.STREAMS[s][1]):
                if self.qcount[(s, i)]:
                    toks.append((self.qsem[(s, i)], self.qcount[(s, i)]))
        return toks

    def build(self):
        nc = self.nc
        with nc.Block() as block:
            def make(eng_name):
                def body(eng):
                    seen = {}
                    for kind, fn, waits, tok in self.items[eng_name]:
                        for (sem, val) in waits:
                            key = id(sem)
                            if seen.get(key, 0) < val:
                                eng.wait_ge(sem, val)
                                seen[key] = val
                        inst = fn(eng)
                        if tok is not None:
                            inst.then_inc(tok[0], 16 if kind == "dma" else 1)
                return body
            block.sync(make("sync"))
            block.gpsimd(make("gpsimd"))
            block.scalar(make("scalar"))
            block.vector(make("vector"))
            block.tensor(make("tensor"))


class Bank:
    def __init__(self, tile):
        self.t = tile
        self.readers = []
        self.writer = None

    def wr_waits(self):
        w = list(self.readers) + ([self.writer] if self.writer else [])
        self.readers = []
        return w

    def set_writer(self, tok):
        assert tok is not None
        self.writer = tok

    def rd_waits(self):
        return [self.writer]

    def add_reader(self, tok):
        self.readers.append(tok)


def build_program():
    nc = bass.Bass("TRN2", target_bir_lowering=False)
    Sc = Sched(nc)

    def ext(name, shape, dtype=F32):
        return nc.dram_tensor(name, list(shape), dtype, kind="ExternalInput").ap()

    xb = ext("xb", [S, D])
    xo = ext("xo", [TEXT, D])
    rtcc = ext("rtcc", [64, TEXT]); rtns = ext("rtns", [64, TEXT])
    cktcc = ext("cktcc", [64, C]); cktns = ext("cktns", [64, C])
    cmv = ext("cm", [NCH, 128, C], BF16); lmv = ext("lm", [2, 128, 256], BF16)
    Wq = ext("Wq", [D, D]); Wk = ext("Wk", [D, HD]); Wv = ext("Wv", [D, HD])
    Wlk = ext("Wlk", [D, D]); Wlv = ext("Wlv", [D, D]); Wo = ext("Wo", [D, D])
    Wgu = ext("Wgu", [HID2, D]); Wp = ext("Wp", [D, H])
    bqT = ext("bqT", [128, 16]); blkT = ext("blkT", [128, 16])
    blvT = ext("blvT", [128, 16]); boT = ext("boT", [128, 16])
    bkc = ext("bkc", [128, 1]); bvc = ext("bvc", [128, 1]); bcc = ext("bcc", [128, 1])
    qnc = ext("qnc", [128, 1]); knc = ext("knc", [128, 1])
    sinkTf = ext("sinkTf", [128, 16]); sinkvRf = ext("sinkvRf", [1, NH * HD])
    wcrepf = ext("wcrepf", [128, D])

    out = nc.dram_tensor("out", [TOK, D], F32, kind="ExternalOutput").ap()

    wbf_gu = nc.dram_tensor("wbf_gu", [HID2, D], BF16, kind="Internal").ap()
    wbf_q = nc.dram_tensor("wbf_q", [D, D], BF16, kind="Internal").ap()
    wbf_lk = nc.dram_tensor("wbf_lk", [D, D], BF16, kind="Internal").ap()
    wbf_lv = nc.dram_tensor("wbf_lv", [D, D], BF16, kind="Internal").ap()
    wbf_o = nc.dram_tensor("wbf_o", [D, D], BF16, kind="Internal").ap()
    wbf_p = nc.dram_tensor("wbf_p", [D, H], BF16, kind="Internal").ap()
    whatT_gu = nc.dram_tensor("whatT_gu", [16, 128, HID2], BF16, kind="Internal").ap()
    whatT_p = nc.dram_tensor("whatT_p", [16, 64, 128, 128], BF16, kind="Internal").ap()
    x2sp = nc.dram_tensor("x2sp", [NCH, 128, D], F32, kind="Internal").ap()
    aosc = nc.dram_tensor("aosc", [NCH, 128, D], F32, kind="Internal").ap()
    ffnsc = nc.dram_tensor("ffnsc", [NCH, 128, D], F32, kind="Internal").ap()
    entsc_d = nc.dram_tensor("entsc_d", [PFX, 8, D], F32, kind="Internal").ap()

    idn_bf_c = nc.inline_tensor(np.eye(128, dtype=bfdt), name="idn_bf").ap()
    ones_col_c = nc.inline_tensor(np.ones((128, 1), dtype=bfdt), name="ones_col").ap()
    ones_row_c = nc.inline_tensor(np.ones((1, 128), dtype=np.float32), name="ones_row").ap()
    patt = np.zeros((128, 8), dtype=np.float32)
    for i in range(128):
        patt[i, i // 16] = 1.0
    patt8_c = nc.inline_tensor(patt, name="patt8").ap()
    patt8T_c = nc.inline_tensor(np.ascontiguousarray(patt.T), name="patt8T").ap()
    patt8b_c = nc.inline_tensor(patt.astype(bfdt), name="patt8b").ap()
    i2 = np.zeros((128, 2), dtype=bfdt); i2[:64, 0] = 1.0 / 128; i2[64:, 1] = 1.0 / 128
    I2_c = nc.inline_tensor(i2, name="I2").ap()
    e2 = np.zeros((2, 128), dtype=bfdt); e2[0, :64] = 1; e2[1, 64:] = 1
    E2_c = nc.inline_tensor(e2, name="E2").ap()
    sw = np.zeros((64, 64), dtype=bfdt)
    for i in range(64):
        sw[i, (i + 32) % 64] = 1
    SW64_c = nc.inline_tensor(sw, name="SW64").ap()
    cstv = np.zeros((128, 4), dtype=np.float32)
    cstv[:, 0] = EPS_LN; cstv[:, 1] = EPS_RMS; cstv[:, 2] = SCALE; cstv[:, 3] = 0.5
    cst_c = nc.inline_tensor(cstv, name="cstc").ap()

    cur = [SB_BASE]

    def sba(name, shape, dtype, at=None, track=True):
        if at is None:
            cur[0] = (cur[0] + 63) // 64 * 64
            at = cur[0]
            if track:
                cur[0] = at + int(np.prod(shape[1:])) * mybir.dt.size(dtype)
                assert cur[0] <= SB_BUDGET, f"SBUF overflow at {name}: {cur[0]}"
        return nc.alloc_sbuf_tensor_at(name, list(shape), dtype, offset=at)

    sb_idbf = sba("sb_idbf", [128, 128], BF16)
    sb_onesc = sba("sb_onesc", [128, 1], BF16)
    sb_onesr = sba("sb_onesr", [1, 128], F32)
    sb_patt8 = sba("sb_patt8", [128, 8], F32)
    sb_patt8T = sba("sb_patt8T", [8, 128], F32)
    sb_patt8b = sba("sb_patt8b", [128, 8], BF16)
    sb_I2 = sba("sb_I2", [128, 2], BF16)
    sb_E2 = sba("sb_E2", [2, 128], BF16)
    sb_SW = sba("sb_SW", [64, 64], BF16)
    sb_bqT = sba("sb_bqT", [128, 16], F32)
    sb_blkT = sba("sb_blkT", [128, 16], F32)
    sb_blvT = sba("sb_blvT", [128, 16], F32)
    sb_boT = sba("sb_boT", [128, 16], F32)
    sb_cols = sba("sb_cols", [128, 5], F32)
    sb_sinkT = sba("sb_sinkT", [128, 16], BF16)
    sb_sinkvR = sba("sb_sinkvR", [1, NH * HD], BF16)
    sb_rtcc = sba("sb_rtcc", [64, TEXT], BF16)
    sb_rtns = sba("sb_rtns", [64, TEXT], BF16)
    sb_ckcc = sba("sb_ckcc", [64, C], BF16)
    sb_ckns = sba("sb_ckns", [64, C], BF16)
    sb_ckr = sba("sb_ckr", [128, C], BF16)
    sb_cvT = sba("sb_cvT", [128, 2, 128], BF16)
    sb_cst = sba("sb_cst", [128, 4], F32)
    sb_cmask = sba("sb_cmask", [128, NCH, C], BF16)
    sb_lmask = sba("sb_lmask", [128, 2, 256], BF16)

    r1 = (cur[0] + 63) // 64 * 64
    cur[0] = r1
    sb_xnT = sba("sb_xnT", [128, 16, TEXT], BF16)
    r1b = (cur[0] + 63) // 64 * 64
    cur[0] = r1b
    sb_merged = sba("sb_merged", [128, 16, TOK], BF16)
    r1_end = cur[0]
    sb_wcrep = sba("sb_wcrep", [128, D], BF16, at=r1)
    sb_x2 = sba("sb_x2", [128, 2, D], F32, at=r1)
    sb_outst = sba("sb_outst", [128, 2, D], F32, at=r1 + 16384)
    sb_fo = sba("sb_fo", [128, D], F32, at=r1b)
    sb_fo2 = sba("sb_fo2", [128, D], F32, at=r1b + 8192)
    sb_ent = sba("sb_ent", [128, 2, D], F32, at=r1b)
    sb_entbf = sba("sb_entbf", [128, 2, D], BF16, at=r1b + 16384)
    sb_entsc = sba("sb_entsc", [8, D], F32, at=r1b + 24576)
    sb_hh = sba("sb_hh", [128, 64, 512], BF16, at=r1)

    def h_tile(j):
        return sb_hh[:, j, :]

    r2 = (r1_end + 63) // 64 * 64
    cur[0] = r2
    sb_xn2T = sba("sb_xn2T", [128, 16, TOK], BF16)
    r2_end = cur[0]
    a = r2
    sb_qf = sba("sb_qf", [128, TEXT], F32, at=a); a += TEXT * 4
    sb_rsqb = sba("sb_rsqb", [128, TEXT], F32, at=a); a += TEXT * 4
    sb_row = sba("sb_row", [1, TEXT], F32, at=a); a += TEXT * 4
    sb_qs = sba("sb_qs", [128, TEXT], BF16, at=a); a += TEXT * 2
    sb_usw = sba("sb_usw", [64, TEXT], BF16, at=a); a += TEXT * 2
    sb_u = sba("sb_u", [64, TEXT], BF16, at=a); a += TEXT * 2
    sb_v = sba("sb_v", [64, TEXT], BF16, at=a); a += TEXT * 2
    sb_lv = sba("sb_lv", [128, TEXT], BF16, at=a); a += TEXT * 2
    sb_qr2 = sba("sb_qr2", [128, 2, TOK], BF16, at=a); a += 2 * TOK * 2
    sb_mq = sba("sb_mq", [128, 1024], BF16, at=a); a += 1024 * 2
    assert a <= r2_end, (a, r2_end)

    cur[0] = r2_end
    sb_xc = sba("sb_xc", [128, 2, D], F32)
    sb_xnbf = sba("sb_xnbf", [128, 2, D], BF16)
    sb_dump = sba("sb_dump", [128, D], BF16)
    sb_stat = sba("sb_stat", [128, 26], F32)
    sc_at = (cur[0] + 63) // 64 * 64
    sb_sc = sba("sb_sc", [128, 2, 512], F32)
    sb_evf = sba("sb_evf", [128, 512], F32, at=sc_at)
    sb_esc = sba("sb_esc", [128, 2, 512], BF16)
    sb_attbf = sba("sb_attbf", [128, 2, 512], BF16)
    sb_attT = sba("sb_attT", [128, 2, 512], BF16)
    sb_attS = sba("sb_attS", [1, 2, 128], BF16)
    sb_mtmp = sba("sb_mtmp", [128, 2, 128], F32)
    sb_snk = sba("sb_snk", [128, 2, 20], F32)
    sb_lkr2 = sba("sb_lkr2", [128, 2, TEXT], BF16)
    sb_lvT2 = sba("sb_lvT2", [128, 2, 9, 128], BF16)
    sb_ask = sba("sb_ask", [128, 2], BF16)
    sb_ckf = sba("sb_ckf", [128, C], F32, at=r2)
    sb_ckq = sba("sb_ckq", [128, C], BF16, at=r2 + 1024)
    sb_wsl = sba("sb_wsl", [128, 4, 16, 128], BF16)
    sb_wt = sba("sb_wt", [128, 2, 1024], BF16)
    sb_absq = sba("sb_absq", [128, 2, 1024], BF16)
    sb_shs = sba("sb_shs", [2, 4, 512], BF16)
    sb_shf = sba("sb_shf", [128, 2, 1024], BF16)
    sb_sgq = sba("sb_sgq", [128, 1024], U16)
    sb_what = sba("sb_what", [128, 3, 1024], BF16)

    assert cur[0] <= SB_BUDGET, cur[0]

    bkA0 = Bank(nc.alloc_psum_tensor("pbA0", [128, 512], F32))
    bkA1 = Bank(nc.alloc_psum_tensor("pbA1", [128, 512], F32))
    bkB0 = Bank(nc.alloc_psum_tensor("pbB0", [128, 512], F32))
    bkB1 = Bank(nc.alloc_psum_tensor("pbB1", [128, 512], F32))
    bkTB = Bank(nc.alloc_psum_tensor("ptrb", [128, 1024], BF16))
    bkTF = Bank(nc.alloc_psum_tensor("ptrf", [128, 512], F32))
    bkSC = Bank(nc.alloc_psum_tensor("psc", [128, 512], F32))
    bkMS = Bank(nc.alloc_psum_tensor("pmisc", [128, 512], F32))

    for dst, src in ((sb_idbf, idn_bf_c), (sb_onesc, ones_col_c), (sb_onesr, ones_row_c),
                     (sb_patt8, patt8_c), (sb_patt8T, patt8T_c), (sb_patt8b, patt8b_c),
                     (sb_I2, I2_c), (sb_E2, E2_c), (sb_SW, SW64_c), (sb_cst, cst_c),
                     (sb_bqT, bqT), (sb_blkT, blkT), (sb_blvT, blvT), (sb_boT, boT)):
        Sc.dma("setup_s", lambda e, d=dst, s=src: e.dma_start(out=d[:], in_=s))
    for i, src in enumerate((bkc, bvc, bcc, qnc, knc)):
        Sc.dma("setup_s", lambda e, i=i, s=src: e.dma_start(out=sb_cols[:, i:i + 1], in_=s))
    Sc.dma("setup_s", lambda e: e.dma_start(out=sb_cmask[:], in_=cmv.rearrange("c p n -> p c n")))
    Sc.dma("setup_s", lambda e: e.dma_start(out=sb_lmask[:], in_=lmv.rearrange("c p n -> p c n")))
    for dst, src in ((sb_sinkT, sinkTf), (sb_sinkvR, sinkvRf), (sb_wcrep, wcrepf),
                     (sb_rtcc, rtcc), (sb_rtns, rtns), (sb_ckcc, cktcc), (sb_ckns, cktns)):
        Sc.dma("setup_p", lambda e, d=dst, s=src: e.dma_start(out=d[:], in_=s))
    Sc.dma("setup_p", lambda e: e.dma_start(
        out=sb_wsl[:, 2, :, :], in_=Wk.rearrange("(k p) n -> p k n", p=128)))
    Sc.dma("setup_p", lambda e: e.dma_start(
        out=sb_wsl[:, 3, :, :], in_=Wv.rearrange("(k p) n -> p k n", p=128)))
    qcast_gu = [Sc.dma("qcast_gu", lambda e, rr=rr: e.dma_start(
        out=wbf_gu[rr * 4096:(rr + 1) * 4096, :], in_=Wgu[rr * 4096:(rr + 1) * 4096, :]))
        for rr in range(4)]
    qcast_p = [Sc.dma("qcast_p", lambda e, rr=rr: e.dma_start(
        out=wbf_p[rr * 512:(rr + 1) * 512, :], in_=Wp[rr * 512:(rr + 1) * 512, :]))
        for rr in range(4)]
    qcast_a = {}
    for nm, (dst, srcw) in {"q": (wbf_q, Wq), "lk": (wbf_lk, Wlk),
                            "lv": (wbf_lv, Wlv), "o": (wbf_o, Wo)}.items():
        qcast_a[nm] = Sc.dma("qcast_a", lambda e, d=dst, s=srcw: e.dma_start(
            out=d, in_=s))
    BAR0 = Sc.barrier(streams=("setup_s", "setup_p"))

    qslot = [0]
    qcons = [[], []]
    gsring = [[], [], [], []]
    shf_cons = [[], []]
    qstate = {}
    wtok = {}
    bguard = [[], []]
    wslot = [0]
    whatg = [[], [], []]

    def emit_qA(which, k, hb):
        sl = qslot[0]; qslot[0] ^= 1
        if which == "gu":
            src = wbf_gu[hb * 1024:(hb + 1) * 1024, k * 128:(k + 1) * 128]
            cast_tok = [qcast_gu[hb // 4]]
        else:
            src = wbf_p[k * 1024:(k + 1) * 1024, hb * 128:(hb + 1) * 128]
            cast_tok = [qcast_p[2 * k], qcast_p[2 * k + 1]]
        t_tr = Sc.dma("wt", lambda e, src=src, sl=sl: e.dma_start(
            out=sb_wt[:, sl, :], in_=src, transpose=True),
            waits=cast_tok + qcons[sl] + BAR0)
        qcons[sl] = []
        t_abs = Sc.op("vector", lambda e, sl=sl: e.tensor_scalar(
            sb_absq[:, sl, :].bitcast(U16), sb_wt[:, sl, :].bitcast(U16), 0x7FFF,
            None, ALU.bitwise_and), waits=[t_tr])
        qcons[sl].append(t_abs)
        tshf2 = []
        for q2 in range(2):
            fs = q2 * 512
            ssl = sl * 2 + q2
            tm = Sc.op("tensor", lambda e, fs=fs, sl=sl: e.matmul(
                bkMS.t[0:2, 0:512], sb_I2[:], sb_absq[:, sl, fs:fs + 512],
                start=True, stop=True), waits=bkMS.wr_waits() + [t_abs])
            bkMS.set_writer(tm)
            tse = Sc.op("scalar", lambda e, ssl=ssl: e.activation(
                sb_shs[:, ssl, :], bkMS.t[0:2, 0:512], AF.Identity),
                waits=bkMS.rd_waits() + gsring[ssl])
            bkMS.add_reader(tse)
            tex = Sc.op("tensor", lambda e, ssl=ssl: e.matmul(
                bkTF.t[:, 0:512], sb_E2[:], sb_shs[:, ssl, :], start=True, stop=True),
                waits=bkTF.wr_waits() + [tse])
            gsring[ssl] = [tex]
            bkTF.set_writer(tex)
            tshf = Sc.op("scalar", lambda e, sl=sl, fs=fs: e.activation(
                sb_shf[:, sl, fs:fs + 512], bkTF.t[:, 0:512], AF.Identity),
                waits=bkTF.rd_waits() + (shf_cons[sl] if q2 == 0 else [])),
            shf_cons[sl] = [] if q2 == 0 else shf_cons[sl]
            tshf = tshf[0]
            bkTF.add_reader(tshf)
            tshf2.append(tshf)
        qstate[sl] = (which, k, hb, tshf2)

    def emit_qB(sl):
        which, k, hb, tshf2 = qstate[sl]
        w3 = wslot[0]
        wslot[0] = (w3 + 1) % 3
        tmq = Sc.op("vector", lambda e, sl=sl: e.tensor_tensor(
            sb_mq[:], sb_absq[:, sl, :], sb_shf[:, sl, :], ALU.is_gt),
            waits=tshf2 + bguard[0])
        tsg = Sc.op("vector", lambda e, sl=sl: e.tensor_scalar(
            sb_sgq[:], sb_wt[:, sl, :].bitcast(U16), 0x8000, None,
            ALU.bitwise_and), waits=bguard[1])
        tss = Sc.op("vector", lambda e, sl=sl: e.tensor_tensor(
            sb_absq[:, sl, :].bitcast(U16), sb_sgq[:], sb_shf[:, sl, :].bitcast(U16),
            ALU.bitwise_or), waits=[tsg, tmq])
        shf_cons[sl] = [tmq, tss]
        twh = Sc.op("vector", lambda e, sl=sl, w3=w3: e.scalar_tensor_tensor(
            sb_what[:, w3, :], sb_mq[:], 2.0, sb_absq[:, sl, :],
            ALU.mult, ALU.mult), waits=[tss] + whatg[w3])
        bguard[0] = [twh]
        bguard[1] = [tss]
        outs = [twh]
        qcons[sl].extend([tmq, tsg, tss, twh])
        if which == "gu":
            dst = whatT_gu[k, :, hb * 1024:(hb + 1) * 1024]
        else:
            dst = whatT_p[k * 8:(k + 1) * 8, hb, :, :].rearrange("m p f -> p m f")
        t_out = Sc.dma("whatw", lambda e, dst=dst, w3=w3: e.dma_start(
            out=dst, in_=sb_what[:, w3, :].rearrange("p (j f) -> p j f", f=128)),
            waits=outs)
        whatg[w3] = [t_out]
        if which == "gu":
            for j in range(hb * 8, hb * 8 + 8):
                wtok[("gu", j, k)] = t_out
        else:
            wtok[("p", k, hb)] = t_out

    lnst = {"sl": 0, "cons": [[], []]}
    dump_guard = []

    def emit_ln(x_ap, out_bf_ap, deps):
        nonlocal dump_guard
        sl = lnst["sl"]; lnst["sl"] ^= 1
        st = sb_stat[:, sl * 10:sl * 10 + 10]
        cons = lnst["cons"][sl]
        t_sum = Sc.op("vector", lambda e: e.reduce_sum(st[:, 0:1], x_ap, AX.X),
                      waits=deps + cons)
        t_sq = Sc.op("scalar", lambda e: e.activation(
            sb_dump[:], x_ap, AF.Square, accum_out=st[:, 1:2]),
            waits=deps + cons + dump_guard)
        dump_guard = [t_sq]
        lnst["cons"][sl] = []
        cons = lnst["cons"][sl]
        t1 = Sc.op("vector", lambda e: e.tensor_scalar(
            st[:, 2:3], st[:, 0:1], 1.0 / D, None, ALU.mult), waits=[t_sum])
        t2 = Sc.op("vector", lambda e: e.tensor_scalar(
            st[:, 3:4], st[:, 1:2], 1.0 / D, None, ALU.mult), waits=[t_sq])
        t3 = Sc.op("vector", lambda e: e.tensor_tensor(
            st[:, 4:5], st[:, 2:3], st[:, 2:3], ALU.mult), waits=[t1])
        t4 = Sc.op("vector", lambda e: e.tensor_tensor(
            st[:, 5:6], st[:, 3:4], st[:, 4:5], ALU.subtract), waits=[t2, t3])
        t5 = Sc.op("scalar", lambda e: e.activation(
            st[:, 6:7], st[:, 5:6], AF.Sqrt, bias=sb_cst[:, 0:1]), waits=[t4])
        t6 = Sc.op("vector", lambda e: e.reciprocal(st[:, 7:8], st[:, 6:7]), waits=[t5])
        t7 = Sc.op("vector", lambda e: e.scalar_tensor_tensor(
            st[:, 8:9], st[:, 2:3], -1.0, st[:, 7:8], ALU.mult, ALU.mult), waits=[t6])
        tn = None
        if out_bf_ap is not None:
            tn = Sc.op("scalar", lambda e: e.activation(
                out_bf_ap, x_ap, AF.Identity, bias=st[:, 8:9], scale=st[:, 7:8]),
                waits=[t7])
            cons.append(tn)
        return tn, st, t7

    quant_iter = ([("gu", k, hb) for hb in range(16) for k in range(16)] +
                  [("p", k2, hb) for k2 in range(2) for hb in range(64)])
    qi = [0]
    qpend = []

    def pump_quant(n):
        for _ in range(n):
            if qi[0] < len(quant_iter):
                sl_next = qslot[0]
                emit_qA(*quant_iter[qi[0]])
                qi[0] += 1
                qpend.append(sl_next)
            if len(qpend) > 1:
                emit_qB(qpend.pop(0))

    def drain_quant():
        while qpend:
            emit_qB(qpend.pop(0))

    xc_cons = [[], []]
    xn_cons = [[], []]
    ent_dmas = []
    t_blk_g = [None]
    x_pend = {}

    def issue_x(i):
        sl = i % 2
        x_pend[i] = Sc.dma("x", lambda e, i=i, sl=sl: e.dma_start(
            out=sb_xc[:, sl, :], in_=xb[i * 128:(i + 1) * 128, :]),
            waits=xc_cons[sl] + BAR0)
        xc_cons[sl] = []

    issue_x(0)
    for i in range(PFX):
        sl = i % 2
        t_x = x_pend.pop(i)
        if i + 1 < PFX:
            issue_x(i + 1)
        t_xn, _, _ = emit_ln(sb_xc[:, sl, :], sb_xnbf[:, sl, :], [t_x] + xn_cons[sl])
        xn_cons[sl] = []
        xc_cons[sl].append(t_xn)
        t_wc = Sc.op("vector", lambda e, sl=sl: e.scalar_tensor_tensor(
            sb_dump[:], sb_xnbf[:, sl, :], 1.0, sb_wcrep[:], ALU.mult, ALU.mult,
            accum_out=sb_stat[:, 20:21]), waits=[t_xn] + dump_guard)
        dump_guard = [t_wc]
        if i >= 10:
            pump_quant(1)
        t_ew = Sc.op("scalar", lambda e: e.activation(
            sb_stat[:, 21:22], sb_stat[:, 20:21], AF.Exp, bias=sb_cols[:, 2:3]),
            waits=[t_wc])
        t_sm = Sc.op("tensor", lambda e: e.matmul(
            bkMS.t[0:8, 0:1], sb_patt8[:], sb_stat[:, 21:22], start=True, stop=True),
            waits=bkMS.wr_waits() + [t_ew])
        bkMS.set_writer(t_sm)
        t_ev = Sc.op("vector", lambda e: e.tensor_copy(
            sb_stat[0:8, 22:23], bkMS.t[0:8, 0:1]), waits=bkMS.rd_waits())
        bkMS.add_reader(t_ev)
        t_rc = Sc.op("vector", lambda e: e.reciprocal(
            sb_stat[0:8, 23:24], sb_stat[0:8, 22:23]), waits=[t_ev])
        t_blk = Sc.op("vector", lambda e: e.tensor_scalar(
            sb_attbf[:, 0, 1:9], sb_patt8b[:], sb_stat[:, 21:22], None, ALU.mult),
            waits=[t_ew, t_blk_g[0]] if t_blk_g[0] else [t_ew])
        t_blk_g[0] = None
        if i >= 10:
            pump_quant(1)
        ev_toks = []
        tm = None
        for q4 in range(4):
            bnk = bkSC if q4 % 2 == 0 else bkB0
            tm = Sc.op("tensor", lambda e, q4=q4, sl=sl, b=bnk: e.matmul(
                b.t[0:8, 0:512], sb_attbf[:, 0, 1:9],
                sb_xnbf[:, sl, q4 * 512:(q4 + 1) * 512], start=True, stop=True),
                waits=bnk.wr_waits() + [t_blk])
            bnk.set_writer(tm)
            te = Sc.op("vector", lambda e, q4=q4, b=bnk: e.tensor_scalar(
                sb_entsc[:, q4 * 512:(q4 + 1) * 512], b.t[0:8, 0:512],
                sb_stat[0:8, 23:24], None, ALU.mult),
                waits=bnk.rd_waits() + [t_rc] + (ent_dmas[-2:] if q4 == 0 else []))
            bnk.add_reader(te)
            ev_toks.append(te)
            t_blk_g[0] = tm
        xn_cons[sl] = [tm, t_wc]
        t_ed = Sc.dma("entw", lambda e, i=i: e.dma_start(
            out=entsc_d[i, :, :], in_=sb_entsc[:]), waits=ev_toks)
        ent_dmas.append(t_ed)
        xn_cons[sl].append(t_ed)
        if i >= 10:
            pump_quant(2)

    t_stk = Sc.dma("stk", lambda e: e.dma_start(
        out=sb_ent[:],
        in_=entsc_d.rearrange("c e d -> (c e) d").rearrange("(t p) d -> p t d", p=128)),
        waits=ent_dmas)
    t_eb = [Sc.op("vector", lambda e, tt=tt: e.tensor_copy(
        sb_entbf[:, tt, :], sb_ent[:, tt, :]), waits=[t_stk]) for tt in range(2)]
    entT_toks = []
    for kd in range(16):
        for tt in range(2):
            tm = Sc.op("tensor", lambda e, kd=kd, tt=tt: e.transpose(
                bkTB.t[:, 0:128], sb_entbf[:, tt, kd * 128:(kd + 1) * 128], sb_idbf[:]),
                waits=bkTB.wr_waits() + t_eb)
            bkTB.set_writer(tm)
            te = Sc.op("vector", lambda e, kd=kd, tt=tt: e.tensor_copy(
                sb_xnT[:, kd, tt * 128:(tt + 1) * 128], bkTB.t[:, 0:128]),
                waits=bkTB.rd_waits())
            bkTB.add_reader(te)
            entT_toks.append(te)

    def emit_ckcv(is_ck):
        slab = 2 if is_ck else 3
        w0 = bkSC.wr_waits() + entT_toks + BAR0
        last = None
        for kd in range(16):
            last = Sc.op("tensor", lambda e, kd=kd, slab=slab: e.matmul(
                bkSC.t[:, 0:256], sb_wsl[:, slab, kd, :], sb_xnT[:, kd, 0:256],
                start=(kd == 0), stop=(kd == 15)),
                waits=w0 if kd == 0 else [], sig=(kd == 15))
        bkSC.set_writer(last)
        bias = sb_cols[:, 0:1] if is_ck else sb_cols[:, 1:2]
        t_ev = Sc.op("scalar", lambda e, bias=bias: e.activation(
            sb_ckf[:], bkSC.t[:, 0:256], AF.Identity, bias=bias),
            waits=bkSC.rd_waits())
        bkSC.add_reader(t_ev)
        return t_ev, last

    t_ckf, ck_mm = emit_ckcv(True)
    t_cq = Sc.op("scalar", lambda e: e.activation(sb_ckq[:], sb_ckf[:], AF.Square),
                 waits=[t_ckf])
    t_cs = Sc.op("tensor", lambda e: e.matmul(
        bkMS.t[0:1, 0:256], sb_onesc[:], sb_ckq[:], start=True, stop=True),
        waits=bkMS.wr_waits() + [t_cq])
    bkMS.set_writer(t_cs)
    t_ce = Sc.op("vector", lambda e: e.tensor_scalar(
        sb_row[0:1, 0:256], bkMS.t[0:1, 0:256], 1.0 / HD, None, ALU.mult),
        waits=bkMS.rd_waits())
    bkMS.add_reader(t_ce)
    t_cs2 = Sc.op("scalar", lambda e: e.activation(
        sb_row[0:1, 256:512], sb_row[0:1, 0:256], AF.Sqrt, bias=sb_cst[0:1, 1:2]),
        waits=[t_ce])
    t_cr = Sc.op("vector", lambda e: e.reciprocal(
        sb_row[0:1, 512:768], sb_row[0:1, 256:512]), waits=[t_cs2])
    t_cb = Sc.op("tensor", lambda e: e.matmul(
        bkTF.t[:, 0:256], sb_onesr[:], sb_row[0:1, 512:768], start=True, stop=True),
        waits=bkTF.wr_waits() + [t_cr])
    bkTF.set_writer(t_cb)
    t_cbe = Sc.op("vector", lambda e: e.tensor_copy(sb_rsqb[:, 0:256], bkTF.t[:, 0:256]),
                  waits=bkTF.rd_waits())
    bkTF.add_reader(t_cbe)
    t_ct = Sc.op("vector", lambda e: e.scalar_tensor_tensor(
        sb_ckr[:], sb_ckf[:], sb_cols[:, 4:5], sb_rsqb[:, 0:256], ALU.mult, ALU.mult),
        waits=[t_cbe])
    t_csw = Sc.op("tensor", lambda e: e.matmul(
        bkMS.t[0:64, 0:256], sb_SW[:], sb_ckr[0:64, :], start=True, stop=True),
        waits=bkMS.wr_waits() + [t_ct])
    bkMS.set_writer(t_csw)
    t_cswe = Sc.op("vector", lambda e: e.tensor_copy(sb_usw[:, 0:256], bkMS.t[0:64, 0:256]),
                   waits=bkMS.rd_waits())
    bkMS.add_reader(t_cswe)
    t_cu = Sc.op("vector", lambda e: e.tensor_tensor(
        sb_u[:, 0:256], sb_ckr[0:64, :], sb_ckcc[:], ALU.mult), waits=[t_ct])
    t_cv2 = Sc.op("vector", lambda e: e.tensor_tensor(
        sb_v[:, 0:256], sb_usw[:, 0:256], sb_ckns[:], ALU.mult), waits=[t_cswe])
    t_ckdone = Sc.op("vector", lambda e: e.tensor_tensor(
        sb_ckr[0:64, :], sb_u[:, 0:256], sb_v[:, 0:256], ALU.add),
        waits=[t_cu, t_cv2])
    t_cvf, cv_mm = emit_ckcv(False)
    t_cvb = Sc.op("vector", lambda e: e.tensor_copy(sb_ckq[:], sb_ckf[:]), waits=[t_cvf])
    cvT_toks = []
    for tt in range(2):
        tm = Sc.op("tensor", lambda e, tt=tt: e.transpose(
            bkTB.t[:, 0:128], sb_ckq[:, tt * 128:(tt + 1) * 128], sb_idbf[:]),
            waits=bkTB.wr_waits() + [t_cvb])
        bkTB.set_writer(tm)
        te = Sc.op("vector", lambda e, tt=tt: e.tensor_copy(
            sb_cvT[:, tt, :], bkTB.t[:, 0:128]), waits=bkTB.rd_waits())
        bkTB.add_reader(te)
        cvT_toks.append(te)
    ckcv_toks = [t_ckdone] + cvT_toks
    BARP = Sc.barrier(streams=("entw", "stk"))
    slab_seed = {2: [ck_mm], 3: [cv_mm]}

    xnT_last = []
    x2_pend = {}

    def issue_x2(i):
        sl = i % 2
        x2_pend[i] = Sc.dma("x", lambda e, i=i, sl=sl: e.dma_start(
            out=sb_xc[:, sl, :], in_=xo[i * 128:(i + 1) * 128, :]),
            waits=xc_cons[sl] + (BARP if i < 2 else []))
        xc_cons[sl] = []

    issue_x2(0)
    for i in range(9):
        sl = i % 2
        t_x = x2_pend.pop(i)
        if i + 1 < 9:
            issue_x2(i + 1)
        t_xn, _, _ = emit_ln(sb_xc[:, sl, :], sb_xnbf[:, sl, :], [t_x] + xn_cons[sl])
        xn_cons[sl] = []
        xc_cons[sl].append(t_xn)
        for kd in range(16):
            tm = Sc.op("tensor", lambda e, kd=kd, sl=sl: e.transpose(
                bkTB.t[:, 0:128], sb_xnbf[:, sl, kd * 128:(kd + 1) * 128], sb_idbf[:]),
                waits=bkTB.wr_waits() + [t_xn] + (BARP if i == 0 else []))
            bkTB.set_writer(tm)
            te = Sc.op("vector", lambda e, kd=kd, i=i: e.tensor_copy(
                sb_xnT[:, kd, i * 128:(i + 1) * 128], bkTB.t[:, 0:128]),
                waits=bkTB.rd_waits())
            bkTB.add_reader(te)
            if kd == 7:
                pump_quant(2)
            if kd == 15:
                xn_cons[sl].append(tm)
                xnT_last = [te]
        pump_quant(2)
    BAR1 = Sc.barrier()

    slab_cons = {i: slab_seed.get(i, []) for i in range(4)}
    slab_rot = [0]
    merged_toks = []
    def next_slab():
        s = slab_rot[0]
        slab_rot[0] = (slab_rot[0] + 1) % 4
        return s

    def emit_proj(Wsrc, cast_tok, h, bias_col, c0, ncol, out_f32, deps, out_bf=None,
                  bank_fixed=None):
        slab = next_slab()
        t_sl = Sc.dma("aslab", lambda e, slab=slab: e.dma_start(
            out=sb_wsl[:, slab, :, :],
            in_=Wsrc[:, h * 128:(h + 1) * 128].rearrange("(k p) n -> p k n", p=128)),
            waits=slab_cons[slab] + [cast_tok])
        slab_cons[slab] = []
        ev_toks = []
        for nb in range((ncol + 511) // 512):
            f0 = nb * 512
            cw = min(512, ncol - f0)
            bank = bank_fixed or (bkA0 if nb % 2 == 0 else bkA1)
            w0 = bank.wr_waits() + [t_sl] + deps
            last = None
            for kd in range(16):
                last = Sc.op("tensor", lambda e, kd=kd, f0=f0, cw=cw, slab=slab, b=bank: e.matmul(
                    b.t[:, 0:cw], sb_wsl[:, slab, kd, :],
                    sb_xnT[:, kd, c0 + f0:c0 + f0 + cw],
                    start=(kd == 0), stop=(kd == 15)),
                    waits=w0 if kd == 0 else [], sig=(kd == 15))
            bank.set_writer(last)
            slab_cons[slab].append(last)
            dst = out_f32 if out_bf is None else out_bf
            tev = Sc.op("scalar", lambda e, f0=f0, cw=cw, dst=dst, b=bank: e.activation(
                dst[:, f0:f0 + cw], b.t[:, 0:cw], AF.Identity, bias=bias_col),
                waits=bank.rd_waits())
            bank.add_reader(tev)
            ev_toks.append(tev)
        return ev_toks

    def emit_rms_rope(ncol, toff, gain_col, out_ap, deps):
        nonlocal dump_guard
        t_sq = Sc.op("scalar", lambda e: e.activation(
            sb_dump[:, 0:ncol], sb_qf[:, 0:ncol], AF.Square),
            waits=deps + dump_guard)
        dump_guard = []
        rs_toks = []
        for nb in range((ncol + 511) // 512):
            f0 = nb * 512; cw = min(512, ncol - f0)
            tm = Sc.op("tensor", lambda e, f0=f0, cw=cw: e.matmul(
                bkMS.t[0:1, 0:cw], sb_onesc[:], sb_dump[:, f0:f0 + cw],
                start=True, stop=True), waits=bkMS.wr_waits() + [t_sq])
            bkMS.set_writer(tm)
            dump_guard.append(tm)
            te = Sc.op("vector", lambda e, f0=f0, cw=cw: e.tensor_scalar(
                sb_row[0:1, f0:f0 + cw], bkMS.t[0:1, 0:cw], 1.0 / HD, None, ALU.mult),
                waits=bkMS.rd_waits())
            bkMS.add_reader(te)
            rs_toks.append(te)
        t_rc2 = Sc.op("scalar", lambda e: e.activation(
            sb_row[0:1, 0:ncol], sb_row[0:1, 0:ncol], AF.Sqrt, bias=sb_cst[0:1, 1:2]),
            waits=rs_toks)
        bc_toks = []
        for nb in range((ncol + 511) // 512):
            f0 = nb * 512; cw = min(512, ncol - f0)
            tm = Sc.op("tensor", lambda e, f0=f0, cw=cw: e.matmul(
                bkTF.t[:, 0:cw], sb_onesr[:], sb_row[0:1, f0:f0 + cw],
                start=True, stop=True), waits=bkTF.wr_waits() + [t_rc2])
            bkTF.set_writer(tm)
            te = Sc.op("vector", lambda e, f0=f0, cw=cw: e.reciprocal(
                sb_rsqb[:, f0:f0 + cw], bkTF.t[:, 0:cw]), waits=bkTF.rd_waits())
            bkTF.add_reader(te)
            bc_toks.append(te)
        t_t = Sc.op("vector", lambda e: e.scalar_tensor_tensor(
            sb_qs[:, 0:ncol], sb_qf[:, 0:ncol], gain_col, sb_rsqb[:, 0:ncol],
            ALU.mult, ALU.mult), waits=bc_toks + deps)
        sw_toks = []
        for nb in range((ncol + 511) // 512):
            f0 = nb * 512; cw = min(512, ncol - f0)
            tm = Sc.op("tensor", lambda e, f0=f0, cw=cw: e.matmul(
                bkMS.t[0:64, 0:cw], sb_SW[:], sb_qs[0:64, f0:f0 + cw],
                start=True, stop=True), waits=bkMS.wr_waits() + [t_t])
            bkMS.set_writer(tm)
            te = Sc.op("vector", lambda e, f0=f0, cw=cw: e.tensor_copy(
                sb_usw[:, f0:f0 + cw], bkMS.t[0:64, 0:cw]), waits=bkMS.rd_waits())
            bkMS.add_reader(te)
            sw_toks.append(te)
        t_u = Sc.op("vector", lambda e: e.tensor_tensor(
            sb_u[:, 0:ncol], sb_qs[0:64, 0:ncol], sb_rtcc[:, toff:toff + ncol],
            ALU.mult), waits=[t_t])
        t_v = Sc.op("vector", lambda e: e.tensor_tensor(
            sb_v[:, 0:ncol], sb_usw[:, 0:ncol], sb_rtns[:, toff:toff + ncol],
            ALU.mult), waits=sw_toks)
        t_o1 = Sc.op("vector", lambda e: e.tensor_tensor(
            out_ap[0:64, 0:ncol], sb_u[:, 0:ncol], sb_v[:, 0:ncol], ALU.add),
            waits=[t_u, t_v])
        t_rest = Sc.op("vector", lambda e: e.tensor_copy(
            out_ap[64:128, 0:ncol], sb_qs[64:128, 0:ncol]), waits=[t_t])
        return [t_o1, t_rest]

    prev_qr_cons = [[], []]
    prev_lkr_cons = [[], []]
    prev_lv_cons = [[], []]
    prev_lvT_cons = [[], []]
    prev_snk_cons = [[], []]
    ropeg = {"qs": [], "u": [], "row": []}
    front_state = [None, None]

    def front_gen(h):
        hs = h % 2
        deps0 = (BAR1 if h == 0 else []) + prev_qr_cons[hs]
        prev_qr_cons[hs] = []
        if h > 0:
            deps0 = deps0 + front_state[1 - hs]["qf_free"]
        q_ev = emit_proj(wbf_q, qcast_a["q"], h, sb_bqT[:, h:h + 1], 128, TOK,
                         sb_qf, deps0, bank_fixed=bkA0)
        yield
        qr_toks = emit_rms_rope(TOK, 128, sb_cols[:, 3:4], sb_qr2[:, hs, :], q_ev)
        yield
        lk_ev = emit_proj(wbf_lk, qcast_a["lk"], h, sb_blkT[:, h:h + 1], 0, TEXT,
                          sb_qf, qr_toks + prev_lkr_cons[hs], bank_fixed=bkA0)
        prev_lkr_cons[hs] = []
        yield
        lkr_toks = emit_rms_rope(TEXT, 0, sb_cols[:, 4:5], sb_lkr2[:, hs, :], lk_ev)
        yield
        lv_ev = emit_proj(wbf_lv, qcast_a["lv"], h, sb_blvT[:, h:h + 1], 0, TEXT,
                          None, lkr_toks + prev_lv_cons[hs], out_bf=sb_lv,
                          bank_fixed=bkA0)
        prev_lv_cons[hs] = []
        yield
        lvT_toks = []
        for j in range(9):
            tm = Sc.op("tensor", lambda e, j=j: e.transpose(
                bkTB.t[:, 0:128], sb_lv[:, j * 128:(j + 1) * 128], sb_idbf[:]),
                waits=bkTB.wr_waits() + lv_ev)
            bkTB.set_writer(tm)
            te = Sc.op("vector", lambda e, j=j, hs=hs: e.tensor_copy(
                sb_lvT2[:, hs, j, :], bkTB.t[:, 0:128]),
                waits=bkTB.rd_waits() + (prev_lvT_cons[hs] if j == 0 else []))
            bkTB.add_reader(te)
            lvT_toks.append(te)
            prev_lv_cons[hs].append(tm)
        prev_lvT_cons[hs] = []
        yield
        t_sr0 = Sc.op("tensor", lambda e, h=h, hs=hs: e.matmul(
            bkMS.t[0:1, 0:512], sb_sinkT[:, h:h + 1], sb_qr2[:, hs, 0:512],
            start=True, stop=True), waits=bkMS.wr_waits() + qr_toks)
        bkMS.set_writer(t_sr0)
        t_se0 = Sc.op("vector", lambda e: e.tensor_scalar(
            sb_row[0:1, 0:512], bkMS.t[0:1, 0:512], float(SCALE), None, ALU.mult),
            waits=bkMS.rd_waits() + ropeg["row"])
        bkMS.add_reader(t_se0)
        t_sr1 = Sc.op("tensor", lambda e, h=h, hs=hs: e.matmul(
            bkMS.t[0:1, 0:512], sb_sinkT[:, h:h + 1], sb_qr2[:, hs, 512:1024],
            start=True, stop=True), waits=bkMS.wr_waits())
        bkMS.set_writer(t_sr1)
        t_se1 = Sc.op("vector", lambda e: e.tensor_scalar(
            sb_row[0:1, 512:1024], bkMS.t[0:1, 0:512], float(SCALE), None, ALU.mult),
            waits=bkMS.rd_waits())
        bkMS.add_reader(t_se1)
        t_sbf = Sc.op("vector", lambda e: e.tensor_copy(
            sb_u[0:1, 0:TOK], sb_row[0:1, 0:TOK]),
            waits=[t_se0, t_se1] + ropeg["u"] + lkr_toks)
        snk_evs = []
        tmx = None
        for cch in range(NCH):
            tmx = Sc.op("tensor", lambda e, cch=cch: e.transpose(
                bkTB.t[:, 896:897], sb_u[0:1, cch * 128:(cch + 1) * 128],
                sb_idbf[0:1, 0:1]),
                waits=bkTB.wr_waits() + [t_sbf] + snk_evs[-1:])
            bkTB.set_writer(tmx)
            tev = Sc.op("vector", lambda e, cch=cch, hs=hs: e.tensor_copy(
                sb_snk[:, hs, cch:cch + 1], bkTB.t[:, 896:897]),
                waits=bkTB.rd_waits() + (prev_snk_cons[hs] if cch == 0 else []))
            bkTB.add_reader(tev)
            snk_evs.append(tev)
        prev_snk_cons[hs] = []
        t_nsk = Sc.op("vector", lambda e, hs=hs: e.tensor_scalar(
            sb_snk[:, hs, 8:16], sb_snk[:, hs, 0:8], -1.0, None, ALU.mult),
            waits=snk_evs)
        ropeg["row"] = [t_sbf]
        ropeg["u"] = [tmx]
        front_state[hs] = {"qr": qr_toks, "lkr": lkr_toks, "lvT": lvT_toks,
                           "nsk": [t_nsk], "qf_free": lkr_toks}
        yield

    def run_gen(g):
        if g is not None:
            try:
                next(g)
                return g
            except StopIteration:
                return None
        return None

    merged_toks = merged_toks if False else merged_toks
    chain_g = [[], []]
    esg = [[], []]
    att_g = [[], []]
    askg = [[], []]
    mtg = [[], []]
    g0 = front_gen(0)
    for _ in range(10):
        g0 = run_gen(g0)
        if g0 is None:
            break

    for h in range(NH):
        hs = h % 2
        fs = front_state[hs]
        qr_toks = fs["qr"]; lkr_toks = fs["lkr"]; lvT_toks = fs["lvT"]
        nsk_toks = fs["nsk"]
        gen_next = front_gen(h + 1) if h + 1 < NH else None

        def emit_scores(cch):
            q0 = cch * 128
            par = cch % 2
            P = bkSC if par == 0 else bkA1
            mm1 = Sc.op("tensor", lambda e, q0=q0, b=P, hs=hs: e.matmul(
                b.t[:, 0:256], sb_qr2[:, hs, q0:q0 + 128], sb_ckr[:],
                start=True, stop=False),
                waits=P.wr_waits() + qr_toks + ckcv_toks + lkr_toks, sig=False)
            mm2 = Sc.op("tensor", lambda e, q0=q0, b=P, hs=hs: e.matmul(
                b.t[:, 256:512], sb_qr2[:, hs, q0:q0 + 128],
                sb_lkr2[:, hs, q0:q0 + 256], start=False, stop=True), waits=[])
            P.set_writer(mm2)
            prev_qr_cons[hs].append(mm2)
            prev_lkr_cons[hs].append(mm2)
            return P

        def emit_chain(cch, P):
            q0 = cch * 128
            par = cch % 2
            X = bkB0 if par == 0 else bkB1
            stc = sb_stat[:, 14 + par * 6: 20 + par * 6]
            midx = 0 if cch == 0 else 1
            t_ev0 = Sc.op("vector", lambda e, cch=cch, par=par, b=P: e.scalar_tensor_tensor(
                sb_sc[:, par, 0:256], b.t[:, 0:256], float(SCALE), sb_cmask[:, cch, :],
                ALU.mult, ALU.add), waits=P.rd_waits() + chain_g[par])
            t_ev = Sc.op("vector", lambda e, midx=midx, par=par, b=P: e.scalar_tensor_tensor(
                sb_sc[:, par, 256:512], b.t[:, 256:512], float(SCALE),
                sb_lmask[:, midx, :], ALU.mult, ALU.add), waits=[t_ev0])
            P.add_reader(t_ev)
            t_mx = Sc.op("vector", lambda e, par=par, stc=stc: e.reduce_max(
                stc[:, 0:2], sb_sc[:, par, :].rearrange("p (g w) -> p g w", w=256),
                AX.X, negate=True), waits=[t_ev])
            t_mf = Sc.op("vector", lambda e, cch=cch, stc=stc, hs=hs: e.tensor_scalar(
                stc[:, 0:1], stc[:, 0:1], sb_snk[:, hs, 8 + cch:9 + cch], None,
                ALU.min), waits=[t_mx] + nsk_toks)
            t_e1 = Sc.op("scalar", lambda e, par=par, stc=stc: e.activation(
                sb_esc[:, par, 0:256], sb_sc[:, par, 0:256], AF.Exp,
                bias=stc[:, 0:1], accum_out=stc[:, 2:3]), waits=[t_mf])
            t_e2 = Sc.op("scalar", lambda e, par=par, stc=stc: e.activation(
                sb_esc[:, par, 256:512], sb_sc[:, par, 256:512], AF.Exp,
                bias=stc[:, 1:2], accum_out=stc[:, 3:4]), waits=[t_mx, t_e1])
            t_es = Sc.op("scalar", lambda e, cch=cch, par=par, stc=stc, hs=hs: e.activation(
                sb_snk[:, hs, 16 + par:17 + par], sb_snk[:, hs, cch:cch + 1], AF.Exp,
                bias=stc[:, 0:1]), waits=[t_mf] + esg[par])
            t_sa = Sc.op("vector", lambda e, par=par, stc=stc, hs=hs: e.tensor_tensor(
                stc[:, 2:3], stc[:, 2:3], sb_snk[:, hs, 16 + par:17 + par], ALU.add),
                waits=[t_e1, t_es])
            t_rc = Sc.op("vector", lambda e, stc=stc: e.reciprocal(
                stc[:, 4:6], stc[:, 2:4]), waits=[t_sa, t_e2])
            t_a1 = Sc.op("scalar", lambda e, par=par, stc=stc: e.activation(
                sb_attbf[:, par, 0:256], sb_esc[:, par, 0:256], AF.Identity,
                scale=stc[:, 4:5]), waits=[t_rc] + att_g[par])
            t_a2 = Sc.op("scalar", lambda e, par=par, stc=stc: e.activation(
                sb_attbf[:, par, 256:512], sb_esc[:, par, 256:512], AF.Identity,
                scale=stc[:, 5:6]), waits=[t_rc, t_a1])
            t_ak = Sc.op("vector", lambda e, par=par, stc=stc, hs=hs: e.scalar_tensor_tensor(
                sb_ask[:, par:par + 1], sb_snk[:, hs, 16 + par:17 + par], 1.0,
                stc[:, 4:5], ALU.mult, ALU.mult), waits=[t_rc] + askg[par])
            prev_snk_cons[hs].extend([t_mf, t_es, t_sa, t_ak])
            esg[par] = [t_sa]
            tt = None
            for r in range(4):
                tt = Sc.op("tensor", lambda e, par=par, r=r: e.transpose(
                    bkTB.t[:, r * 128:(r + 1) * 128],
                    sb_attbf[:, par, r * 128:(r + 1) * 128], sb_idbf[:]),
                    waits=(bkTB.wr_waits() if r == 0 else []) +
                    ([t_a1] if r < 2 else [t_a2]), sig=(r == 3))
            tt5 = Sc.op("tensor", lambda e, par=par: e.transpose(
                bkTB.t[0:1, 512:640], sb_ask[:, par:par + 1], sb_idbf[:]),
                waits=[t_ak])
            bkTB.set_writer(tt5)
            t_aT = Sc.op("vector", lambda e, par=par: e.tensor_copy(
                sb_attT[:, par, :], bkTB.t[:, 0:512]), waits=bkTB.rd_waits())
            t_aS = Sc.op("vector", lambda e, par=par: e.tensor_copy(
                sb_attS[:, par, :], bkTB.t[0:1, 512:640]), waits=bkTB.rd_waits())
            bkTB.add_reader(t_aT)
            bkTB.add_reader(t_aS)
            att_g[par] = [tt, tt5]
            askg[par] = [tt5]
            c1 = Sc.op("tensor", lambda e, par=par, b=X: e.matmul(
                b.t[:, 0:128], sb_cvT[:, 0, :], sb_attT[:, par, 0:128],
                start=True, stop=False), waits=X.wr_waits() + [t_aT], sig=False)
            c2 = Sc.op("tensor", lambda e, par=par, b=X: e.matmul(
                b.t[:, 0:128], sb_cvT[:, 1, :], sb_attT[:, par, 128:256],
                start=False, stop=False), waits=[], sig=False)
            c3 = Sc.op("tensor", lambda e, par=par, h=h, b=X: e.matmul(
                b.t[:, 0:128], sb_sinkvR[:, h * 128:(h + 1) * 128],
                sb_attS[:, par, :], start=False, stop=False), waits=[t_aS], sig=False)
            l1 = Sc.op("tensor", lambda e, par=par, cch=cch, b=X, hs=hs: e.matmul(
                b.t[:, 128:256], sb_lvT2[:, hs, cch, :], sb_attT[:, par, 256:384],
                start=False, stop=False), waits=lvT_toks, sig=False)
            l2 = Sc.op("tensor", lambda e, par=par, cch=cch, b=X, hs=hs: e.matmul(
                b.t[:, 128:256], sb_lvT2[:, hs, cch + 1, :], sb_attT[:, par, 384:512],
                start=False, stop=True), waits=[])
            X.set_writer(l2)
            prev_lvT_cons[hs].append(l2)
            t_m1 = Sc.op("scalar", lambda e, par=par, b=X: e.activation(
                sb_mtmp[:, par, :], b.t[:, 0:128], AF.Identity,
                scale=sb_cst[:, 3:4]), waits=X.rd_waits() + mtg[par])
            X.add_reader(t_m1)
            t_m2 = Sc.op("vector", lambda e, h=h, q0=q0, par=par, b=X: e.scalar_tensor_tensor(
                sb_merged[:, h, q0:q0 + 128], b.t[:, 128:256], 0.5,
                sb_mtmp[:, par, :], ALU.mult, ALU.add),
                waits=X.rd_waits() + [t_m1])
            X.add_reader(t_m2)
            mtg[par] = [t_m2]
            chain_g[par] = [t_e2, t_aT]
            merged_toks.append(t_m2)

        pend = []
        for cch in range(NCH + 1):
            if cch < NCH:
                pend.append((cch, emit_scores(cch)))
            if len(pend) > (1 if cch < NCH else 0):
                c0_, P0_ = pend.pop(0)
                emit_chain(c0_, P0_)
            gen_next = run_gen(gen_next)
            pump_quant(2)
        while gen_next is not None:
            gen_next = run_gen(gen_next)

    drain_quant()

    BAR2 = Sc.barrier()

    ao_dmas = []
    evf_guard = []
    for m in range(16):
        slab = next_slab()
        t_sl = Sc.dma("aslab", lambda e, m=m, slab=slab: e.dma_start(
            out=sb_wsl[:, slab, :, :],
            in_=wbf_o[:, m * 128:(m + 1) * 128].rearrange("(k p) n -> p k n", p=128)),
            waits=slab_cons[slab] + [qcast_a["o"]])
        slab_cons[slab] = []
        for th in range(2):
            bank = bkA0 if th == 0 else bkA1
            w0 = bank.wr_waits() + [t_sl] + merged_toks[-1:]
            last = None
            for kd in range(16):
                last = Sc.op("tensor", lambda e, kd=kd, th=th, slab=slab, b=bank: e.matmul(
                    b.t[:, 0:512], sb_wsl[:, slab, kd, :],
                    sb_merged[:, kd, th * 512:(th + 1) * 512],
                    start=(kd == 0), stop=(kd == 15)),
                    waits=w0 if kd == 0 else [], sig=(kd == 15))
            bank.set_writer(last)
            slab_cons[slab].append(last)
            t_ev = Sc.op("scalar", lambda e, m=m, b=bank: e.activation(
                sb_evf[:], b.t[:, 0:512], AF.Identity, bias=sb_boT[:, m:m + 1]),
                waits=bank.rd_waits() + evf_guard)
            bank.add_reader(t_ev)
            t_evb = Sc.op("vector", lambda e: e.tensor_copy(
                sb_dump[:, 0:512], sb_evf[:]), waits=[t_ev] + dump_guard)
            evf_guard = [t_evb]
            dump_guard = []
            for p4 in range(4):
                cch = th * 4 + p4
                tm = Sc.op("tensor", lambda e, p4=p4: e.transpose(
                    bkTB.t[:, 0:128], sb_dump[:, p4 * 128:(p4 + 1) * 128], sb_idbf[:]),
                    waits=bkTB.wr_waits() + [t_evb])
                bkTB.set_writer(tm)
                dump_guard.append(tm)
                te = Sc.op("vector", lambda e: e.tensor_copy(
                    sb_mtmp[:, 0, :], bkTB.t[:, 0:128]), waits=bkTB.rd_waits())
                bkTB.add_reader(te)
                td = Sc.dma("aoscw", lambda e, cch=cch, m=m: e.dma_start(
                    out=aosc[cch, :, m * 128:(m + 1) * 128], in_=sb_mtmp[:, 0, :]),
                    waits=[te])
                bkTB.add_reader(td)
                ao_dmas.append(td)

    x2_cons = [[], []]
    xn2T_last = []
    x2sp_toks = []
    fo_guard = []
    for cch in range(NCH):
        sl = cch % 2
        t_ldao = Sc.dma("x", lambda e, cch=cch: e.dma_start(
            out=sb_fo[:], in_=aosc[cch, :, :]), waits=ao_dmas[-2:] + fo_guard)
        t_x = Sc.dma("x", lambda e, cch=cch, sl=sl: e.dma_start(
            out=sb_xc[:, sl, :], in_=xo[(cch + 1) * 128:(cch + 2) * 128, :]),
            waits=xc_cons[sl])
        xc_cons[sl] = []
        _, st, t7 = emit_ln(sb_fo[:], None, [t_ldao])
        t_nf = Sc.op("vector", lambda e, st=st: e.tensor_scalar(
            sb_fo[:], sb_fo[:], st[:, 7:8], st[:, 8:9], ALU.mult, ALU.add),
            waits=[t7])
        t_x2 = Sc.op("vector", lambda e, sl=sl: e.tensor_tensor(
            sb_x2[:, sl, :], sb_fo[:], sb_xc[:, sl, :], ALU.add),
            waits=[t_nf, t_x] + x2_cons[sl])
        fo_guard = [t_x2]
        x2_cons[sl] = []
        xc_cons[sl].append(t_x2)
        t_sp = Sc.dma("x2spw", lambda e, cch=cch, sl=sl: e.dma_start(
            out=x2sp[cch, :, :], in_=sb_x2[:, sl, :]), waits=[t_x2])
        x2_cons[sl].append(t_sp)
        x2sp_toks.append(t_sp)
        t_xn2, _, _ = emit_ln(sb_x2[:, sl, :], sb_xnbf[:, sl, :],
                              [t_x2] + xn_cons[sl])
        xn_cons[sl] = []
        x2_cons[sl].append(t_xn2)
        for kd in range(16):
            tm = Sc.op("tensor", lambda e, kd=kd, sl=sl: e.transpose(
                bkTB.t[:, 0:128], sb_xnbf[:, sl, kd * 128:(kd + 1) * 128], sb_idbf[:]),
                waits=bkTB.wr_waits() + [t_xn2])
            bkTB.set_writer(tm)
            te = Sc.op("vector", lambda e, kd=kd, cch=cch: e.tensor_copy(
                sb_xn2T[:, kd, cch * 128:(cch + 1) * 128], bkTB.t[:, 0:128]),
                waits=bkTB.rd_waits())
            bkTB.add_reader(te)
            if kd == 15:
                xn_cons[sl].append(tm)
                xn2T_last = [te]
    BAR3 = Sc.barrier(streams=("x2spw", "aoscw"))

    ffn_dmas = []
    gsl_cons = {i: [] for i in range(4)}
    evf_guard2 = []
    for hf in range(2):
        hoff = hf * 512
        h_last = []
        for jj in range(64):
            sl = jj % 2
            jg, ju = jj, jj + 64
            t_sg = Sc.dma("gslab", lambda e, jg=jg, sl=sl: e.dma_start(
                out=sb_wsl[:, sl * 2, :, :],
                in_=whatT_gu[:, :, jg * 128:(jg + 1) * 128].rearrange("k p f -> p k f")),
                waits=gsl_cons[sl * 2] + [wtok[("gu", jg, kk)] for kk in range(16)] +
                (BAR3 if jj < 2 and hf == 0 else []))
            t_su = Sc.dma("gslab", lambda e, ju=ju, sl=sl: e.dma_start(
                out=sb_wsl[:, sl * 2 + 1, :, :],
                in_=whatT_gu[:, :, ju * 128:(ju + 1) * 128].rearrange("k p f -> p k f")),
                waits=gsl_cons[sl * 2 + 1] + [wtok[("gu", ju, kk)] for kk in range(16)])
            gsl_cons[sl * 2] = []
            gsl_cons[sl * 2 + 1] = []
            bg = bkA0 if sl == 0 else bkA1
            bu = bkB0 if sl == 0 else bkB1
            w0 = bg.wr_waits() + [t_sg] + (xn2T_last if jj == 0 else [])
            lastg = None
            for kd in range(16):
                lastg = Sc.op("tensor", lambda e, kd=kd, sl=sl, hoff=hoff, b=bg: e.matmul(
                    b.t[:, 0:512], sb_wsl[:, sl * 2, kd, :],
                    sb_xn2T[:, kd, hoff:hoff + 512],
                    start=(kd == 0), stop=(kd == 15)),
                    waits=w0 if kd == 0 else [], sig=(kd == 15))
            bg.set_writer(lastg)
            gsl_cons[sl * 2].append(lastg)
            w0 = bu.wr_waits() + [t_su]
            lastu = None
            for kd in range(16):
                lastu = Sc.op("tensor", lambda e, kd=kd, sl=sl, hoff=hoff, b=bu: e.matmul(
                    b.t[:, 0:512], sb_wsl[:, sl * 2 + 1, kd, :],
                    sb_xn2T[:, kd, hoff:hoff + 512],
                    start=(kd == 0), stop=(kd == 15)),
                    waits=w0 if kd == 0 else [], sig=(kd == 15))
            bu.set_writer(lastu)
            gsl_cons[sl * 2 + 1].append(lastu)
            t_si = Sc.op("scalar", lambda e, b=bg: e.activation(
                sb_evf[:], b.t[:, 0:512], AF.Silu), waits=bg.rd_waits() + evf_guard2)
            bg.add_reader(t_si)
            t_h = Sc.op("vector", lambda e, jj=jj, b=bu: e.tensor_tensor(
                h_tile(jj), b.t[:, 0:512], sb_evf[:], ALU.mult),
                waits=bu.rd_waits() + [t_si] + (BAR3 if jj < 2 and hf == 0 else []))
            bu.add_reader(t_h)
            evf_guard2 = [t_h]
            h_last = [t_h]
        for m2 in range(16):
            sl = m2 % 2
            osl = 1 - sl
            pq = [wtok[("p", m2 // 8, hb)] for hb in range(64)] if hf == 0 else []
            t_s1 = Sc.dma("pslab", lambda e, m2=m2, sl=sl: e.dma_start(
                out=sb_wsl[:, sl * 2:sl * 2 + 2, :, :].rearrange("p a k n -> p (a k) n"),
                in_=whatT_p[m2, 0:32].rearrange("k p f -> p k f")),
                waits=gsl_cons[sl * 2] + gsl_cons[sl * 2 + 1] + pq)
            t_s2 = Sc.dma("pslab", lambda e, m2=m2, osl=osl: e.dma_start(
                out=sb_wsl[:, osl * 2:osl * 2 + 2, :, :].rearrange("p a k n -> p (a k) n"),
                in_=whatT_p[m2, 32:64].rearrange("k p f -> p k f")),
                waits=gsl_cons[osl * 2] + gsl_cons[osl * 2 + 1])
            for i4 in range(4):
                gsl_cons[i4] = []
            bank = bkA0 if sl == 0 else bkA1
            w0 = bank.wr_waits() + [t_s1, t_s2] + h_last
            last = None
            for kd in range(64):
                wsrc = (sb_wsl[:, sl * 2 + kd // 16, kd % 16, :] if kd < 32 else
                        sb_wsl[:, osl * 2 + (kd - 32) // 16, (kd - 32) % 16, :])
                last = Sc.op("tensor", lambda e, kd=kd, wsrc=wsrc, b=bank: e.matmul(
                    b.t[:, 0:512], wsrc, h_tile(kd), start=(kd == 0), stop=(kd == 63)),
                    waits=w0 if kd == 0 else [], sig=(kd == 63))
            bank.set_writer(last)
            for i4 in range(4):
                gsl_cons[i4].append(last)
            t_ev = Sc.op("scalar", lambda e, b=bank: e.activation(
                sb_evf[:], b.t[:, 0:512], AF.Identity),
                waits=bank.rd_waits() + evf_guard2)
            bank.add_reader(t_ev)
            t_evb = Sc.op("vector", lambda e: e.tensor_copy(
                sb_dump[:, 0:512], sb_evf[:]), waits=[t_ev] + dump_guard)
            evf_guard2 = [t_evb]
            dump_guard = []
            for p4 in range(4):
                cch = hf * 4 + p4
                tm = Sc.op("tensor", lambda e, p4=p4: e.transpose(
                    bkTB.t[:, 0:128], sb_dump[:, p4 * 128:(p4 + 1) * 128], sb_idbf[:]),
                    waits=bkTB.wr_waits() + [t_evb])
                bkTB.set_writer(tm)
                dump_guard.append(tm)
                te = Sc.op("vector", lambda e: e.tensor_copy(
                    sb_mtmp[:, 0, :], bkTB.t[:, 0:128]), waits=bkTB.rd_waits())
                bkTB.add_reader(te)
                td = Sc.dma("ffnw", lambda e, cch=cch, m2=m2: e.dma_start(
                    out=ffnsc[cch, :, m2 * 128:(m2 + 1) * 128], in_=sb_mtmp[:, 0, :]),
                    waits=[te])
                bkTB.add_reader(td)
                ffn_dmas.append(td)

    BAR4 = Sc.barrier(streams=("whatw",))

    out_dmas = []
    fo_cons = []
    for cch in range(NCH):
        sl = cch % 2
        t_f = Sc.dma("x", lambda e, cch=cch: e.dma_start(
            out=sb_fo2[:], in_=ffnsc[cch, :, :]),
            waits=ffn_dmas[-2:] + fo_cons + (BAR4 if cch < 2 else []))
        t_x2l = Sc.dma("x", lambda e, cch=cch, sl=sl: e.dma_start(
            out=sb_x2[:, sl, :], in_=x2sp[cch, :, :]),
            waits=x2_cons[sl] + (BAR4 if cch < 2 else []) + x2sp_toks[-1:])
        x2_cons[sl] = []
        _, st, t7 = emit_ln(sb_fo2[:], None, [t_f])
        t_nf = Sc.op("vector", lambda e, st=st: e.tensor_scalar(
            sb_fo[:], sb_fo2[:], st[:, 7:8], st[:, 8:9], ALU.mult, ALU.add),
            waits=[t7] + (BAR4 if cch == 0 else []))
        fo_cons = [t_nf]
        t_o = Sc.op("vector", lambda e, sl=sl: e.tensor_tensor(
            sb_outst[:, sl, :], sb_fo[:], sb_x2[:, sl, :], ALU.add),
            waits=[t_nf, t_x2l] + xc_cons[sl])
        xc_cons[sl] = []
        x2_cons[sl].append(t_o)
        td = Sc.dma("outw", lambda e, cch=cch, sl=sl: e.dma_start(
            out=out[cch * 128:(cch + 1) * 128, :], in_=sb_outst[:, sl, :]),
            waits=[t_o])
        xc_cons[sl].append(td)
        out_dmas.append(td)

    Sc.op("sync", lambda e: e.nop(), waits=[out_dmas[-1]], sig=False)
    Sc.build()
    n_inst = sum(len(v) for v in Sc.items.values())
    print(f"program instructions (approx): {n_inst}")
    return nc


def _host_inputs(inputs, core):
    b, qv = core // 4, core % 4
    start = qv * TOK
    x = np.asarray(inputs["L0_attn_hidden_states"], dtype=np.float32)[b]
    xb_ = np.ascontiguousarray(x)
    halo = np.zeros((128, D), np.float32) if qv == 0 else x[start - 128:start]
    xo_ = np.ascontiguousarray(np.concatenate([halo, x[start:start + TOK]], axis=0))
    pos = np.maximum(np.arange(start - 128, start + TOK), 0).astype(np.float32)
    inv_freq = 1.0 / 10000.0 ** (np.arange(32, dtype=np.float32) * 2.0 / 64.0)
    ang = pos[None, :] * inv_freq[:, None]
    cos, sin = np.cos(ang), np.sin(ang)
    rtcc_ = np.ascontiguousarray(np.concatenate([cos, cos], 0).astype(np.float32))
    rtns_ = np.ascontiguousarray(np.concatenate([-sin, sin], 0).astype(np.float32))
    bend = (np.arange(C) * R + (R - 1)).astype(np.float32)
    angc = bend[None, :] * inv_freq[:, None]
    cc_, sc_ = np.cos(angc), np.sin(angc)
    cktcc_ = np.ascontiguousarray(np.concatenate([cc_, cc_], 0).astype(np.float32))
    cktns_ = np.ascontiguousarray(np.concatenate([-sc_, sc_], 0).astype(np.float32))
    cmask = np.zeros((NCH, 128, C), np.float32)
    for cch in range(NCH):
        posr = start + cch * 128 + np.arange(128)
        cmask[cch] = np.where(bend[None, :] <= posr[:, None], 0.0, -1e9)
    lmask = np.zeros((2, 128, 256), np.float32)
    i_ = np.arange(128)[:, None]
    j_ = np.arange(256)[None, :]
    band = (j_ >= i_) & (j_ <= i_ + 128)
    lmask[1] = np.where(band, 0.0, -1e9)
    lmask[0] = np.where(band & (j_ >= 128), 0.0, -1e9) if qv == 0 else lmask[1]
    g = lambda k: np.ascontiguousarray(np.asarray(inputs[k], dtype=np.float32))
    return {
        "xb": xb_, "xo": xo_, "rtcc": rtcc_, "rtns": rtns_,
        "cktcc": cktcc_, "cktns": cktns_, "cm": cmask.astype(bfdt),
        "lm": lmask.astype(bfdt),
        "Wq": g("L0_attn_Wq"), "Wk": g("L0_attn_Wk"), "Wv": g("L0_attn_Wv"),
        "Wlk": g("L0_attn_Wlk"), "Wlv": g("L0_attn_Wlv"), "Wo": g("L0_attn_Wo"),
        "Wgu": g("L0_ffn_w_gate_up"), "Wp": g("L0_ffn_w_proj"),
        "bqT": np.ascontiguousarray(g("L0_attn_bq").reshape(16, 128).T),
        "blkT": np.ascontiguousarray(g("L0_attn_blk").reshape(16, 128).T),
        "blvT": np.ascontiguousarray(g("L0_attn_blv").reshape(16, 128).T),
        "boT": np.ascontiguousarray(g("L0_attn_bo").reshape(16, 128).T),
        "bkc": g("L0_attn_bk").reshape(128, 1),
        "bvc": g("L0_attn_bv").reshape(128, 1),
        "bcc": np.ascontiguousarray(np.broadcast_to(
            g("L0_attn_bc").reshape(1, 1), (128, 1))),
        "qnc": g("L0_attn_qn_w").reshape(128, 1),
        "knc": g("L0_attn_kn_w").reshape(128, 1),
        "sinkTf": np.ascontiguousarray(g("L0_attn_sink_k").T),
        "sinkvRf": g("L0_attn_sink_v").reshape(1, NH * HD),
        "wcrepf": np.ascontiguousarray(np.broadcast_to(
            g("L0_attn_Wc").reshape(1, D), (128, D))),
    }


_CACHE = {}


def kernel(**inputs):
    if "nc" not in _CACHE:
        _CACHE["nc"] = build_program()
    nc = _CACHE["nc"]
    in_maps = [_host_inputs(inputs, c) for c in range(8)]
    res = run_bass_kernel_spmd(nc, in_maps, core_ids=list(range(8)))
    outs = [np.asarray(res.results[c]["out"], dtype=np.float32) for c in range(8)]
    return np.stack([np.concatenate(outs[0:4], 0), np.concatenate(outs[4:8], 0)], 0)


if __name__ == "__main__":
    build_program()
    print("BUILD OK")


# version 16 (final)
# speedup vs baseline: 1.0717x; 1.0189x over previous
import math

import numpy as np
import ml_dtypes
import concourse.bass as bass
import concourse.mybir as mybir
from concourse.bass_utils import run_bass_kernel_spmd

F32 = mybir.dt.float32
BF16 = mybir.dt.bfloat16
U16 = mybir.dt.uint16
AF = mybir.ActivationFunctionType
ALU = mybir.AluOpType
AX = mybir.AxisListType
bfdt = ml_dtypes.bfloat16

B, S, D = 2, 4096, 2048
NH, HD = 16, 128
R, C, WIN = 16, 256, 128
HID2, H = 16384, 8192
TOK, TEXT, NCH, PFX = 1024, 1152, 8, 32
EPS_LN, EPS_RMS = 1e-5, 1e-6
SCALE = 1.0 / math.sqrt(HD)
SB_BASE = 16512
SB_BUDGET = 229376


class Sched:
    ENGINES = ("sync", "gpsimd", "scalar", "vector", "tensor")
    STREAMS = {
        "setup_s": ("sync", 1), "setup_p": ("gpsimd", 1),
        "x": ("sync", 2), "entw": ("sync", 2), "stk": ("sync", 1),
        "whatw": ("gpsimd", 2),
        "aoscw": ("sync", 2), "ffnw": ("sync", 2), "x2spw": ("sync", 1),
        "outw": ("sync", 1),
        "aslab": ("sync", 2), "qcast_gu": ("gpsimd", 2), "qcast_p": ("gpsimd", 2),
        "qcast_a": ("gpsimd", 2),
        "wt": ("scalar", 3), "gslab": ("scalar", 2), "pslab": ("scalar", 2),
    }

    def __init__(self, nc):
        self.nc = nc
        self.items = {e: [] for e in self.ENGINES}
        self.csem = {e: nc.alloc_semaphore(f"cs_{e}") for e in self.ENGINES}
        self.ccount = {e: 0 for e in self.ENGINES}
        self.qsem = {}
        self.qcount = {}
        self.qrr = {}
        for s, (q, ns) in self.STREAMS.items():
            for i in range(ns):
                self.qsem[(s, i)] = nc.alloc_semaphore(f"dq_{s}{i}")
                self.qcount[(s, i)] = 0
            self.qrr[s] = 0

    def op(self, eng, fn, waits=(), sig=True):
        tok = None
        if sig:
            self.ccount[eng] += 1
            tok = (self.csem[eng], self.ccount[eng])
        wl = [w for w in waits if w is not None]
        self.items[eng].append(("op", fn, wl, tok))
        return tok

    def dma(self, stream, fn, waits=()):
        q, ns = self.STREAMS[stream]
        i = self.qrr[stream]
        self.qrr[stream] = (i + 1) % ns
        key = (stream, i)
        prev = (self.qsem[key], self.qcount[key]) if self.qcount[key] else None
        self.qcount[key] += 16
        tok = (self.qsem[key], self.qcount[key])
        wl = [w for w in list(waits) + [prev] if w is not None]
        self.items[q].append(("dma", fn, wl, tok))
        return tok

    def barrier(self, streams=()):
        toks = []
        for e in ("gpsimd", "scalar", "vector", "tensor"):
            pv = (self.csem[e], self.ccount[e])
            toks.append(self.op(e, lambda eng: eng.nop(), waits=[pv], sig=True))
        for s in streams:
            for i in range(self.STREAMS[s][1]):
                if self.qcount[(s, i)]:
                    toks.append((self.qsem[(s, i)], self.qcount[(s, i)]))
        return toks

    def build(self):
        nc = self.nc
        with nc.Block() as block:
            def make(eng_name):
                def body(eng):
                    seen = {}
                    for kind, fn, waits, tok in self.items[eng_name]:
                        for (sem, val) in waits:
                            key = id(sem)
                            if seen.get(key, 0) < val:
                                eng.wait_ge(sem, val)
                                seen[key] = val
                        inst = fn(eng)
                        if tok is not None:
                            inst.then_inc(tok[0], 16 if kind == "dma" else 1)
                return body
            block.sync(make("sync"))
            block.gpsimd(make("gpsimd"))
            block.scalar(make("scalar"))
            block.vector(make("vector"))
            block.tensor(make("tensor"))


class Bank:
    def __init__(self, tile):
        self.t = tile
        self.readers = []
        self.writer = None

    def wr_waits(self):
        w = list(self.readers) + ([self.writer] if self.writer else [])
        self.readers = []
        return w

    def set_writer(self, tok):
        assert tok is not None
        self.writer = tok

    def rd_waits(self):
        return [self.writer]

    def add_reader(self, tok):
        self.readers.append(tok)


def build_program():
    nc = bass.Bass("TRN2", target_bir_lowering=False)
    Sc = Sched(nc)

    def ext(name, shape, dtype=F32):
        return nc.dram_tensor(name, list(shape), dtype, kind="ExternalInput").ap()

    xb = ext("xb", [S, D])
    xo = ext("xo", [TEXT, D])
    rtcc = ext("rtcc", [64, TEXT]); rtns = ext("rtns", [64, TEXT])
    cktcc = ext("cktcc", [64, C]); cktns = ext("cktns", [64, C])
    cmv = ext("cm", [NCH, 128, C], BF16); lmv = ext("lm", [2, 128, 256], BF16)
    Wq = ext("Wq", [D, D]); Wk = ext("Wk", [D, HD]); Wv = ext("Wv", [D, HD])
    Wlk = ext("Wlk", [D, D]); Wlv = ext("Wlv", [D, D]); Wo = ext("Wo", [D, D])
    Wgu = ext("Wgu", [HID2, D]); Wp = ext("Wp", [D, H])
    bqT = ext("bqT", [128, 16]); blkT = ext("blkT", [128, 16])
    blvT = ext("blvT", [128, 16]); boT = ext("boT", [128, 16])
    bkc = ext("bkc", [128, 1]); bvc = ext("bvc", [128, 1]); bcc = ext("bcc", [128, 1])
    qnc = ext("qnc", [128, 1]); knc = ext("knc", [128, 1])
    sinkTf = ext("sinkTf", [128, 16]); sinkvRf = ext("sinkvRf", [1, NH * HD])
    wcrepf = ext("wcrepf", [128, D])

    out = nc.dram_tensor("out", [TOK, D], F32, kind="ExternalOutput").ap()

    wbf_gu = nc.dram_tensor("wbf_gu", [HID2, D], BF16, kind="Internal").ap()
    wbf_q = nc.dram_tensor("wbf_q", [D, D], BF16, kind="Internal").ap()
    wbf_lk = nc.dram_tensor("wbf_lk", [D, D], BF16, kind="Internal").ap()
    wbf_lv = nc.dram_tensor("wbf_lv", [D, D], BF16, kind="Internal").ap()
    wbf_o = nc.dram_tensor("wbf_o", [D, D], BF16, kind="Internal").ap()
    wbf_p = nc.dram_tensor("wbf_p", [D, H], BF16, kind="Internal").ap()
    whatT_gu = nc.dram_tensor("whatT_gu", [16, 128, HID2], BF16, kind="Internal").ap()
    whatT_p = nc.dram_tensor("whatT_p", [16, 64, 128, 128], BF16, kind="Internal").ap()
    x2sp = nc.dram_tensor("x2sp", [NCH, 128, D], F32, kind="Internal").ap()
    aosc = nc.dram_tensor("aosc", [NCH, 128, D], F32, kind="Internal").ap()
    ffnsc = nc.dram_tensor("ffnsc", [NCH, 128, D], F32, kind="Internal").ap()
    entsc_d = nc.dram_tensor("entsc_d", [PFX, 8, D], F32, kind="Internal").ap()

    idn_bf_c = nc.inline_tensor(np.eye(128, dtype=bfdt), name="idn_bf").ap()
    ones_col_c = nc.inline_tensor(np.ones((128, 1), dtype=bfdt), name="ones_col").ap()
    ones_row_c = nc.inline_tensor(np.ones((1, 128), dtype=np.float32), name="ones_row").ap()
    patt = np.zeros((128, 8), dtype=np.float32)
    for i in range(128):
        patt[i, i // 16] = 1.0
    patt8_c = nc.inline_tensor(patt, name="patt8").ap()
    patt8T_c = nc.inline_tensor(np.ascontiguousarray(patt.T), name="patt8T").ap()
    patt8b_c = nc.inline_tensor(patt.astype(bfdt), name="patt8b").ap()
    i2 = np.zeros((128, 2), dtype=bfdt); i2[:64, 0] = 1.0 / 128; i2[64:, 1] = 1.0 / 128
    I2_c = nc.inline_tensor(i2, name="I2").ap()
    e2 = np.zeros((2, 128), dtype=bfdt); e2[0, :64] = 1; e2[1, 64:] = 1
    E2_c = nc.inline_tensor(e2, name="E2").ap()
    sw = np.zeros((64, 64), dtype=bfdt)
    for i in range(64):
        sw[i, (i + 32) % 64] = 1
    SW64_c = nc.inline_tensor(sw, name="SW64").ap()
    cstv = np.zeros((128, 4), dtype=np.float32)
    cstv[:, 0] = EPS_LN; cstv[:, 1] = EPS_RMS; cstv[:, 2] = SCALE; cstv[:, 3] = 0.5
    cst_c = nc.inline_tensor(cstv, name="cstc").ap()

    cur = [SB_BASE]

    def sba(name, shape, dtype, at=None, track=True):
        if at is None:
            cur[0] = (cur[0] + 63) // 64 * 64
            at = cur[0]
            if track:
                cur[0] = at + int(np.prod(shape[1:])) * mybir.dt.size(dtype)
                assert cur[0] <= SB_BUDGET, f"SBUF overflow at {name}: {cur[0]}"
        return nc.alloc_sbuf_tensor_at(name, list(shape), dtype, offset=at)

    sb_idbf = sba("sb_idbf", [128, 128], BF16)
    sb_onesc = sba("sb_onesc", [128, 1], BF16)
    sb_onesr = sba("sb_onesr", [1, 128], F32)
    sb_patt8 = sba("sb_patt8", [128, 8], F32)
    sb_patt8T = sba("sb_patt8T", [8, 128], F32)
    sb_patt8b = sba("sb_patt8b", [128, 8], BF16)
    sb_I2 = sba("sb_I2", [128, 2], BF16)
    sb_E2 = sba("sb_E2", [2, 128], BF16)
    sb_SW = sba("sb_SW", [64, 64], BF16)
    sb_bqT = sba("sb_bqT", [128, 16], F32)
    sb_blkT = sba("sb_blkT", [128, 16], F32)
    sb_blvT = sba("sb_blvT", [128, 16], F32)
    sb_boT = sba("sb_boT", [128, 16], F32)
    sb_cols = sba("sb_cols", [128, 5], F32)
    sb_sinkT = sba("sb_sinkT", [128, 16], BF16)
    sb_sinkvR = sba("sb_sinkvR", [1, NH * HD], BF16)
    sb_rtcc = sba("sb_rtcc", [64, TEXT], BF16)
    sb_rtns = sba("sb_rtns", [64, TEXT], BF16)
    sb_ckcc = sba("sb_ckcc", [64, C], BF16)
    sb_ckns = sba("sb_ckns", [64, C], BF16)
    sb_ckr = sba("sb_ckr", [128, C], BF16)
    sb_cvT = sba("sb_cvT", [128, 2, 128], BF16)
    sb_cst = sba("sb_cst", [128, 4], F32)
    sb_cmask = sba("sb_cmask", [128, NCH, C], BF16)
    sb_lmask = sba("sb_lmask", [128, 2, 256], BF16)

    r1 = (cur[0] + 63) // 64 * 64
    cur[0] = r1
    sb_xnT = sba("sb_xnT", [128, 16, TEXT], BF16)
    r1b = (cur[0] + 63) // 64 * 64
    cur[0] = r1b
    sb_merged = sba("sb_merged", [128, 16, TOK], BF16)
    r1_end = cur[0]
    sb_wcrep = None
    sb_x2 = sba("sb_x2", [128, 2, D], F32, at=r1)
    sb_outst = sba("sb_outst", [128, 2, D], F32, at=r1 + 16384)
    sb_fo = sba("sb_fo", [128, D], F32, at=r1b)
    sb_fo2 = sba("sb_fo2", [128, D], F32, at=r1b + 8192)
    sb_ent = sba("sb_ent", [128, 2, D], F32, at=r1b)
    sb_entbf = sba("sb_entbf", [128, 2, D], BF16, at=r1b + 16384)
    sb_entsc = sba("sb_entsc", [8, D], F32, at=r1b + 24576)
    sb_hh = sba("sb_hh", [128, 64, 512], BF16, at=r1)

    def h_tile(j):
        return sb_hh[:, j, :]

    r2 = (r1_end + 63) // 64 * 64
    cur[0] = r2
    sb_xn2T = sba("sb_xn2T", [128, 16, TOK], BF16)
    r2_end = cur[0]
    a = r2
    sb_qf = sba("sb_qf", [128, TEXT], F32, at=a); a += TEXT * 4
    sb_rsqb = sba("sb_rsqb", [128, TEXT], F32, at=a); a += TEXT * 4
    sb_row = sba("sb_row", [1, TEXT], F32, at=a); a += TEXT * 4
    sb_qs = sba("sb_qs", [128, TEXT], BF16, at=a); a += TEXT * 2
    sb_usw = sba("sb_usw", [64, TEXT], BF16, at=a); a += TEXT * 2
    sb_u = sba("sb_u", [64, TEXT], BF16, at=a); a += TEXT * 2
    sb_v = sba("sb_v", [64, TEXT], BF16, at=a); a += TEXT * 2
    sb_lv = sba("sb_lv", [128, TEXT], BF16, at=a); a += TEXT * 2
    sb_qr2 = sba("sb_qr2", [128, 2, TOK], BF16, at=a); a += 2 * TOK * 2
    sb_mq = sba("sb_mq", [128, 1024], BF16, at=a); a += 1024 * 2
    assert a <= r2_end, (a, r2_end)

    cur[0] = r2_end
    sb_xc = sba("sb_xc", [128, 2, D], F32)
    sb_xnbf = sba("sb_xnbf", [128, 2, D], BF16)
    sb_dump = sba("sb_dump", [128, D], BF16)
    sb_stat = sba("sb_stat", [128, 26], F32)
    sc_at = (cur[0] + 63) // 64 * 64
    sb_sc = sba("sb_sc", [128, 2, 512], F32)
    sb_evf = sba("sb_evf", [128, 512], F32, at=sc_at)
    sb_wcrep = sba("sb_wcrep", [128, D], BF16, at=sc_at)
    sb_esc = sba("sb_esc", [128, 2, 512], BF16)
    sb_attbf = sba("sb_attbf", [128, 2, 512], BF16)
    sb_attT = sba("sb_attT", [128, 2, 512], BF16)
    sb_attS = sba("sb_attS", [1, 2, 128], BF16)
    sb_mtmp = sba("sb_mtmp", [128, 2, 128], F32)
    sb_snk = sba("sb_snk", [128, 2, 20], F32)
    sb_lkr2 = sba("sb_lkr2", [128, 2, TEXT], BF16)
    sb_lvT2 = sba("sb_lvT2", [128, 2, 9, 128], BF16)
    sb_ask = sba("sb_ask", [128, 2], BF16)
    sb_ckf = sba("sb_ckf", [128, C], F32, at=r2)
    sb_ckq = sba("sb_ckq", [128, C], BF16, at=r2 + 1024)
    sb_wsl = sba("sb_wsl", [128, 4, 16, 128], BF16)
    sb_wt = sba("sb_wt", [128, 2, 1024], BF16)
    sb_absq = sba("sb_absq", [128, 2, 1024], BF16)
    sb_shs = sba("sb_shs", [2, 4, 512], BF16)
    sb_shf = sba("sb_shf", [128, 2, 1024], BF16)
    sb_sgq = sba("sb_sgq", [128, 1024], U16)
    sb_what = sba("sb_what", [128, 3, 1024], BF16)

    assert cur[0] <= SB_BUDGET, cur[0]

    bkA0 = Bank(nc.alloc_psum_tensor("pbA0", [128, 512], F32))
    bkA1 = Bank(nc.alloc_psum_tensor("pbA1", [128, 512], F32))
    bkB0 = Bank(nc.alloc_psum_tensor("pbB0", [128, 512], F32))
    bkB1 = Bank(nc.alloc_psum_tensor("pbB1", [128, 512], F32))
    bkTB = Bank(nc.alloc_psum_tensor("ptrb", [128, 1024], BF16))
    bkTF = Bank(nc.alloc_psum_tensor("ptrf", [128, 512], F32))
    bkSC = Bank(nc.alloc_psum_tensor("psc", [128, 512], F32))
    bkMS = Bank(nc.alloc_psum_tensor("pmisc", [128, 512], F32))

    for dst, src in ((sb_idbf, idn_bf_c), (sb_onesc, ones_col_c), (sb_onesr, ones_row_c),
                     (sb_patt8, patt8_c), (sb_patt8T, patt8T_c), (sb_patt8b, patt8b_c),
                     (sb_I2, I2_c), (sb_E2, E2_c), (sb_SW, SW64_c), (sb_cst, cst_c),
                     (sb_bqT, bqT), (sb_blkT, blkT), (sb_blvT, blvT), (sb_boT, boT)):
        Sc.dma("setup_s", lambda e, d=dst, s=src: e.dma_start(out=d[:], in_=s))
    for i, src in enumerate((bkc, bvc, bcc, qnc, knc)):
        Sc.dma("setup_s", lambda e, i=i, s=src: e.dma_start(out=sb_cols[:, i:i + 1], in_=s))
    Sc.dma("setup_s", lambda e: e.dma_start(out=sb_cmask[:], in_=cmv.rearrange("c p n -> p c n")))
    Sc.dma("setup_s", lambda e: e.dma_start(out=sb_lmask[:], in_=lmv.rearrange("c p n -> p c n")))
    for dst, src in ((sb_sinkT, sinkTf), (sb_sinkvR, sinkvRf), (sb_wcrep, wcrepf),
                     (sb_rtcc, rtcc), (sb_rtns, rtns), (sb_ckcc, cktcc), (sb_ckns, cktns)):
        Sc.dma("setup_p", lambda e, d=dst, s=src: e.dma_start(out=d[:], in_=s))
    Sc.dma("setup_p", lambda e: e.dma_start(
        out=sb_wsl[:, 2, :, :], in_=Wk.rearrange("(k p) n -> p k n", p=128)))
    Sc.dma("setup_p", lambda e: e.dma_start(
        out=sb_wsl[:, 3, :, :], in_=Wv.rearrange("(k p) n -> p k n", p=128)))
    qcast_gu = [Sc.dma("qcast_gu", lambda e, rr=rr: e.dma_start(
        out=wbf_gu[rr * 4096:(rr + 1) * 4096, :], in_=Wgu[rr * 4096:(rr + 1) * 4096, :]))
        for rr in range(4)]
    qcast_p = [Sc.dma("qcast_p", lambda e, rr=rr: e.dma_start(
        out=wbf_p[rr * 512:(rr + 1) * 512, :], in_=Wp[rr * 512:(rr + 1) * 512, :]))
        for rr in range(4)]
    qcast_a = {}
    for nm, (dst, srcw) in {"q": (wbf_q, Wq), "lk": (wbf_lk, Wlk),
                            "lv": (wbf_lv, Wlv), "o": (wbf_o, Wo)}.items():
        qcast_a[nm] = Sc.dma("qcast_a", lambda e, d=dst, s=srcw: e.dma_start(
            out=d, in_=s))
    BAR0 = Sc.barrier(streams=("setup_s", "setup_p"))

    qslot = [0]
    qcons = [[], []]
    gsring = [[], [], [], []]
    shf_cons = [[], []]
    qstate = {}
    wtok = {}
    bguard = [[], []]
    wslot = [0]
    whatg = [[], [], []]

    def emit_qA(which, k, hb):
        sl = qslot[0]; qslot[0] ^= 1
        if which == "gu":
            src = wbf_gu[hb * 1024:(hb + 1) * 1024, k * 128:(k + 1) * 128]
            cast_tok = [qcast_gu[hb // 4]]
        else:
            src = wbf_p[k * 1024:(k + 1) * 1024, hb * 128:(hb + 1) * 128]
            cast_tok = [qcast_p[2 * k], qcast_p[2 * k + 1]]
        t_tr = Sc.dma("wt", lambda e, src=src, sl=sl: e.dma_start(
            out=sb_wt[:, sl, :], in_=src, transpose=True),
            waits=cast_tok + qcons[sl] + BAR0)
        qcons[sl] = []
        t_abs = Sc.op("vector", lambda e, sl=sl: e.tensor_scalar(
            sb_absq[:, sl, :].bitcast(U16), sb_wt[:, sl, :].bitcast(U16), 0x7FFF,
            None, ALU.bitwise_and), waits=[t_tr])
        qcons[sl].append(t_abs)
        tshf2 = []
        for q2 in range(2):
            fs = q2 * 512
            ssl = sl * 2 + q2
            tm = Sc.op("tensor", lambda e, fs=fs, sl=sl: e.matmul(
                bkMS.t[0:2, 0:512], sb_I2[:], sb_absq[:, sl, fs:fs + 512],
                start=True, stop=True), waits=bkMS.wr_waits() + [t_abs])
            bkMS.set_writer(tm)
            tse = Sc.op("scalar", lambda e, ssl=ssl: e.activation(
                sb_shs[:, ssl, :], bkMS.t[0:2, 0:512], AF.Identity),
                waits=bkMS.rd_waits() + gsring[ssl])
            bkMS.add_reader(tse)
            tex = Sc.op("tensor", lambda e, ssl=ssl: e.matmul(
                bkTF.t[:, 0:512], sb_E2[:], sb_shs[:, ssl, :], start=True, stop=True),
                waits=bkTF.wr_waits() + [tse])
            gsring[ssl] = [tex]
            bkTF.set_writer(tex)
            tshf = Sc.op("scalar", lambda e, sl=sl, fs=fs: e.activation(
                sb_shf[:, sl, fs:fs + 512], bkTF.t[:, 0:512], AF.Identity),
                waits=bkTF.rd_waits() + (shf_cons[sl] if q2 == 0 else [])),
            shf_cons[sl] = [] if q2 == 0 else shf_cons[sl]
            tshf = tshf[0]
            bkTF.add_reader(tshf)
            tshf2.append(tshf)
        qstate[sl] = (which, k, hb, tshf2)

    def emit_qB(sl):
        which, k, hb, tshf2 = qstate[sl]
        w3 = wslot[0]
        wslot[0] = (w3 + 1) % 3
        tmq = Sc.op("vector", lambda e, sl=sl: e.tensor_tensor(
            sb_mq[:], sb_absq[:, sl, :], sb_shf[:, sl, :], ALU.is_gt),
            waits=tshf2 + bguard[0])
        tsg = Sc.op("vector", lambda e, sl=sl: e.tensor_scalar(
            sb_sgq[:], sb_wt[:, sl, :].bitcast(U16), 0x8000, None,
            ALU.bitwise_and), waits=bguard[1])
        tss = Sc.op("vector", lambda e, sl=sl: e.tensor_tensor(
            sb_absq[:, sl, :].bitcast(U16), sb_sgq[:], sb_shf[:, sl, :].bitcast(U16),
            ALU.bitwise_or), waits=[tsg, tmq])
        shf_cons[sl] = [tmq, tss]
        twh = Sc.op("vector", lambda e, sl=sl, w3=w3: e.scalar_tensor_tensor(
            sb_what[:, w3, :], sb_mq[:], 2.0, sb_absq[:, sl, :],
            ALU.mult, ALU.mult), waits=[tss] + whatg[w3])
        bguard[0] = [twh]
        bguard[1] = [tss]
        outs = [twh]
        qcons[sl].extend([tmq, tsg, tss, twh])
        if which == "gu":
            dst = whatT_gu[k, :, hb * 1024:(hb + 1) * 1024]
        else:
            dst = whatT_p[k * 8:(k + 1) * 8, hb, :, :].rearrange("m p f -> p m f")
        t_out = Sc.dma("whatw", lambda e, dst=dst, w3=w3: e.dma_start(
            out=dst, in_=sb_what[:, w3, :].rearrange("p (j f) -> p j f", f=128)),
            waits=outs)
        whatg[w3] = [t_out]
        if which == "gu":
            for j in range(hb * 8, hb * 8 + 8):
                wtok[("gu", j, k)] = t_out
        else:
            wtok[("p", k, hb)] = t_out

    lnst = {"sl": 0, "cons": [[], []]}
    dump_guard = []

    def emit_ln(x_ap, out_bf_ap, deps):
        nonlocal dump_guard
        sl = lnst["sl"]; lnst["sl"] ^= 1
        st = sb_stat[:, sl * 10:sl * 10 + 10]
        cons = lnst["cons"][sl]
        t_sum = Sc.op("vector", lambda e: e.reduce_sum(st[:, 0:1], x_ap, AX.X),
                      waits=deps + cons)
        t_sq = Sc.op("scalar", lambda e: e.activation(
            sb_dump[:], x_ap, AF.Square, accum_out=st[:, 1:2]),
            waits=deps + cons + dump_guard)
        dump_guard = [t_sq]
        lnst["cons"][sl] = []
        cons = lnst["cons"][sl]
        t1 = Sc.op("vector", lambda e: e.tensor_scalar(
            st[:, 2:3], st[:, 0:1], 1.0 / D, None, ALU.mult), waits=[t_sum])
        t2 = Sc.op("vector", lambda e: e.tensor_scalar(
            st[:, 3:4], st[:, 1:2], 1.0 / D, None, ALU.mult), waits=[t_sq])
        t3 = Sc.op("vector", lambda e: e.tensor_tensor(
            st[:, 4:5], st[:, 2:3], st[:, 2:3], ALU.mult), waits=[t1])
        t4 = Sc.op("vector", lambda e: e.tensor_tensor(
            st[:, 5:6], st[:, 3:4], st[:, 4:5], ALU.subtract), waits=[t2, t3])
        t5 = Sc.op("scalar", lambda e: e.activation(
            st[:, 6:7], st[:, 5:6], AF.Sqrt, bias=sb_cst[:, 0:1]), waits=[t4])
        t6 = Sc.op("vector", lambda e: e.reciprocal(st[:, 7:8], st[:, 6:7]), waits=[t5])
        t7 = Sc.op("vector", lambda e: e.scalar_tensor_tensor(
            st[:, 8:9], st[:, 2:3], -1.0, st[:, 7:8], ALU.mult, ALU.mult), waits=[t6])
        tn = None
        if out_bf_ap is not None:
            tn = Sc.op("scalar", lambda e: e.activation(
                out_bf_ap, x_ap, AF.Identity, bias=st[:, 8:9], scale=st[:, 7:8]),
                waits=[t7])
            cons.append(tn)
        return tn, st, t7

    quant_iter = ([("gu", k, hb) for hb in range(16) for k in range(16)] +
                  [("p", k2, hb) for k2 in range(2) for hb in range(64)])
    qi = [0]
    qpend = []

    def pump_quant(n):
        for _ in range(n):
            if qi[0] < len(quant_iter):
                sl_next = qslot[0]
                emit_qA(*quant_iter[qi[0]])
                qi[0] += 1
                qpend.append(sl_next)
            if len(qpend) > 1:
                emit_qB(qpend.pop(0))

    def drain_quant():
        while qpend:
            emit_qB(qpend.pop(0))

    xc_cons = [[], []]
    xn_cons = [[], []]

    xnT_last = []
    x2_pend = {}

    def issue_x2(i):
        sl = i % 2
        x2_pend[i] = Sc.dma("x", lambda e, i=i, sl=sl: e.dma_start(
            out=sb_xc[:, sl, :], in_=xo[i * 128:(i + 1) * 128, :]),
            waits=xc_cons[sl] + BAR0)
        xc_cons[sl] = []

    issue_x2(0)
    for i in range(9):
        sl = i % 2
        t_x = x2_pend.pop(i)
        if i + 1 < 9:
            issue_x2(i + 1)
        t_xn, _, _ = emit_ln(sb_xc[:, sl, :], sb_xnbf[:, sl, :], [t_x] + xn_cons[sl])
        xn_cons[sl] = []
        xc_cons[sl].append(t_xn)
        for kd in range(16):
            tm = Sc.op("tensor", lambda e, kd=kd, sl=sl: e.transpose(
                bkTB.t[:, 0:128], sb_xnbf[:, sl, kd * 128:(kd + 1) * 128], sb_idbf[:]),
                waits=bkTB.wr_waits() + [t_xn])
            bkTB.set_writer(tm)
            te = Sc.op("vector", lambda e, kd=kd, i=i: e.tensor_copy(
                sb_xnT[:, kd, i * 128:(i + 1) * 128], bkTB.t[:, 0:128]),
                waits=bkTB.rd_waits())
            bkTB.add_reader(te)
            if kd == 15:
                xn_cons[sl].append(tm)
                xnT_last = [te]

    ent_dmas = []
    t_blk_g = [None]
    x_pend = {}

    def issue_x(i):
        sl = i % 2
        x_pend[i] = Sc.dma("x", lambda e, i=i, sl=sl: e.dma_start(
            out=sb_xc[:, sl, :], in_=xb[i * 128:(i + 1) * 128, :]),
            waits=xc_cons[sl] + BAR0)
        xc_cons[sl] = []

    issue_x(0)
    for i in range(PFX):
        sl = i % 2
        t_x = x_pend.pop(i)
        if i + 1 < PFX:
            issue_x(i + 1)
        t_xn, _, _ = emit_ln(sb_xc[:, sl, :], sb_xnbf[:, sl, :], [t_x] + xn_cons[sl])
        xn_cons[sl] = []
        xc_cons[sl].append(t_xn)
        t_wc = Sc.op("vector", lambda e, sl=sl: e.scalar_tensor_tensor(
            sb_dump[:], sb_xnbf[:, sl, :], 1.0, sb_wcrep[:], ALU.mult, ALU.mult,
            accum_out=sb_stat[:, 20:21]), waits=[t_xn] + dump_guard)
        dump_guard = [t_wc]
        if i >= 10:
            pump_quant(2)
        t_ew = Sc.op("scalar", lambda e: e.activation(
            sb_stat[:, 21:22], sb_stat[:, 20:21], AF.Exp, bias=sb_cols[:, 2:3]),
            waits=[t_wc])
        t_sm = Sc.op("tensor", lambda e: e.matmul(
            bkMS.t[0:8, 0:1], sb_patt8[:], sb_stat[:, 21:22], start=True, stop=True),
            waits=bkMS.wr_waits() + [t_ew])
        bkMS.set_writer(t_sm)
        t_ev = Sc.op("vector", lambda e: e.tensor_copy(
            sb_stat[0:8, 22:23], bkMS.t[0:8, 0:1]), waits=bkMS.rd_waits())
        bkMS.add_reader(t_ev)
        t_rc = Sc.op("vector", lambda e: e.reciprocal(
            sb_stat[0:8, 23:24], sb_stat[0:8, 22:23]), waits=[t_ev])
        t_blk = Sc.op("vector", lambda e: e.tensor_scalar(
            sb_attbf[:, 0, 1:9], sb_patt8b[:], sb_stat[:, 21:22], None, ALU.mult),
            waits=[t_ew, t_blk_g[0]] if t_blk_g[0] else [t_ew])
        t_blk_g[0] = None
        if i >= 10:
            pump_quant(1)
        ev_toks = []
        tm = None
        for q4 in range(4):
            bnk = bkSC if q4 % 2 == 0 else bkB0
            tm = Sc.op("tensor", lambda e, q4=q4, sl=sl, b=bnk: e.matmul(
                b.t[0:8, 0:512], sb_attbf[:, 0, 1:9],
                sb_xnbf[:, sl, q4 * 512:(q4 + 1) * 512], start=True, stop=True),
                waits=bnk.wr_waits() + [t_blk])
            bnk.set_writer(tm)
            te = Sc.op("vector", lambda e, q4=q4, b=bnk: e.tensor_scalar(
                sb_entsc[:, q4 * 512:(q4 + 1) * 512], b.t[0:8, 0:512],
                sb_stat[0:8, 23:24], None, ALU.mult),
                waits=bnk.rd_waits() + [t_rc] + (ent_dmas[-2:] if q4 == 0 else []))
            bnk.add_reader(te)
            ev_toks.append(te)
            t_blk_g[0] = tm
        xn_cons[sl] = [tm, t_wc]
        t_ed = Sc.dma("entw", lambda e, i=i: e.dma_start(
            out=entsc_d[i, :, :], in_=sb_entsc[:]), waits=ev_toks)
        ent_dmas.append(t_ed)
        xn_cons[sl].append(t_ed)
        if i >= 10:
            pump_quant(2)

    t_stk = Sc.dma("stk", lambda e: e.dma_start(
        out=sb_ent[:],
        in_=entsc_d.rearrange("c e d -> (c e) d").rearrange("(t p) d -> p t d", p=128)),
        waits=ent_dmas)
    t_eb = [Sc.op("vector", lambda e, tt=tt: e.tensor_copy(
        sb_entbf[:, tt, :], sb_ent[:, tt, :]), waits=[t_stk]) for tt in range(2)]
    entT_toks = []
    for kd in range(16):
        for tt in range(2):
            tm = Sc.op("tensor", lambda e, kd=kd, tt=tt: e.transpose(
                bkTB.t[:, 0:128], sb_entbf[:, tt, kd * 128:(kd + 1) * 128], sb_idbf[:]),
                waits=bkTB.wr_waits() + t_eb)
            bkTB.set_writer(tm)
            te = Sc.op("vector", lambda e, kd=kd, tt=tt: e.tensor_copy(
                sb_merged[:, kd, tt * 128:(tt + 1) * 128], bkTB.t[:, 0:128]),
                waits=bkTB.rd_waits())
            bkTB.add_reader(te)
            entT_toks.append(te)

    def emit_ckcv(is_ck):
        slab = 2 if is_ck else 3
        w0 = bkSC.wr_waits() + entT_toks + BAR0
        last = None
        for kd in range(16):
            last = Sc.op("tensor", lambda e, kd=kd, slab=slab: e.matmul(
                bkSC.t[:, 0:256], sb_wsl[:, slab, kd, :], sb_merged[:, kd, 0:256],
                start=(kd == 0), stop=(kd == 15)),
                waits=w0 if kd == 0 else [], sig=(kd == 15))
        bkSC.set_writer(last)
        bias = sb_cols[:, 0:1] if is_ck else sb_cols[:, 1:2]
        t_ev = Sc.op("scalar", lambda e, bias=bias: e.activation(
            sb_ckf[:], bkSC.t[:, 0:256], AF.Identity, bias=bias),
            waits=bkSC.rd_waits())
        bkSC.add_reader(t_ev)
        return t_ev, last

    t_ckf, ck_mm = emit_ckcv(True)
    t_cq = Sc.op("scalar", lambda e: e.activation(sb_ckq[:], sb_ckf[:], AF.Square),
                 waits=[t_ckf])
    t_cs = Sc.op("tensor", lambda e: e.matmul(
        bkMS.t[0:1, 0:256], sb_onesc[:], sb_ckq[:], start=True, stop=True),
        waits=bkMS.wr_waits() + [t_cq])
    bkMS.set_writer(t_cs)
    t_ce = Sc.op("vector", lambda e: e.tensor_scalar(
        sb_row[0:1, 0:256], bkMS.t[0:1, 0:256], 1.0 / HD, None, ALU.mult),
        waits=bkMS.rd_waits())
    bkMS.add_reader(t_ce)
    t_cs2 = Sc.op("scalar", lambda e: e.activation(
        sb_row[0:1, 256:512], sb_row[0:1, 0:256], AF.Sqrt, bias=sb_cst[0:1, 1:2]),
        waits=[t_ce])
    t_cr = Sc.op("vector", lambda e: e.reciprocal(
        sb_row[0:1, 512:768], sb_row[0:1, 256:512]), waits=[t_cs2])
    t_cb = Sc.op("tensor", lambda e: e.matmul(
        bkTF.t[:, 0:256], sb_onesr[:], sb_row[0:1, 512:768], start=True, stop=True),
        waits=bkTF.wr_waits() + [t_cr])
    bkTF.set_writer(t_cb)
    t_cbe = Sc.op("vector", lambda e: e.tensor_copy(sb_rsqb[:, 0:256], bkTF.t[:, 0:256]),
                  waits=bkTF.rd_waits())
    bkTF.add_reader(t_cbe)
    t_ct = Sc.op("vector", lambda e: e.scalar_tensor_tensor(
        sb_ckr[:], sb_ckf[:], sb_cols[:, 4:5], sb_rsqb[:, 0:256], ALU.mult, ALU.mult),
        waits=[t_cbe])
    t_csw = Sc.op("tensor", lambda e: e.matmul(
        bkMS.t[0:64, 0:256], sb_SW[:], sb_ckr[0:64, :], start=True, stop=True),
        waits=bkMS.wr_waits() + [t_ct])
    bkMS.set_writer(t_csw)
    t_cswe = Sc.op("vector", lambda e: e.tensor_copy(sb_usw[:, 0:256], bkMS.t[0:64, 0:256]),
                   waits=bkMS.rd_waits())
    bkMS.add_reader(t_cswe)
    t_cu = Sc.op("vector", lambda e: e.tensor_tensor(
        sb_u[:, 0:256], sb_ckr[0:64, :], sb_ckcc[:], ALU.mult), waits=[t_ct])
    t_cv2 = Sc.op("vector", lambda e: e.tensor_tensor(
        sb_v[:, 0:256], sb_usw[:, 0:256], sb_ckns[:], ALU.mult), waits=[t_cswe])
    t_ckdone = Sc.op("vector", lambda e: e.tensor_tensor(
        sb_ckr[0:64, :], sb_u[:, 0:256], sb_v[:, 0:256], ALU.add),
        waits=[t_cu, t_cv2])
    t_cvf, cv_mm = emit_ckcv(False)
    t_cvb = Sc.op("vector", lambda e: e.tensor_copy(sb_ckq[:], sb_ckf[:]), waits=[t_cvf])
    cvT_toks = []
    for tt in range(2):
        tm = Sc.op("tensor", lambda e, tt=tt: e.transpose(
            bkTB.t[:, 0:128], sb_ckq[:, tt * 128:(tt + 1) * 128], sb_idbf[:]),
            waits=bkTB.wr_waits() + [t_cvb])
        bkTB.set_writer(tm)
        te = Sc.op("vector", lambda e, tt=tt: e.tensor_copy(
            sb_cvT[:, tt, :], bkTB.t[:, 0:128]), waits=bkTB.rd_waits())
        bkTB.add_reader(te)
        cvT_toks.append(te)
    ckcv_toks = [t_ckdone] + cvT_toks
    slab_seed = {2: [ck_mm], 3: [cv_mm]}

    BAR1 = Sc.barrier()

    slab_cons = {i: slab_seed.get(i, []) for i in range(4)}
    slab_rot = [0]
    merged_toks = []
    def next_slab():
        s = slab_rot[0]
        slab_rot[0] = (slab_rot[0] + 1) % 4
        return s

    def emit_proj(Wsrc, cast_tok, h, bias_col, c0, ncol, out_f32, deps, out_bf=None,
                  bank_fixed=None):
        slab = next_slab()
        t_sl = Sc.dma("aslab", lambda e, slab=slab: e.dma_start(
            out=sb_wsl[:, slab, :, :],
            in_=Wsrc[:, h * 128:(h + 1) * 128].rearrange("(k p) n -> p k n", p=128)),
            waits=slab_cons[slab] + [cast_tok])
        slab_cons[slab] = []
        ev_toks = []
        for nb in range((ncol + 511) // 512):
            f0 = nb * 512
            cw = min(512, ncol - f0)
            bank = bank_fixed or (bkA0 if nb % 2 == 0 else bkA1)
            w0 = bank.wr_waits() + [t_sl] + deps
            last = None
            for kd in range(16):
                last = Sc.op("tensor", lambda e, kd=kd, f0=f0, cw=cw, slab=slab, b=bank: e.matmul(
                    b.t[:, 0:cw], sb_wsl[:, slab, kd, :],
                    sb_xnT[:, kd, c0 + f0:c0 + f0 + cw],
                    start=(kd == 0), stop=(kd == 15)),
                    waits=w0 if kd == 0 else [], sig=(kd == 15))
            bank.set_writer(last)
            slab_cons[slab].append(last)
            dst = out_f32 if out_bf is None else out_bf
            tev = Sc.op("scalar", lambda e, f0=f0, cw=cw, dst=dst, b=bank: e.activation(
                dst[:, f0:f0 + cw], b.t[:, 0:cw], AF.Identity, bias=bias_col),
                waits=bank.rd_waits())
            bank.add_reader(tev)
            ev_toks.append(tev)
        return ev_toks

    def emit_rms_rope(ncol, toff, gain_col, out_ap, deps):
        nonlocal dump_guard
        t_sq = Sc.op("scalar", lambda e: e.activation(
            sb_dump[:, 0:ncol], sb_qf[:, 0:ncol], AF.Square),
            waits=deps + dump_guard)
        dump_guard = []
        rs_toks = []
        for nb in range((ncol + 511) // 512):
            f0 = nb * 512; cw = min(512, ncol - f0)
            tm = Sc.op("tensor", lambda e, f0=f0, cw=cw: e.matmul(
                bkMS.t[0:1, 0:cw], sb_onesc[:], sb_dump[:, f0:f0 + cw],
                start=True, stop=True), waits=bkMS.wr_waits() + [t_sq])
            bkMS.set_writer(tm)
            dump_guard.append(tm)
            te = Sc.op("vector", lambda e, f0=f0, cw=cw: e.tensor_scalar(
                sb_row[0:1, f0:f0 + cw], bkMS.t[0:1, 0:cw], 1.0 / HD, None, ALU.mult),
                waits=bkMS.rd_waits())
            bkMS.add_reader(te)
            rs_toks.append(te)
        t_rc2 = Sc.op("scalar", lambda e: e.activation(
            sb_row[0:1, 0:ncol], sb_row[0:1, 0:ncol], AF.Sqrt, bias=sb_cst[0:1, 1:2]),
            waits=rs_toks)
        bc_toks = []
        for nb in range((ncol + 511) // 512):
            f0 = nb * 512; cw = min(512, ncol - f0)
            tm = Sc.op("tensor", lambda e, f0=f0, cw=cw: e.matmul(
                bkTF.t[:, 0:cw], sb_onesr[:], sb_row[0:1, f0:f0 + cw],
                start=True, stop=True), waits=bkTF.wr_waits() + [t_rc2])
            bkTF.set_writer(tm)
            te = Sc.op("vector", lambda e, f0=f0, cw=cw: e.reciprocal(
                sb_rsqb[:, f0:f0 + cw], bkTF.t[:, 0:cw]), waits=bkTF.rd_waits())
            bkTF.add_reader(te)
            bc_toks.append(te)
        t_t = Sc.op("vector", lambda e: e.scalar_tensor_tensor(
            sb_qs[:, 0:ncol], sb_qf[:, 0:ncol], gain_col, sb_rsqb[:, 0:ncol],
            ALU.mult, ALU.mult), waits=bc_toks + deps)
        sw_toks = []
        for nb in range((ncol + 511) // 512):
            f0 = nb * 512; cw = min(512, ncol - f0)
            tm = Sc.op("tensor", lambda e, f0=f0, cw=cw: e.matmul(
                bkMS.t[0:64, 0:cw], sb_SW[:], sb_qs[0:64, f0:f0 + cw],
                start=True, stop=True), waits=bkMS.wr_waits() + [t_t])
            bkMS.set_writer(tm)
            te = Sc.op("vector", lambda e, f0=f0, cw=cw: e.tensor_copy(
                sb_usw[:, f0:f0 + cw], bkMS.t[0:64, 0:cw]), waits=bkMS.rd_waits())
            bkMS.add_reader(te)
            sw_toks.append(te)
        t_u = Sc.op("vector", lambda e: e.tensor_tensor(
            sb_u[:, 0:ncol], sb_qs[0:64, 0:ncol], sb_rtcc[:, toff:toff + ncol],
            ALU.mult), waits=[t_t])
        t_v = Sc.op("vector", lambda e: e.tensor_tensor(
            sb_v[:, 0:ncol], sb_usw[:, 0:ncol], sb_rtns[:, toff:toff + ncol],
            ALU.mult), waits=sw_toks)
        t_o1 = Sc.op("vector", lambda e: e.tensor_tensor(
            out_ap[0:64, 0:ncol], sb_u[:, 0:ncol], sb_v[:, 0:ncol], ALU.add),
            waits=[t_u, t_v])
        t_rest = Sc.op("vector", lambda e: e.tensor_copy(
            out_ap[64:128, 0:ncol], sb_qs[64:128, 0:ncol]), waits=[t_t])
        return [t_o1, t_rest]

    prev_qr_cons = [[], []]
    prev_lkr_cons = [[], []]
    prev_lv_cons = [[], []]
    prev_lvT_cons = [[], []]
    prev_snk_cons = [[], []]
    ropeg = {"qs": [], "u": [], "row": []}
    front_state = [None, None]

    def front_gen(h):
        hs = h % 2
        deps0 = (BAR1 if h == 0 else []) + prev_qr_cons[hs]
        prev_qr_cons[hs] = []
        if h > 0:
            deps0 = deps0 + front_state[1 - hs]["qf_free"]
        q_ev = emit_proj(wbf_q, qcast_a["q"], h, sb_bqT[:, h:h + 1], 128, TOK,
                         sb_qf, deps0, bank_fixed=bkA0)
        yield
        qr_toks = emit_rms_rope(TOK, 128, sb_cols[:, 3:4], sb_qr2[:, hs, :], q_ev)
        yield
        lk_ev = emit_proj(wbf_lk, qcast_a["lk"], h, sb_blkT[:, h:h + 1], 0, TEXT,
                          sb_qf, qr_toks + prev_lkr_cons[hs], bank_fixed=bkA0)
        prev_lkr_cons[hs] = []
        yield
        lkr_toks = emit_rms_rope(TEXT, 0, sb_cols[:, 4:5], sb_lkr2[:, hs, :], lk_ev)
        yield
        lv_ev = emit_proj(wbf_lv, qcast_a["lv"], h, sb_blvT[:, h:h + 1], 0, TEXT,
                          None, lkr_toks + prev_lv_cons[hs], out_bf=sb_lv,
                          bank_fixed=bkA0)
        prev_lv_cons[hs] = []
        yield
        lvT_toks = []
        for j in range(9):
            tm = Sc.op("tensor", lambda e, j=j: e.transpose(
                bkTB.t[:, 0:128], sb_lv[:, j * 128:(j + 1) * 128], sb_idbf[:]),
                waits=bkTB.wr_waits() + lv_ev)
            bkTB.set_writer(tm)
            te = Sc.op("vector", lambda e, j=j, hs=hs: e.tensor_copy(
                sb_lvT2[:, hs, j, :], bkTB.t[:, 0:128]),
                waits=bkTB.rd_waits() + (prev_lvT_cons[hs] if j == 0 else []))
            bkTB.add_reader(te)
            lvT_toks.append(te)
            prev_lv_cons[hs].append(tm)
        prev_lvT_cons[hs] = []
        yield
        t_sr0 = Sc.op("tensor", lambda e, h=h, hs=hs: e.matmul(
            bkMS.t[0:1, 0:512], sb_sinkT[:, h:h + 1], sb_qr2[:, hs, 0:512],
            start=True, stop=True), waits=bkMS.wr_waits() + qr_toks)
        bkMS.set_writer(t_sr0)
        t_se0 = Sc.op("vector", lambda e: e.tensor_scalar(
            sb_row[0:1, 0:512], bkMS.t[0:1, 0:512], float(SCALE), None, ALU.mult),
            waits=bkMS.rd_waits() + ropeg["row"])
        bkMS.add_reader(t_se0)
        t_sr1 = Sc.op("tensor", lambda e, h=h, hs=hs: e.matmul(
            bkMS.t[0:1, 0:512], sb_sinkT[:, h:h + 1], sb_qr2[:, hs, 512:1024],
            start=True, stop=True), waits=bkMS.wr_waits())
        bkMS.set_writer(t_sr1)
        t_se1 = Sc.op("vector", lambda e: e.tensor_scalar(
            sb_row[0:1, 512:1024], bkMS.t[0:1, 0:512], float(SCALE), None, ALU.mult),
            waits=bkMS.rd_waits())
        bkMS.add_reader(t_se1)
        t_sbf = Sc.op("vector", lambda e: e.tensor_copy(
            sb_u[0:1, 0:TOK], sb_row[0:1, 0:TOK]),
            waits=[t_se0, t_se1] + ropeg["u"] + lkr_toks)
        snk_evs = []
        tmx = None
        for cch in range(NCH):
            tmx = Sc.op("tensor", lambda e, cch=cch: e.transpose(
                bkTB.t[:, 896:897], sb_u[0:1, cch * 128:(cch + 1) * 128],
                sb_idbf[0:1, 0:1]),
                waits=bkTB.wr_waits() + [t_sbf] + snk_evs[-1:])
            bkTB.set_writer(tmx)
            tev = Sc.op("vector", lambda e, cch=cch, hs=hs: e.tensor_copy(
                sb_snk[:, hs, cch:cch + 1], bkTB.t[:, 896:897]),
                waits=bkTB.rd_waits() + (prev_snk_cons[hs] if cch == 0 else []))
            bkTB.add_reader(tev)
            snk_evs.append(tev)
        prev_snk_cons[hs] = []
        t_nsk = Sc.op("vector", lambda e, hs=hs: e.tensor_scalar(
            sb_snk[:, hs, 8:16], sb_snk[:, hs, 0:8], -1.0, None, ALU.mult),
            waits=snk_evs)
        ropeg["row"] = [t_sbf]
        ropeg["u"] = [tmx]
        front_state[hs] = {"qr": qr_toks, "lkr": lkr_toks, "lvT": lvT_toks,
                           "nsk": [t_nsk], "qf_free": lkr_toks}
        yield

    def run_gen(g):
        if g is not None:
            try:
                next(g)
                return g
            except StopIteration:
                return None
        return None

    merged_toks = merged_toks if False else merged_toks
    chain_g = [[], []]
    esg = [[], []]
    att_g = [[], []]
    askg = [[], []]
    mtg = [[], []]
    g0 = front_gen(0)
    for _ in range(10):
        g0 = run_gen(g0)
        if g0 is None:
            break

    for h in range(NH):
        hs = h % 2
        fs = front_state[hs]
        qr_toks = fs["qr"]; lkr_toks = fs["lkr"]; lvT_toks = fs["lvT"]
        nsk_toks = fs["nsk"]
        gen_next = front_gen(h + 1) if h + 1 < NH else None

        def emit_scores(cch):
            q0 = cch * 128
            par = cch % 2
            P = bkSC if par == 0 else bkA1
            mm1 = Sc.op("tensor", lambda e, q0=q0, b=P, hs=hs: e.matmul(
                b.t[:, 0:256], sb_qr2[:, hs, q0:q0 + 128], sb_ckr[:],
                start=True, stop=False),
                waits=P.wr_waits() + qr_toks + ckcv_toks + lkr_toks, sig=False)
            mm2 = Sc.op("tensor", lambda e, q0=q0, b=P, hs=hs: e.matmul(
                b.t[:, 256:512], sb_qr2[:, hs, q0:q0 + 128],
                sb_lkr2[:, hs, q0:q0 + 256], start=False, stop=True), waits=[])
            P.set_writer(mm2)
            prev_qr_cons[hs].append(mm2)
            prev_lkr_cons[hs].append(mm2)
            return P

        def emit_chain(cch, P):
            q0 = cch * 128
            par = cch % 2
            X = bkB0 if par == 0 else bkB1
            stc = sb_stat[:, 14 + par * 6: 20 + par * 6]
            midx = 0 if cch == 0 else 1
            t_ev0 = Sc.op("vector", lambda e, cch=cch, par=par, b=P: e.scalar_tensor_tensor(
                sb_sc[:, par, 0:256], b.t[:, 0:256], float(SCALE), sb_cmask[:, cch, :],
                ALU.mult, ALU.add), waits=P.rd_waits() + chain_g[par])
            t_ev = Sc.op("vector", lambda e, midx=midx, par=par, b=P: e.scalar_tensor_tensor(
                sb_sc[:, par, 256:512], b.t[:, 256:512], float(SCALE),
                sb_lmask[:, midx, :], ALU.mult, ALU.add), waits=[t_ev0])
            P.add_reader(t_ev)
            t_mx = Sc.op("vector", lambda e, par=par, stc=stc: e.reduce_max(
                stc[:, 0:2], sb_sc[:, par, :].rearrange("p (g w) -> p g w", w=256),
                AX.X, negate=True), waits=[t_ev])
            t_mf = Sc.op("vector", lambda e, cch=cch, stc=stc, hs=hs: e.tensor_scalar(
                stc[:, 0:1], stc[:, 0:1], sb_snk[:, hs, 8 + cch:9 + cch], None,
                ALU.min), waits=[t_mx] + nsk_toks)
            t_e1 = Sc.op("scalar", lambda e, par=par, stc=stc: e.activation(
                sb_esc[:, par, 0:256], sb_sc[:, par, 0:256], AF.Exp,
                bias=stc[:, 0:1], accum_out=stc[:, 2:3]), waits=[t_mf])
            t_e2 = Sc.op("scalar", lambda e, par=par, stc=stc: e.activation(
                sb_esc[:, par, 256:512], sb_sc[:, par, 256:512], AF.Exp,
                bias=stc[:, 1:2], accum_out=stc[:, 3:4]), waits=[t_mx, t_e1])
            t_es = Sc.op("scalar", lambda e, cch=cch, par=par, stc=stc, hs=hs: e.activation(
                sb_snk[:, hs, 16 + par:17 + par], sb_snk[:, hs, cch:cch + 1], AF.Exp,
                bias=stc[:, 0:1]), waits=[t_mf] + esg[par])
            t_sa = Sc.op("vector", lambda e, par=par, stc=stc, hs=hs: e.tensor_tensor(
                stc[:, 2:3], stc[:, 2:3], sb_snk[:, hs, 16 + par:17 + par], ALU.add),
                waits=[t_e1, t_es])
            t_rc = Sc.op("vector", lambda e, stc=stc: e.reciprocal(
                stc[:, 4:6], stc[:, 2:4]), waits=[t_sa, t_e2])
            t_a1 = Sc.op("scalar", lambda e, par=par, stc=stc: e.activation(
                sb_attbf[:, par, 0:256], sb_esc[:, par, 0:256], AF.Identity,
                scale=stc[:, 4:5]), waits=[t_rc] + att_g[par])
            t_a2 = Sc.op("scalar", lambda e, par=par, stc=stc: e.activation(
                sb_attbf[:, par, 256:512], sb_esc[:, par, 256:512], AF.Identity,
                scale=stc[:, 5:6]), waits=[t_rc, t_a1])
            t_ak = Sc.op("vector", lambda e, par=par, stc=stc, hs=hs: e.scalar_tensor_tensor(
                sb_ask[:, par:par + 1], sb_snk[:, hs, 16 + par:17 + par], 1.0,
                stc[:, 4:5], ALU.mult, ALU.mult), waits=[t_rc] + askg[par])
            prev_snk_cons[hs].extend([t_mf, t_es, t_sa, t_ak])
            esg[par] = [t_sa]
            tt = None
            for r in range(4):
                tt = Sc.op("tensor", lambda e, par=par, r=r: e.transpose(
                    bkTB.t[:, r * 128:(r + 1) * 128],
                    sb_attbf[:, par, r * 128:(r + 1) * 128], sb_idbf[:]),
                    waits=(bkTB.wr_waits() if r == 0 else []) +
                    ([t_a1] if r < 2 else [t_a2]), sig=(r == 3))
            tt5 = Sc.op("tensor", lambda e, par=par: e.transpose(
                bkTB.t[0:1, 512:640], sb_ask[:, par:par + 1], sb_idbf[:]),
                waits=[t_ak])
            bkTB.set_writer(tt5)
            t_aT = Sc.op("vector", lambda e, par=par: e.tensor_copy(
                sb_attT[:, par, :], bkTB.t[:, 0:512]), waits=bkTB.rd_waits())
            t_aS = Sc.op("vector", lambda e, par=par: e.tensor_copy(
                sb_attS[:, par, :], bkTB.t[0:1, 512:640]), waits=bkTB.rd_waits())
            bkTB.add_reader(t_aT)
            bkTB.add_reader(t_aS)
            att_g[par] = [tt, tt5]
            askg[par] = [tt5]
            c1 = Sc.op("tensor", lambda e, par=par, b=X: e.matmul(
                b.t[:, 0:128], sb_cvT[:, 0, :], sb_attT[:, par, 0:128],
                start=True, stop=False), waits=X.wr_waits() + [t_aT], sig=False)
            c2 = Sc.op("tensor", lambda e, par=par, b=X: e.matmul(
                b.t[:, 0:128], sb_cvT[:, 1, :], sb_attT[:, par, 128:256],
                start=False, stop=False), waits=[], sig=False)
            c3 = Sc.op("tensor", lambda e, par=par, h=h, b=X: e.matmul(
                b.t[:, 0:128], sb_sinkvR[:, h * 128:(h + 1) * 128],
                sb_attS[:, par, :], start=False, stop=False), waits=[t_aS], sig=False)
            l1 = Sc.op("tensor", lambda e, par=par, cch=cch, b=X, hs=hs: e.matmul(
                b.t[:, 128:256], sb_lvT2[:, hs, cch, :], sb_attT[:, par, 256:384],
                start=False, stop=False), waits=lvT_toks, sig=False)
            l2 = Sc.op("tensor", lambda e, par=par, cch=cch, b=X, hs=hs: e.matmul(
                b.t[:, 128:256], sb_lvT2[:, hs, cch + 1, :], sb_attT[:, par, 384:512],
                start=False, stop=True), waits=[])
            X.set_writer(l2)
            prev_lvT_cons[hs].append(l2)
            t_m1 = Sc.op("scalar", lambda e, par=par, b=X: e.activation(
                sb_mtmp[:, par, :], b.t[:, 0:128], AF.Identity,
                scale=sb_cst[:, 3:4]), waits=X.rd_waits() + mtg[par])
            X.add_reader(t_m1)
            t_m2 = Sc.op("vector", lambda e, h=h, q0=q0, par=par, b=X: e.scalar_tensor_tensor(
                sb_merged[:, h, q0:q0 + 128], b.t[:, 128:256], 0.5,
                sb_mtmp[:, par, :], ALU.mult, ALU.add),
                waits=X.rd_waits() + [t_m1])
            X.add_reader(t_m2)
            mtg[par] = [t_m2]
            chain_g[par] = [t_e2, t_aT]
            merged_toks.append(t_m2)

        pend = []
        for cch in range(NCH + 1):
            if cch < NCH:
                pend.append((cch, emit_scores(cch)))
            if len(pend) > (1 if cch < NCH else 0):
                c0_, P0_ = pend.pop(0)
                emit_chain(c0_, P0_)
            gen_next = run_gen(gen_next)
            pump_quant(2)
        while gen_next is not None:
            gen_next = run_gen(gen_next)

    drain_quant()

    BAR2 = Sc.barrier()

    ao_dmas = []
    evf_guard = []
    for m in range(16):
        slab = next_slab()
        t_sl = Sc.dma("aslab", lambda e, m=m, slab=slab: e.dma_start(
            out=sb_wsl[:, slab, :, :],
            in_=wbf_o[:, m * 128:(m + 1) * 128].rearrange("(k p) n -> p k n", p=128)),
            waits=slab_cons[slab] + [qcast_a["o"]])
        slab_cons[slab] = []
        for th in range(2):
            bank = bkA0 if th == 0 else bkA1
            w0 = bank.wr_waits() + [t_sl] + merged_toks[-1:]
            last = None
            for kd in range(16):
                last = Sc.op("tensor", lambda e, kd=kd, th=th, slab=slab, b=bank: e.matmul(
                    b.t[:, 0:512], sb_wsl[:, slab, kd, :],
                    sb_merged[:, kd, th * 512:(th + 1) * 512],
                    start=(kd == 0), stop=(kd == 15)),
                    waits=w0 if kd == 0 else [], sig=(kd == 15))
            bank.set_writer(last)
            slab_cons[slab].append(last)
            t_ev = Sc.op("scalar", lambda e, m=m, b=bank: e.activation(
                sb_evf[:], b.t[:, 0:512], AF.Identity, bias=sb_boT[:, m:m + 1]),
                waits=bank.rd_waits() + evf_guard)
            bank.add_reader(t_ev)
            t_evb = Sc.op("vector", lambda e: e.tensor_copy(
                sb_dump[:, 0:512], sb_evf[:]), waits=[t_ev] + dump_guard)
            evf_guard = [t_evb]
            dump_guard = []
            for p4 in range(4):
                cch = th * 4 + p4
                tm = Sc.op("tensor", lambda e, p4=p4: e.transpose(
                    bkTB.t[:, 0:128], sb_dump[:, p4 * 128:(p4 + 1) * 128], sb_idbf[:]),
                    waits=bkTB.wr_waits() + [t_evb])
                bkTB.set_writer(tm)
                dump_guard.append(tm)
                te = Sc.op("vector", lambda e: e.tensor_copy(
                    sb_mtmp[:, 0, :], bkTB.t[:, 0:128]), waits=bkTB.rd_waits())
                bkTB.add_reader(te)
                td = Sc.dma("aoscw", lambda e, cch=cch, m=m: e.dma_start(
                    out=aosc[cch, :, m * 128:(m + 1) * 128], in_=sb_mtmp[:, 0, :]),
                    waits=[te])
                bkTB.add_reader(td)
                ao_dmas.append(td)

    x2_cons = [[], []]
    xn2T_last = []
    x2sp_toks = []
    fo_guard = []
    for cch in range(NCH):
        sl = cch % 2
        t_ldao = Sc.dma("x", lambda e, cch=cch: e.dma_start(
            out=sb_fo[:], in_=aosc[cch, :, :]), waits=ao_dmas[-2:] + fo_guard)
        t_x = Sc.dma("x", lambda e, cch=cch, sl=sl: e.dma_start(
            out=sb_xc[:, sl, :], in_=xo[(cch + 1) * 128:(cch + 2) * 128, :]),
            waits=xc_cons[sl])
        xc_cons[sl] = []
        _, st, t7 = emit_ln(sb_fo[:], None, [t_ldao])
        t_nf = Sc.op("vector", lambda e, st=st: e.tensor_scalar(
            sb_fo[:], sb_fo[:], st[:, 7:8], st[:, 8:9], ALU.mult, ALU.add),
            waits=[t7])
        t_x2 = Sc.op("vector", lambda e, sl=sl: e.tensor_tensor(
            sb_x2[:, sl, :], sb_fo[:], sb_xc[:, sl, :], ALU.add),
            waits=[t_nf, t_x] + x2_cons[sl])
        fo_guard = [t_x2]
        x2_cons[sl] = []
        xc_cons[sl].append(t_x2)
        t_sp = Sc.dma("x2spw", lambda e, cch=cch, sl=sl: e.dma_start(
            out=x2sp[cch, :, :], in_=sb_x2[:, sl, :]), waits=[t_x2])
        x2_cons[sl].append(t_sp)
        x2sp_toks.append(t_sp)
        t_xn2, _, _ = emit_ln(sb_x2[:, sl, :], sb_xnbf[:, sl, :],
                              [t_x2] + xn_cons[sl])
        xn_cons[sl] = []
        x2_cons[sl].append(t_xn2)
        for kd in range(16):
            tm = Sc.op("tensor", lambda e, kd=kd, sl=sl: e.transpose(
                bkTB.t[:, 0:128], sb_xnbf[:, sl, kd * 128:(kd + 1) * 128], sb_idbf[:]),
                waits=bkTB.wr_waits() + [t_xn2])
            bkTB.set_writer(tm)
            te = Sc.op("vector", lambda e, kd=kd, cch=cch: e.tensor_copy(
                sb_xn2T[:, kd, cch * 128:(cch + 1) * 128], bkTB.t[:, 0:128]),
                waits=bkTB.rd_waits())
            bkTB.add_reader(te)
            if kd == 15:
                xn_cons[sl].append(tm)
                xn2T_last = [te]
    BAR3 = Sc.barrier(streams=("x2spw", "aoscw"))

    ffn_dmas = []
    gsl_cons = {i: [] for i in range(4)}
    evf_guard2 = []
    for hf in range(2):
        hoff = hf * 512
        h_last = []
        for jj in range(64):
            sl = jj % 2
            jg, ju = jj, jj + 64
            t_sg = Sc.dma("gslab", lambda e, jg=jg, sl=sl: e.dma_start(
                out=sb_wsl[:, sl * 2, :, :],
                in_=whatT_gu[:, :, jg * 128:(jg + 1) * 128].rearrange("k p f -> p k f")),
                waits=gsl_cons[sl * 2] + [wtok[("gu", jg, kk)] for kk in range(16)] +
                (BAR3 if jj < 2 and hf == 0 else []))
            t_su = Sc.dma("gslab", lambda e, ju=ju, sl=sl: e.dma_start(
                out=sb_wsl[:, sl * 2 + 1, :, :],
                in_=whatT_gu[:, :, ju * 128:(ju + 1) * 128].rearrange("k p f -> p k f")),
                waits=gsl_cons[sl * 2 + 1] + [wtok[("gu", ju, kk)] for kk in range(16)])
            gsl_cons[sl * 2] = []
            gsl_cons[sl * 2 + 1] = []
            bg = bkA0 if sl == 0 else bkA1
            bu = bkB0 if sl == 0 else bkB1
            w0 = bg.wr_waits() + [t_sg] + (xn2T_last if jj == 0 else [])
            lastg = None
            for kd in range(16):
                lastg = Sc.op("tensor", lambda e, kd=kd, sl=sl, hoff=hoff, b=bg: e.matmul(
                    b.t[:, 0:512], sb_wsl[:, sl * 2, kd, :],
                    sb_xn2T[:, kd, hoff:hoff + 512],
                    start=(kd == 0), stop=(kd == 15)),
                    waits=w0 if kd == 0 else [], sig=(kd == 15))
            bg.set_writer(lastg)
            gsl_cons[sl * 2].append(lastg)
            w0 = bu.wr_waits() + [t_su]
            lastu = None
            for kd in range(16):
                lastu = Sc.op("tensor", lambda e, kd=kd, sl=sl, hoff=hoff, b=bu: e.matmul(
                    b.t[:, 0:512], sb_wsl[:, sl * 2 + 1, kd, :],
                    sb_xn2T[:, kd, hoff:hoff + 512],
                    start=(kd == 0), stop=(kd == 15)),
                    waits=w0 if kd == 0 else [], sig=(kd == 15))
            bu.set_writer(lastu)
            gsl_cons[sl * 2 + 1].append(lastu)
            t_si = Sc.op("scalar", lambda e, b=bg: e.activation(
                sb_evf[:], b.t[:, 0:512], AF.Silu), waits=bg.rd_waits() + evf_guard2)
            bg.add_reader(t_si)
            t_h = Sc.op("vector", lambda e, jj=jj, b=bu: e.tensor_tensor(
                h_tile(jj), b.t[:, 0:512], sb_evf[:], ALU.mult),
                waits=bu.rd_waits() + [t_si] + (BAR3 if jj < 2 and hf == 0 else []))
            bu.add_reader(t_h)
            evf_guard2 = [t_h]
            h_last = [t_h]
        for m2 in range(16):
            sl = m2 % 2
            osl = 1 - sl
            pq = [wtok[("p", m2 // 8, hb)] for hb in range(64)] if hf == 0 else []
            t_s1 = Sc.dma("pslab", lambda e, m2=m2, sl=sl: e.dma_start(
                out=sb_wsl[:, sl * 2:sl * 2 + 2, :, :].rearrange("p a k n -> p (a k) n"),
                in_=whatT_p[m2, 0:32].rearrange("k p f -> p k f")),
                waits=gsl_cons[sl * 2] + gsl_cons[sl * 2 + 1] + pq)
            t_s2 = Sc.dma("pslab", lambda e, m2=m2, osl=osl: e.dma_start(
                out=sb_wsl[:, osl * 2:osl * 2 + 2, :, :].rearrange("p a k n -> p (a k) n"),
                in_=whatT_p[m2, 32:64].rearrange("k p f -> p k f")),
                waits=gsl_cons[osl * 2] + gsl_cons[osl * 2 + 1])
            for i4 in range(4):
                gsl_cons[i4] = []
            bank = bkA0 if sl == 0 else bkA1
            w0 = bank.wr_waits() + [t_s1, t_s2] + h_last
            last = None
            for kd in range(64):
                wsrc = (sb_wsl[:, sl * 2 + kd // 16, kd % 16, :] if kd < 32 else
                        sb_wsl[:, osl * 2 + (kd - 32) // 16, (kd - 32) % 16, :])
                last = Sc.op("tensor", lambda e, kd=kd, wsrc=wsrc, b=bank: e.matmul(
                    b.t[:, 0:512], wsrc, h_tile(kd), start=(kd == 0), stop=(kd == 63)),
                    waits=w0 if kd == 0 else [], sig=(kd == 63))
            bank.set_writer(last)
            for i4 in range(4):
                gsl_cons[i4].append(last)
            t_ev = Sc.op("scalar", lambda e, b=bank: e.activation(
                sb_evf[:], b.t[:, 0:512], AF.Identity),
                waits=bank.rd_waits() + evf_guard2)
            bank.add_reader(t_ev)
            t_evb = Sc.op("vector", lambda e: e.tensor_copy(
                sb_dump[:, 0:512], sb_evf[:]), waits=[t_ev] + dump_guard)
            evf_guard2 = [t_evb]
            dump_guard = []
            for p4 in range(4):
                cch = hf * 4 + p4
                tm = Sc.op("tensor", lambda e, p4=p4: e.transpose(
                    bkTB.t[:, 0:128], sb_dump[:, p4 * 128:(p4 + 1) * 128], sb_idbf[:]),
                    waits=bkTB.wr_waits() + [t_evb])
                bkTB.set_writer(tm)
                dump_guard.append(tm)
                te = Sc.op("vector", lambda e: e.tensor_copy(
                    sb_mtmp[:, 0, :], bkTB.t[:, 0:128]), waits=bkTB.rd_waits())
                bkTB.add_reader(te)
                td = Sc.dma("ffnw", lambda e, cch=cch, m2=m2: e.dma_start(
                    out=ffnsc[cch, :, m2 * 128:(m2 + 1) * 128], in_=sb_mtmp[:, 0, :]),
                    waits=[te])
                bkTB.add_reader(td)
                ffn_dmas.append(td)

    BAR4 = Sc.barrier(streams=("whatw",))

    out_dmas = []
    fo_cons = []
    for cch in range(NCH):
        sl = cch % 2
        t_f = Sc.dma("x", lambda e, cch=cch: e.dma_start(
            out=sb_fo2[:], in_=ffnsc[cch, :, :]),
            waits=ffn_dmas[-2:] + fo_cons + (BAR4 if cch < 2 else []))
        t_x2l = Sc.dma("x", lambda e, cch=cch, sl=sl: e.dma_start(
            out=sb_x2[:, sl, :], in_=x2sp[cch, :, :]),
            waits=x2_cons[sl] + (BAR4 if cch < 2 else []) + x2sp_toks[-1:])
        x2_cons[sl] = []
        _, st, t7 = emit_ln(sb_fo2[:], None, [t_f])
        t_nf = Sc.op("vector", lambda e, st=st: e.tensor_scalar(
            sb_fo[:], sb_fo2[:], st[:, 7:8], st[:, 8:9], ALU.mult, ALU.add),
            waits=[t7] + (BAR4 if cch == 0 else []))
        fo_cons = [t_nf]
        t_o = Sc.op("vector", lambda e, sl=sl: e.tensor_tensor(
            sb_outst[:, sl, :], sb_fo[:], sb_x2[:, sl, :], ALU.add),
            waits=[t_nf, t_x2l] + xc_cons[sl])
        xc_cons[sl] = []
        x2_cons[sl].append(t_o)
        td = Sc.dma("outw", lambda e, cch=cch, sl=sl: e.dma_start(
            out=out[cch * 128:(cch + 1) * 128, :], in_=sb_outst[:, sl, :]),
            waits=[t_o])
        xc_cons[sl].append(td)
        out_dmas.append(td)

    Sc.op("sync", lambda e: e.nop(), waits=[out_dmas[-1]], sig=False)
    Sc.build()
    n_inst = sum(len(v) for v in Sc.items.values())
    print(f"program instructions (approx): {n_inst}")
    return nc


def _host_inputs(inputs, core):
    b, qv = core // 4, core % 4
    start = qv * TOK
    x = np.asarray(inputs["L0_attn_hidden_states"], dtype=np.float32)[b]
    xb_ = np.ascontiguousarray(x)
    halo = np.zeros((128, D), np.float32) if qv == 0 else x[start - 128:start]
    xo_ = np.ascontiguousarray(np.concatenate([halo, x[start:start + TOK]], axis=0))
    pos = np.maximum(np.arange(start - 128, start + TOK), 0).astype(np.float32)
    inv_freq = 1.0 / 10000.0 ** (np.arange(32, dtype=np.float32) * 2.0 / 64.0)
    ang = pos[None, :] * inv_freq[:, None]
    cos, sin = np.cos(ang), np.sin(ang)
    rtcc_ = np.ascontiguousarray(np.concatenate([cos, cos], 0).astype(np.float32))
    rtns_ = np.ascontiguousarray(np.concatenate([-sin, sin], 0).astype(np.float32))
    bend = (np.arange(C) * R + (R - 1)).astype(np.float32)
    angc = bend[None, :] * inv_freq[:, None]
    cc_, sc_ = np.cos(angc), np.sin(angc)
    cktcc_ = np.ascontiguousarray(np.concatenate([cc_, cc_], 0).astype(np.float32))
    cktns_ = np.ascontiguousarray(np.concatenate([-sc_, sc_], 0).astype(np.float32))
    cmask = np.zeros((NCH, 128, C), np.float32)
    for cch in range(NCH):
        posr = start + cch * 128 + np.arange(128)
        cmask[cch] = np.where(bend[None, :] <= posr[:, None], 0.0, -1e9)
    lmask = np.zeros((2, 128, 256), np.float32)
    i_ = np.arange(128)[:, None]
    j_ = np.arange(256)[None, :]
    band = (j_ >= i_) & (j_ <= i_ + 128)
    lmask[1] = np.where(band, 0.0, -1e9)
    lmask[0] = np.where(band & (j_ >= 128), 0.0, -1e9) if qv == 0 else lmask[1]
    g = lambda k: np.ascontiguousarray(np.asarray(inputs[k], dtype=np.float32))
    return {
        "xb": xb_, "xo": xo_, "rtcc": rtcc_, "rtns": rtns_,
        "cktcc": cktcc_, "cktns": cktns_, "cm": cmask.astype(bfdt),
        "lm": lmask.astype(bfdt),
        "Wq": g("L0_attn_Wq"), "Wk": g("L0_attn_Wk"), "Wv": g("L0_attn_Wv"),
        "Wlk": g("L0_attn_Wlk"), "Wlv": g("L0_attn_Wlv"), "Wo": g("L0_attn_Wo"),
        "Wgu": g("L0_ffn_w_gate_up"), "Wp": g("L0_ffn_w_proj"),
        "bqT": np.ascontiguousarray(g("L0_attn_bq").reshape(16, 128).T),
        "blkT": np.ascontiguousarray(g("L0_attn_blk").reshape(16, 128).T),
        "blvT": np.ascontiguousarray(g("L0_attn_blv").reshape(16, 128).T),
        "boT": np.ascontiguousarray(g("L0_attn_bo").reshape(16, 128).T),
        "bkc": g("L0_attn_bk").reshape(128, 1),
        "bvc": g("L0_attn_bv").reshape(128, 1),
        "bcc": np.ascontiguousarray(np.broadcast_to(
            g("L0_attn_bc").reshape(1, 1), (128, 1))),
        "qnc": g("L0_attn_qn_w").reshape(128, 1),
        "knc": g("L0_attn_kn_w").reshape(128, 1),
        "sinkTf": np.ascontiguousarray(g("L0_attn_sink_k").T),
        "sinkvRf": g("L0_attn_sink_v").reshape(1, NH * HD),
        "wcrepf": np.ascontiguousarray(np.broadcast_to(
            g("L0_attn_Wc").reshape(1, D), (128, D))),
    }


_CACHE = {}


def kernel(**inputs):
    if "nc" not in _CACHE:
        _CACHE["nc"] = build_program()
    nc = _CACHE["nc"]
    in_maps = [_host_inputs(inputs, c) for c in range(8)]
    res = run_bass_kernel_spmd(nc, in_maps, core_ids=list(range(8)))
    outs = [np.asarray(res.results[c]["out"], dtype=np.float32) for c in range(8)]
    return np.stack([np.concatenate(outs[0:4], 0), np.concatenate(outs[4:8], 0)], 0)


if __name__ == "__main__":
    build_program()
    print("BUILD OK")
